# version 08
# baseline (speedup 1.0000x reference)
import numpy as np
import ml_dtypes
from contextlib import ExitStack

import concourse.bass as bass
import concourse.mybir as mybir
from concourse.bass_utils import run_bass_kernel_spmd

F32 = mybir.dt.float32
BF16 = mybir.dt.bfloat16
I16 = mybir.dt.int16
ALU = mybir.AluOpType
AF = mybir.ActivationFunctionType
AX = mybir.AxisListType

DEBUG = False

D = 1024
DCH = 8
S = 4096
OWN = 2048
NT = S // 128
NTO = OWN // 128
HPC = 8
NHP = HPC // 2
NQG = S // 512
DK = 64
IFF = 2816
IPAD = 3072
ICH = 22
NIB = IPAD // 512
NEG = -1.0e30

KT_O, QT_O, VH_O = 0, 4096, 8192
XR_O = 12352
PTND_O = 20544
PTDG_O = 24640
WQ_O, WK_O, WV_O, WO_O = 26688, 30784, 34880, 38976
BIG_N = 43072
HQT_O = 0
WG_O = 22528
WU_O = 30720
WD_O = 38912
GST_O = 41728
AR_O = 0


class Sem:
    def __init__(self, h):
        self.h = h
        self.n = 0


class Sched:

    ENGS = ("tensor", "vector", "scalar", "gpsimd", "sync")

    def __init__(self):
        self.steps = {e: [] for e in self.ENGS}

    def emit(self, eng, fn, waits=(), inc=None, inc_by=1):
        w = [(s, v) for (s, v) in waits if v > 0]
        self.steps[eng].append((w, fn, inc, inc_by))
        if inc is not None:
            inc.n += inc_by
        return inc.n if inc is not None else 0


def build_program(debug=False):
    nc = bass.Bass("TRN2", target_bir_lowering=False)

    xb = nc.dram_tensor("xb", [S, D], F32, kind="ExternalInput").ap()
    x_own = nc.dram_tensor("x_own", [OWN, D], F32, kind="ExternalInput").ap()
    wq_in = nc.dram_tensor("wq_in", [D, 512], F32, kind="ExternalInput").ap()
    wk_in = nc.dram_tensor("wk_in", [D, 512], F32, kind="ExternalInput").ap()
    wv_in = nc.dram_tensor("wv_in", [D, 512], F32, kind="ExternalInput").ap()
    wo_in = nc.dram_tensor("wo_in", [512, D], F32, kind="ExternalInput").ap()
    wg_sh = nc.dram_tensor("wg_sh", [12 * 128, D], F32,
                           kind="ExternalInput").ap()
    wu_sh = nc.dram_tensor("wu_sh", [12 * 128, D], F32,
                           kind="ExternalInput").ap()
    wd_sh = nc.dram_tensor("wd_sh", [4 * 128, IFF], F32,
                           kind="ExternalInput").ap()
    rolei = nc.dram_tensor("rolei", [1, 2], mybir.dt.int32,
                           kind="ExternalInput").ap()
    epochi = nc.dram_tensor("epochi", [1, 2], mybir.dt.int32,
                            kind="ExternalInput").ap()
    ident = nc.dram_tensor("ident", [128, 128], BF16, kind="ExternalInput").ap()
    maskt = nc.dram_tensor("maskt", [128, 128], F32, kind="ExternalInput").ap()
    selv = nc.dram_tensor("selv", [128, 2], F32, kind="ExternalInput").ap()
    out_own = nc.dram_tensor("out_own", [OWN, D], F32, kind="ExternalOutput").ap()

    dkind = "ExternalOutput" if debug else "Internal"
    xnT_d = nc.dram_tensor("xnT_d", [DCH, 128, S], BF16, kind=dkind).ap()
    xnTo_d = nc.dram_tensor("xnTo_d", [DCH, 128, OWN], BF16, kind=dkind).ap()
    attnT_d = nc.dram_tensor("attnT_d", [NHP, 128, S], BF16, kind=dkind).ap()
    gT_d = nc.dram_tensor("gT_d", [DCH, 128, OWN], BF16, kind=dkind).ap()
    shwg = nc.dram_tensor("shwg", [24 * D, 128], BF16,
                          addr_space="Shared").ap()
    shwu = nc.dram_tensor("shwu", [24 * D, 128], BF16,
                          addr_space="Shared").ap()
    shwd = nc.dram_tensor("shwd", [8 * IFF, 128], BF16,
                          addr_space="Shared").ap()
    shs = nc.dram_tensor("shs", [2, DCH * 128, S], BF16,
                         addr_space="Shared").ap()
    shflag = nc.dram_tensor("shflag", [2, 2, 16], mybir.dt.int32,
                            addr_space="Shared").ap()
    if debug:
        xnq_dbg = nc.dram_tensor("xnq_dbg", [DCH, 128, OWN], BF16,
                                 kind="ExternalOutput").ap()
        sp_dbg = nc.dram_tensor("sp_dbg", [DCH * 128, S], BF16,
                                kind="ExternalOutput").ap()
        kqv_dbg = nc.dram_tensor("kqv_dbg", [3, 128, 4160], BF16,
                                 kind="ExternalOutput").ap()
        p_dbg = nc.dram_tensor("p_dbg", [6, 128, 2048], BF16,
                               kind="ExternalOutput").ap()
        av_dbg = nc.dram_tensor("av_dbg", [4, 128, 512], F32,
                                kind="ExternalOutput").ap()

    sch = Sched()
    es = ExitStack()

    def sb(name, shape, dtype):
        return es.enter_context(nc.sbuf_tensor(name, shape, dtype))

    def ps(name, shape, dtype):
        return es.enter_context(nc.psum_tensor(name, shape, dtype))

    def sem(name):
        return Sem(es.enter_context(nc.semaphore(name)))

    with es:
        big = sb("big", [128, BIG_N], BF16)
        xnqT = sb("xnqT", [128, DCH * OWN], BF16)
        t_x = [sb(f"t_x{i}", [128, D], F32) for i in range(2)]
        t_xn = sb("t_xn", [128, D], F32)
        t_xnb = sb("t_xnb", [128, D], BF16)
        bnb = sb("bnb", [128, 12], F32)
        mv = sb("mv", [128, 2], F32)
        ln_s = sb("ln_s", [128, 4], F32)
        q_xn = sb("q_xn", [128, 1024], F32)
        q_a = sb("q_a", [128, 1024], F32)
        q_i1 = sb("q_i1", [128, 1024], I16)
        q_i2 = sb("q_i2", [128, 1024], I16)
        q_i3 = sb("q_i3", [128, 1024], I16)
        q_t1 = sb("q_t1", [128, 1024], BF16)
        q_t2 = sb("q_t2", [128, 1024], BF16)
        q_sg = sb("q_sg", [128, 1024], BF16)
        q_out = sb("q_out", [128, 1024], BF16)
        q_bm = sb("q_bm", [128, 64], F32)
        q_S = sb("q_S", [128, 64], F32)
        q_R = sb("q_R", [128, 64], F32)
        q_Sb = sb("q_Sb", [128, 64], BF16)
        st = [sb(f"st{i}", [128, 1024], BF16) for i in range(2)]
        t_mask = sb("t_mask", [128, 128], F32)
        t_id = sb("t_id", [128, 128], BF16)
        t_sel = sb("t_sel", [128, 2], F32)
        rden = [sb(f"rden{i}", [128, 512], F32) for i in range(2)]
        rb64 = [sb(f"rb64{i}", [128, 512], F32) for i in range(2)]
        outst = [sb(f"outst{i}", [128, 512], BF16) for i in range(2)]
        t_ones = sb("t_ones", [128, 64], F32)
        t_ri = sb("t_ri", [1, 2], mybir.dt.int32)
        t_ep = sb("t_ep", [1, 16], mybir.dt.int32)
        t_zz = sb("t_zz", [1, 16], mybir.dt.int32)
        regs = {}
        t_wf = None
        hsc = sb("hsc", [128, 512], F32)
        hbuf = [sb(f"hbuf{i}", [128, 512], F32) for i in range(2)]
        f_sA = sb("f_sA", [128, 1024], BF16)
        f_sA2 = sb("f_sA2", [128, 1024], BF16)
        f_sB = sb("f_sB", [128, 1024], BF16)
        f_sB2 = sb("f_sB2", [128, 1024], BF16)
        f_g = sb("f_g", [128, 1024], BF16)
        f_xn = sb("f_xn", [128, 1024], BF16)
        f_t = sb("f_t", [128, 1024], F32)
        f_s = sb("f_s", [128, 1024], F32)
        f_s2 = sb("f_s2", [128, 1024], F32)
        f_m = sb("f_m", [128, 1024], BF16)
        f_o = sb("f_o", [128, 1024], F32)

        P0 = ps("P0", [128, 2048], F32)
        P1 = ps("P1", [128, 512], F32)
        P2 = ps("P2", [128, 512], F32)
        P3 = ps("P3", [128, 512], F32)
        PT = ps("PT", [128, 1024], BF16)

        dS = sem("dS")
        dA = sem("dA")
        dP = sem("dP")
        pe = sem("pe")
        dv = sem("dv")
        ac = sem("ac")
        gp = sem("gp")

        E = sch.emit

        def dma(q, qs, out, in_, waits=()):
            return E(q, lambda e, o=out, i=in_: e.dma_start(out=o, in_=i),
                     waits=waits, inc=qs, inc_by=16)

        def quant(src_f32, n, dst_bf16, after=(), dst_waits=()):
            assert n % 32 == 0
            h = n // 2
            g = h // 16

            def halves(fn):
                for i in range(2):
                    fn(i)

            co = [0, h]
            go = [0, g]

            def stp(fn, waits=()):
                E("vector", fn, waits=waits, inc=dv)

            w0 = list(after)
            for i in range(2):
                c, gg = co[i], go[i]
                x3 = src_f32[:, c:c + h].rearrange("p (g w) -> p g w", w=16)
                stp(lambda e, x3=x3, gg=gg: e.tensor_reduce(
                    q_bm[:, gg:gg + g], x3, AX.X, ALU.max,
                    apply_absolute_value=True), waits=w0)
                w0 = ()
            for i in range(2):
                gg = go[i]
                stp(lambda e, gg=gg: e.tensor_scalar(
                    q_S[:, gg:gg + g], q_bm[:, gg:gg + g], 1.0 / 6.0, 1e-12,
                    ALU.mult, ALU.max))
            for i in range(2):
                gg = go[i]
                stp(lambda e, gg=gg: e.reciprocal(
                    q_R[:, gg:gg + g], q_S[:, gg:gg + g]))
            for i in range(2):
                gg = go[i]
                stp(lambda e, gg=gg: e.tensor_copy(
                    q_Sb[:, gg:gg + g], q_S[:, gg:gg + g]))
            for i in range(2):
                c, gg = co[i], go[i]
                x3 = src_f32[:, c:c + h].rearrange("p (g w) -> p g w", w=16)
                rb = q_R[:, gg:gg + g].unsqueeze(2).broadcast_to((128, g, 16))
                o3 = q_xn[:, c:c + h].rearrange("p (g w) -> p g w", w=16)
                stp(lambda e, x3=x3, rb=rb, o3=o3: e.tensor_tensor(
                    o3, x3, rb, ALU.mult))
            for i in range(2):
                c = co[i]
                stp(lambda e, c=c: e.tensor_scalar(
                    q_sg[:, c:c + h], q_xn[:, c:c + h], 0.0, 2.0,
                    ALU.is_gt, ALU.mult))
            for i in range(2):
                c = co[i]
                stp(lambda e, c=c: e.scalar_tensor_tensor(
                    q_a[:, c:c + h], q_sg[:, c:c + h], -1.0,
                    q_xn[:, c:c + h], ALU.add, ALU.mult))
            for i in range(2):
                c = co[i]
                stp(lambda e, c=c: e.tensor_scalar(
                    q_xn[:, c:c + h], q_a[:, c:c + h], 2.0, 4.0,
                    ALU.mult, ALU.min))
            for i in range(2):
                c = co[i]
                stp(lambda e, c=c: e.tensor_copy(
                    q_i1[:, c:c + h], q_xn[:, c:c + h]))
            for i in range(2):
                c = co[i]
                stp(lambda e, c=c: e.tensor_scalar(
                    q_xn[:, c:c + h], q_a[:, c:c + h], 2.0, 4.0,
                    ALU.max, ALU.min))
            for i in range(2):
                c = co[i]
                stp(lambda e, c=c: e.tensor_copy(
                    q_i2[:, c:c + h], q_xn[:, c:c + h]))
            for i in range(2):
                c = co[i]
                stp(lambda e, c=c: e.tensor_scalar(
                    q_xn[:, c:c + h], q_a[:, c:c + h], 0.5, 2.0,
                    ALU.mult, ALU.max))
            for i in range(2):
                c = co[i]
                stp(lambda e, c=c: e.tensor_copy(
                    q_i3[:, c:c + h], q_xn[:, c:c + h]))
            for i in range(2):
                c = co[i]
                stp(lambda e, c=c: e.scalar_tensor_tensor(
                    q_t1[:, c:c + h], q_i1[:, c:c + h], 0.5, q_i2[:, c:c + h],
                    ALU.mult, ALU.add))
            for i in range(2):
                c = co[i]
                stp(lambda e, c=c: e.scalar_tensor_tensor(
                    q_t2[:, c:c + h], q_i3[:, c:c + h], 2.0, q_t1[:, c:c + h],
                    ALU.mult, ALU.add))
            for i in range(2):
                c = co[i]
                stp(lambda e, c=c: e.tensor_scalar(
                    q_t2[:, c:c + h], q_t2[:, c:c + h], -6.0, None, ALU.add))
            for i in range(2):
                c, gg = co[i], go[i]
                sbb = q_Sb[:, gg:gg + g].unsqueeze(2).broadcast_to((128, g, 16))
                t23 = q_t2[:, c:c + h].rearrange("p (g w) -> p g w", w=16)
                o3 = q_t1[:, c:c + h].rearrange("p (g w) -> p g w", w=16)
                stp(lambda e, t23=t23, sbb=sbb, o3=o3: e.tensor_tensor(
                    o3, t23, sbb, ALU.mult))
            for i in range(2):
                c = co[i]
                stp(lambda e, c=c: e.scalar_tensor_tensor(
                    dst_bf16[:, c:c + h], q_sg[:, c:c + h], -1.0,
                    q_t1[:, c:c + h], ALU.add, ALU.mult),
                    waits=(list(dst_waits) if i == 0 else ()))
            return dv.n

        def transpose8(src_bf16, nblk, pt_off=0, waits=()):
            for m in range(nblk):
                w = waits if m == 0 else ()
                E("tensor", lambda e, m=m: e.transpose(
                    PT[:, pt_off + 128 * m:pt_off + 128 * (m + 1)],
                    src_bf16[:, 128 * m:128 * (m + 1)], t_id[:, :]),
                    waits=w, inc=(pe if m == nblk - 1 else None))
            return pe.n

        dma("sync", dS, t_id[:, :], ident)
        dma("sync", dS, t_mask[:, :], maskt)
        dma("sync", dS, t_sel[:, :], selv)
        dma("sync", dS, t_ri[0:1, :], rolei)
        dma("sync", dS, t_ep[0:1, 0:2], epochi)
        a_small = dS.n

        def _init_regs_sync(e):
            regs["sy_role"] = e.alloc_register("sy_role")
            e.wait_ge(dS.h, a_small)
            return e.load(regs["sy_role"], t_ri[0:1, 0:1])
        E("sync", _init_regs_sync, waits=(), inc=None)

        def _init_regs_gp(e):
            e.memset(t_zz[0:1, :], 0)
            regs["gp_role"] = e.alloc_register("gp_role")
            regs["gp_f"] = e.alloc_register("gp_f")
            regs["gp_c"] = e.alloc_register("gp_c")
            regs["gp_k"] = e.alloc_register("gp_k")
            e.wait_ge(dS.h, a_small)
            e.load(regs["gp_role"], t_ri[0:1, 0:1])
            e.load(regs["gp_k"], t_ep[0:1, 0:1])
            with e.If_eq(regs["gp_role"], 0):
                e.dma_start(out=shflag[0, 0, 0:2],
                            in_=t_zz[0:1, 0:2]).then_inc(dP.h, 16)
                e.dma_start(out=shflag[0, 1, 0:2],
                            in_=t_zz[0:1, 0:2]).then_inc(dP.h, 16)
            with e.Else():
                e.dma_start(out=shflag[1, 0, 0:2],
                            in_=t_zz[0:1, 0:2]).then_inc(dP.h, 16)
                e.dma_start(out=shflag[1, 1, 0:2],
                            in_=t_zz[0:1, 0:2]).then_inc(dP.h, 16)
            return e.wait_ge(dP.h, 32)
        E("gpsimd", _init_regs_gp, waits=(), inc=None)
        dP.n += 32

        def _barrier_body(e, bar, after_dS, dp_wait):
            e.wait_ge(dS.h, after_dS)
            r = regs["gp_role"]
            f, c, k = regs["gp_f"], regs["gp_c"], regs["gp_k"]
            for me in (0, 1):
                other = 1 - me
                with (e.If_eq(r, 0) if me == 0 else e.Else()):
                    e.dma_start(out=shflag[me, bar, 0:2],
                                in_=t_ep[0:1, 0:2]).then_inc(dP.h, 16)
                    e.wait_ge(dP.h, dp_wait)
                    e.load(f, shflag[other, bar:bar + 1, 0:1])
                    e.reg_alu(c, f, k, ALU.not_equal)
                    with e.While(c):
                        e.load(f, shflag[other, bar:bar + 1, 0:1])
                        e.reg_alu(c, f, k, ALU.not_equal)
            return e.nop()

        def pair_barrier(bar, after_dS):
            dp_wait = dP.n + 16
            E("gpsimd", lambda e, bar=bar, a=after_dS, w=dp_wait:
              _barrier_body(e, bar, a, w), waits=(), inc=gp)
            dP.n += 16
            return gp.n

        for (w_in, w_off, scale, ncol) in [
                (wq_in, WQ_O, 0.125, 512), (wk_in, WK_O, 1.0, 512),
                (wv_in, WV_O, 1.0, 512), (wo_in, WO_O, 1.0, 1024)]:
            nch = w_in.shape[0] // 128
            for pc in range(nch):
                wv_ = dma("sync", dS, t_x[0][:, :ncol],
                          w_in[128 * pc:128 * (pc + 1), :],
                          waits=[(ac, ac.n)])
                E("scalar", lambda e, w_off=w_off, pc=pc, scale=scale,
                  ncol=ncol: e.activation(
                      big[:, w_off + ncol * pc:w_off + ncol * (pc + 1)],
                      t_x[0][:, :ncol], AF.Copy, scale=scale),
                  waits=[(dS, wv_)], inc=ac)
        aw_done_ac = ac.n

        st_dma_free = [0, 0]
        qout_free_pe = 0

        def prep_piece(src_ap, cn, dst0_ap, dst1_ap, par):
            nonlocal qout_free_pe
            land = t_x[par % 2][:, :cn]
            ldw = dma("sync", dS, land, src_ap,
                      waits=[(dv, dv.n), (pe, pe.n), (ac, ac.n)])
            qdv = quant(land, cn, q_out, after=[(dS, ldw)],
                        dst_waits=[(pe, qout_free_pe)])
            nb8 = cn // 128
            tpe = transpose8(q_out, nb8,
                             waits=[(dv, qdv), (ac, aw_done_ac)])
            qout_free_pe = tpe
            sti = st[par % 2]
            E("vector", lambda e, sti=sti, nb8=nb8: e.tensor_copy(
                sti[:, :128 * nb8], PT[:, :128 * nb8]),
              waits=[(pe, tpe), (dS, st_dma_free[par % 2])], inc=dv)
            src_st = sti[:, :128 * nb8].rearrange("p (m w) -> p m w", w=128)
            d0 = dst0_ap.rearrange("(m p) w -> p m w", p=128)
            d1 = dst1_ap.rearrange("(m p) w -> p m w", p=128)
            dvw = dv.n

            def _branched_store(e, d0=d0, d1=d1, src_st=src_st, dvw=dvw):
                e.wait_ge(dv.h, dvw)
                with e.If_eq(regs["sy_role"], 0):
                    e.dma_start(out=d0, in_=src_st).then_inc(dS.h, 16)
                with e.Else():
                    e.dma_start(out=d1, in_=src_st).then_inc(dS.h, 16)
                return None
            E("sync", _branched_store, waits=(), inc=None)
            dS.n += 16
            st_dma_free[par % 2] = dS.n

        par = 0
        for (src, sh_dst) in [(wg_sh, shwg), (wu_sh, shwu)]:
            for lb in range(12):
                prep_piece(src[128 * lb:128 * (lb + 1), :], D,
                           sh_dst[D * lb:D * (lb + 1), :],
                           sh_dst[D * (12 + lb):D * (13 + lb), :], par)
                par += 1
        for lb in range(4):
            for pc in range(3):
                c0 = 1024 * pc
                cn = min(1024, IFF - c0)
                prep_piece(wd_sh[128 * lb:128 * (lb + 1), c0:c0 + cn], cn,
                           shwd[IFF * lb + c0:IFF * lb + c0 + cn, :],
                           shwd[IFF * (4 + lb) + c0:IFF * (4 + lb) + c0 + cn,
                                :], par)
                par += 1
        wprep_dS = dS.n

        wbar_gp = pair_barrier(0, wprep_dS)
        wg4 = shwg.rearrange("(b c p) i -> p c b i", c=8, p=128)
        wu4 = shwu.rearrange("(b c p) i -> p c b i", c=8, p=128)
        wd4 = shwd.rearrange("(h ic p) w -> h p ic w", h=8, p=128)

        def ln_tile(xsrc, ti, store_dst, do_quant, own_ti=None):
            nonlocal qout_free_pe
            land = t_x[ti % 2][:, :]
            ldw = dma("sync", dS, land, xsrc[128 * ti:128 * (ti + 1), :],
                      waits=[(dv, dv.n), (ac, ac.n)])
            E("vector", lambda e: e.bn_stats(bnb[:, 0:6], land[:, 0:512]),
              waits=[(dS, ldw)], inc=dv)
            E("vector", lambda e: e.bn_stats(bnb[:, 6:12], land[:, 512:1024]),
              inc=dv)
            E("vector", lambda e: e.bn_aggr(mv[:, :], bnb[:, :]),
              waits=[(dv, dv.n)], inc=dv)
            E("vector", lambda e: e.tensor_scalar(
                ln_s[:, 0:1], mv[:, 1:2], 1e-5, None, ALU.add),
              waits=[(dv, dv.n)], inc=dv)
            E("scalar", lambda e: e.activation(
                ln_s[:, 1:2], ln_s[:, 0:1], AF.Sqrt),
              waits=[(dv, dv.n)], inc=ac)
            E("vector", lambda e: e.reciprocal(ln_s[:, 2:3], ln_s[:, 1:2]),
              waits=[(ac, ac.n)], inc=dv)
            E("vector", lambda e: e.scalar_tensor_tensor(
                ln_s[:, 3:4], mv[:, 0:1], -1.0, ln_s[:, 2:3],
                ALU.mult, ALU.mult), waits=[(dv, dv.n)], inc=dv)
            E("scalar", lambda e: e.activation(
                t_xnb[:, :], land[:, :], AF.Identity,
                scale=ln_s[:, 2:3], bias=ln_s[:, 3:4]),
              waits=[(dv, dv.n)], inc=ac)
            tpe = transpose8(t_xnb, 8, waits=[(ac, ac.n), (dv, dv.n)])
            sti = st[ti % 2]
            E("vector", lambda e, sti=sti: e.tensor_copy(sti[:, :], PT[:, :]),
              waits=[(pe, tpe), (dS, st_dma_free[ti % 2])], inc=dv)
            dst = store_dst[:, :, 128 * ti:128 * (ti + 1)].rearrange(
                "c p w -> p c w")
            st_dma_free[ti % 2] = dma(
                "sync", dS, dst,
                sti[:, :].rearrange("p (c w) -> p c w", w=128),
                waits=[(dv, dv.n)])
            if do_quant:
                t = own_ti
                E("vector", lambda e: e.tensor_scalar(
                    t_xn[:, :], land[:, :], ln_s[:, 2:3], ln_s[:, 3:4],
                    ALU.mult, ALU.add), waits=[(dv, dv.n)], inc=dv)
                qdv = quant(t_xn, D, q_out, after=[(dv, dv.n)],
                            dst_waits=[(pe, qout_free_pe)])
                tpe = transpose8(q_out, 8, waits=[(dv, qdv)])
                qout_free_pe = tpe
                for c in range(DCH):
                    E("vector", lambda e, c=c, t=t: e.tensor_copy(
                        xnqT[:, OWN * c + 128 * t:OWN * c + 128 * (t + 1)],
                        PT[:, 128 * c:128 * (c + 1)]),
                      waits=([(pe, tpe)] if c == 0 else ()), inc=dv)
                if debug:
                    dstq = xnq_dbg[:, :, 128 * t:128 * (t + 1)].rearrange(
                        "c p w -> p c w")
                    dma("sync", dS, dstq,
                        q_out[:, :].rearrange("p (c w) -> p c w", w=128),
                        waits=[(dv, dv.n)])

        for ti in range(NT):
            ln_tile(xb, ti, xnT_d, False)
        for ti in range(NTO):
            ln_tile(x_own, ti, xnTo_d, True, own_ti=ti)
        ln_done_dS = dS.n
        ln_done_dv = dv.n

        kT = big[:, KT_O:KT_O + 4096]
        qT = big[:, QT_O:QT_O + 4096]
        vh = big[:, VH_O:VH_O + 4160]
        xr = [big[:, XR_O + 4096 * i:XR_O + 4096 * (i + 1)] for i in range(2)]
        pnd = [big[:, PTND_O + 2048 * i:PTND_O + 2048 * (i + 1)]
               for i in range(2)]
        pdg = [big[:, PTDG_O + 512 * j:PTDG_O + 512 * (j + 1)]
               for j in range(4)]

        vh3 = vh.rearrange("p (c w) -> p c w", w=65)
        E("vector", lambda e: e.memset(vh3[:, :, 64:65], 1.0),
          waits=[], inc=dv)
        E("vector", lambda e: e.memset(t_ones[0:1, :], 1.0), inc=dv)
        for j in range(4):
            E("vector", lambda e, j=j: e.memset(pdg[j][:, :], 0.0), inc=dv)
        attn_init_dv = dv.n

        xr_free_pe = [0, 0]
        pnd_free_pe = [0, 0]
        pdg_free_pe = [0, 0, 0, 0]
        p1_free_dv = 0
        p0_free_ac = 0
        p2div_free_dv = 0
        outst_free_dS = [0, 0]
        rr_free_dv = [0, 0]
        dbg_gate_dv = 0
        ogi = 0

        for hp in range(NHP):
            kq_ready_ac = {}
            v_ready_dv = {}
            for tg in range(DCH):
                xbuf = xr[tg % 2]
                lw = [(pe, xr_free_pe[tg % 2]), (dS, ln_done_dS)]
                src = xnT_d[:, :, 512 * tg:512 * (tg + 1)].rearrange(
                    "c p w -> p c w")
                ldx = dma("sync", dS,
                          xbuf.rearrange("p (c w) -> p c w", w=512),
                          src, waits=lw)
                for c in range(DCH):
                    E("tensor", lambda e, c=c, xbuf=xbuf, hp=hp: e.matmul(
                        P2[:, :],
                        big[:, WK_O + 512 * c + 128 * hp:
                             WK_O + 512 * c + 128 * (hp + 1)],
                        xbuf[:, 512 * c:512 * (c + 1)],
                        start=(c == 0), stop=(c == DCH - 1)),
                      waits=([(dS, ldx), (ac, ac.n), (dv, dv.n)]
                             if c == 0 else ()),
                      inc=(pe if c == DCH - 1 else None))
                E("scalar", lambda e, tg=tg: e.activation(
                    kT[:, 512 * tg:512 * (tg + 1)], P2[:, :], AF.Copy),
                  waits=[(pe, pe.n)], inc=ac)
                for c in range(DCH):
                    E("tensor", lambda e, c=c, xbuf=xbuf, hp=hp: e.matmul(
                        P2[:, :],
                        big[:, WQ_O + 512 * c + 128 * hp:
                             WQ_O + 512 * c + 128 * (hp + 1)],
                        xbuf[:, 512 * c:512 * (c + 1)],
                        start=(c == 0), stop=(c == DCH - 1)),
                      waits=([(ac, ac.n)] if c == 0 else ()),
                      inc=(pe if c == DCH - 1 else None))
                E("scalar", lambda e, tg=tg: e.activation(
                    qT[:, 512 * tg:512 * (tg + 1)], P2[:, :], AF.Copy),
                  waits=[(pe, pe.n)], inc=ac)
                kq_ready_ac[tg] = ac.n
                for tt in range(4):
                    for c in range(DCH):
                        E("tensor", lambda e, c=c, tt=tt, xbuf=xbuf, hp=hp:
                          e.matmul(
                              P2[:, 128 * tt:128 * (tt + 1)],
                              xbuf[:, 512 * c + 128 * tt:
                                   512 * c + 128 * (tt + 1)],
                              big[:, WV_O + 512 * c + 128 * hp:
                                   WV_O + 512 * c + 128 * (hp + 1)],
                              start=(c == 0), stop=(c == DCH - 1)),
                          waits=([(ac, kq_ready_ac[tg]),
                                  (dv, dv.n)] if (tt == 0 and c == 0) else ()),
                          inc=(pe if c == DCH - 1 else None))
                vpe = pe.n
                for tt in range(4):
                    c32 = 4 * tg + tt
                    for h2 in range(2):
                        E("vector", lambda e, tt=tt, h2=h2, c32=c32:
                          e.tensor_copy(
                              vh[:, 2080 * h2 + 65 * c32:
                                 2080 * h2 + 65 * c32 + 64],
                              P2[:, 128 * tt + 64 * h2:
                                 128 * tt + 64 * h2 + 64]),
                          waits=([(pe, vpe),
                                  (dv, attn_init_dv)] if tt == 0 and h2 == 0
                                 else ()), inc=dv)
                v_ready_dv[tg] = dv.n
                xr_free_pe[tg % 2] = pe.n
            proj_ac = ac.n
            proj_dv = dv.n
            if debug and hp == 0:
                dma("sync", dS, kqv_dbg[0, :, 0:4096], kT,
                    waits=[(ac, proj_ac), (dv, proj_dv)])
                dma("sync", dS, kqv_dbg[1, :, 0:4096], qT, waits=())
                dma("sync", dS, kqv_dbg[2, :, :], vh, waits=())

            for h2 in range(2):
                r0 = 64 * h2
                for g in range(NQG):
                    nb = g
                    scores_pe = {}
                    exp_ac = {}
                    for b in range(nb + 1):
                        is_diag = (b == nb)
                        if not is_diag:
                            for j4 in range(4):
                                kc = 4 * b + j4
                                w = []
                                if j4 == 0:
                                    w = [(ac, exp_ac.get(b - 1, p0_free_ac)),
                                         (ac, proj_ac), (dv, proj_dv)]
                                E("tensor", lambda e, kc=kc, j4=j4, r0=r0, g=g:
                                  e.matmul(
                                      P0[:, 512 * j4:512 * (j4 + 1)],
                                      kT[r0:r0 + 64,
                                         128 * kc:128 * (kc + 1)],
                                      qT[r0:r0 + 64, 512 * g:512 * (g + 1)],
                                      start=True, stop=True),
                                  waits=w, inc=(pe if j4 == 3 else None))
                            scores_pe[b] = pe.n
                        else:
                            for j in range(4):
                                kc = 4 * g + j
                                nw = 512 - 128 * j
                                w = [(ac, exp_ac.get(b - 1, p0_free_ac))] \
                                    if j == 0 else []
                                if j == 0:
                                    w += [(ac, proj_ac), (dv, proj_dv)]
                                E("tensor", lambda e, kc=kc, j=j, nw=nw,
                                  r0=r0, g=g: e.matmul(
                                      P0[:, 512 * j:512 * j + nw],
                                      kT[r0:r0 + 64,
                                         128 * kc:128 * (kc + 1)],
                                      qT[r0:r0 + 64,
                                         512 * g + 128 * j:512 * (g + 1)],
                                      start=True, stop=True),
                                  waits=w, inc=pe)
                                scores_pe[(b, j)] = pe.n
                        if b >= 1:
                            pb = pnd[(b - 1) % 2]
                            for j4 in range(4):
                                kc = 4 * (b - 1) + j4
                                w = []
                                if j4 == 0:
                                    w = [(ac, exp_ac[b - 1]),
                                         (dv, dbg_gate_dv)]
                                    if b == 1:
                                        w.append((dv, p1_free_dv))
                                E("tensor", lambda e, kc=kc, j4=j4, h2=h2,
                                  b=b, pb=pb: e.matmul(
                                      P1[0:65, :],
                                      vh[:, 2080 * h2 + 65 * kc:
                                         2080 * h2 + 65 * kc + 65],
                                      pb[:, 512 * j4:512 * (j4 + 1)],
                                      start=(b == 1 and j4 == 0), stop=False),
                                  waits=w, inc=(pe if j4 == 3 else None))
                            pnd_free_pe[(b - 1) % 2] = pe.n
                            if debug and hp == 0 and h2 == 1 and g == 0:
                                E("vector", lambda e: e.tensor_copy(
                                    hbuf[0][:, :], P1[:, :]),
                                  waits=[(pe, pe.n)], inc=dv)
                                dma("sync", dS, av_dbg[b], hbuf[0][:, :],
                                    waits=[(dv, dv.n)])
                                dbg_gate_dv = dv.n
                        if not is_diag:
                            E("scalar", lambda e, b=b: e.activation(
                                pnd[b % 2][:, :], P0[:, :], AF.Exp),
                              waits=[(pe, scores_pe[b]),
                                     (pe, pnd_free_pe[b % 2])], inc=ac)
                            exp_ac[b] = ac.n
                        else:
                            for j in range(4):
                                E("vector", lambda e, j=j: e.tensor_tensor(
                                    P0[:, 512 * j:512 * j + 128],
                                    P0[:, 512 * j:512 * j + 128],
                                    t_mask[:, :], ALU.add),
                                  waits=[(pe, scores_pe[(b, j)])], inc=dv)
                                mdv = dv.n
                                nw = 512 - 128 * j
                                E("scalar", lambda e, j=j, nw=nw:
                                  e.activation(
                                      pdg[j][:, 128 * j:128 * j + nw],
                                      P0[:, 512 * j:512 * j + nw], AF.Exp),
                                  waits=[(dv, mdv),
                                         (pe, pdg_free_pe[j])], inc=ac)
                                exp_ac[(b, j)] = ac.n
                    p0_free_ac = ac.n
                    if debug and hp == 0 and h2 == 1 and g == 0:
                        dma("sync", dS, p_dbg[0], pnd[0],
                            waits=[(ac, ac.n)])
                        dma("sync", dS, p_dbg[1], pnd[1], waits=())
                        for j in range(4):
                            dma("sync", dS, p_dbg[2 + j][:, 0:512], pdg[j],
                                waits=())
                    for j in range(4):
                        kc = 4 * g + j
                        w = [(ac, exp_ac[(nb, j)]), (dv, dbg_gate_dv)]
                        if nb == 0 and j == 0:
                            w.append((dv, p1_free_dv))
                        E("tensor", lambda e, kc=kc, j=j, h2=h2, g=g, nb=nb:
                          e.matmul(
                              P1[0:65, :],
                              vh[:, 2080 * h2 + 65 * kc:
                                 2080 * h2 + 65 * kc + 65],
                              pdg[j][:, :],
                              start=(nb == 0 and j == 0), stop=(j == 3)),
                          waits=w, inc=(pe if j == 3 else None))
                        pdg_free_pe[j] = pe.n
                    av_pe = pe.n
                    if debug and hp == 0 and h2 == 1 and g == 0:
                        E("vector", lambda e: e.tensor_copy(
                            hbuf[0][:, :], P1[:, :]),
                          waits=[(pe, av_pe)], inc=dv)
                        dma("sync", dS, av_dbg[3], hbuf[0][:, :],
                            waits=[(dv, dv.n)])
                    oi = ogi % 2
                    E("vector", lambda e, oi=oi: e.reciprocal(
                        rden[oi][0:1, :], P1[64:65, :]),
                      waits=[(pe, av_pe), (pe, rr_free_dv[oi])], inc=dv)
                    rdv = dv.n
                    E("tensor", lambda e, oi=oi: e.matmul(
                        P2[0:64, :], t_ones[0:1, 0:64], rden[oi][0:1, :],
                        start=True, stop=True),
                      waits=[(dv, max(rdv, p2div_free_dv, proj_dv))], inc=pe)
                    rr_free_dv[oi] = pe.n
                    E("vector", lambda e, oi=oi: e.tensor_copy(
                        rb64[oi][0:64, :], P2[0:64, :]),
                      waits=[(pe, pe.n)], inc=dv)
                    p2div_free_dv = dv.n
                    E("vector", lambda e, oi=oi: e.tensor_tensor(
                        outst[oi][0:64, :], P1[0:64, :], rb64[oi][0:64, :],
                        ALU.mult),
                      waits=[(dS, outst_free_dS[oi]), (dv, dv.n)], inc=dv)
                    p1_free_dv = dv.n
                    od = dma("sync", dS,
                             attnT_d[hp, r0:r0 + 64, 512 * g:512 * (g + 1)],
                             outst[oi][0:64, :], waits=[(dv, dv.n)])
                    outst_free_dS[oi] = od
                    ogi += 1

        attn_done_dS = dS.n

        ar = [big[:, AR_O + 2048 * i:AR_O + 2048 * (i + 1)] for i in range(2)]
        ar_free_pe = [pe.n, pe.n]
        for tg in range(DCH):
            abuf = ar[tg % 2]
            src = attnT_d[:, :, 512 * tg:512 * (tg + 1)].rearrange(
                "c p w -> p c w")
            lda = dma("sync", dS, abuf.rearrange("p (c w) -> p c w", w=512),
                      src, waits=[(pe, ar_free_pe[tg % 2]),
                                  (dS, attn_done_dS)])
            for hc in range(DCH):
                for a in range(4):
                    E("tensor", lambda e, a=a, hc=hc, abuf=abuf: e.matmul(
                        P2[:, :],
                        big[:, WO_O + 1024 * a + 128 * hc:
                             WO_O + 1024 * a + 128 * (hc + 1)],
                        abuf[:, 512 * a:512 * (a + 1)],
                        start=(a == 0), stop=(a == 3)),
                      waits=([(dS, lda), (dv, dv.n)] if a == 0 else ()),
                      inc=(pe if a == 3 else None))
                sti = st[hc % 2]
                E("vector", lambda e, sti=sti: e.tensor_copy(
                    sti[:, 0:512], P2[:, :]),
                  waits=[(pe, pe.n), (dS, st_dma_free[hc % 2])], inc=dv)
                d0 = shs[0, 128 * hc:128 * (hc + 1), 512 * tg:512 * (tg + 1)]
                d1 = shs[1, 128 * hc:128 * (hc + 1), 512 * tg:512 * (tg + 1)]
                dvw = dv.n

                def _st_s(e, d0=d0, d1=d1, sti=sti, dvw=dvw):
                    e.wait_ge(dv.h, dvw)
                    with e.If_eq(regs["sy_role"], 0):
                        e.dma_start(out=d0, in_=sti[:, 0:512]).then_inc(
                            dS.h, 16)
                    with e.Else():
                        e.dma_start(out=d1, in_=sti[:, 0:512]).then_inc(
                            dS.h, 16)
                    return None
                E("sync", _st_s, waits=(), inc=None)
                dS.n += 16
                st_dma_free[hc % 2] = dS.n
            ar_free_pe[tg % 2] = pe.n
        wo_dS = dS.n
        if debug:
            dma("sync", dS, sp_dbg, shs[0], waits=[(dS, wo_dS)])
        sbar_gp = pair_barrier(1, wo_dS)

        hqT = big[:, HQT_O:HQT_O + ICH * 1024]
        wgb = [big[:, WG_O + 4096 * i:WG_O + 4096 * (i + 1)] for i in range(2)]
        wub = [big[:, WU_O + 4096 * i:WU_O + 4096 * (i + 1)] for i in range(2)]
        wdb = big[:, WD_O:WD_O + 2816]
        gst = big[:, GST_O:GST_O + 512]

        wgb_free = [pe.n, pe.n]
        hq_free_pe = pe.n
        gst_free_dA = 0
        ffn_start_pe = pe.n

        for tg in range(2):
            for ib in range(NIB):
                bi = ib % 2
                for hc in range(DCH):
                    ldg = dma("sync", dS,
                              wgb[bi][:, 512 * hc:512 * (hc + 1)].rearrange(
                                  "p (b i) -> p b i", i=128),
                              wg4[:, hc, 4 * ib:4 * (ib + 1), :],
                              waits=([(pe, wgb_free[bi]), (gp, wbar_gp)]
                                     if hc == 0 else ()))
                for hc in range(DCH):
                    ldu = dma("sync", dS,
                              wub[bi][:, 512 * hc:512 * (hc + 1)].rearrange(
                                  "p (b i) -> p b i", i=128),
                              wu4[:, hc, 4 * ib:4 * (ib + 1), :],
                              waits=())
                for tt in range(8):
                    t16 = 8 * tg + tt
                    for c in range(DCH):
                        E("tensor", lambda e, c=c, t16=t16, bi=bi: e.matmul(
                            P1[:, :],
                            xnqT[:, OWN * c + 128 * t16:
                                 OWN * c + 128 * (t16 + 1)],
                            wgb[bi][:, 512 * c:512 * (c + 1)],
                            start=(c == 0), stop=(c == DCH - 1)),
                          waits=([(dS, ldg), (ac, ac.n), (dv, dv.n)]
                                 if c == 0 else ()),
                          inc=(pe if c == DCH - 1 else None))
                    gate_pe = pe.n
                    for c in range(DCH):
                        E("tensor", lambda e, c=c, t16=t16, bi=bi: e.matmul(
                            P3[:, :],
                            xnqT[:, OWN * c + 128 * t16:
                                 OWN * c + 128 * (t16 + 1)],
                            wub[bi][:, 512 * c:512 * (c + 1)],
                            start=(c == 0), stop=(c == DCH - 1)),
                          waits=([(dS, ldu), (dv, dv.n)] if c == 0 else ()),
                          inc=(pe if c == DCH - 1 else None))
                    up_pe = pe.n
                    E("scalar", lambda e: e.activation(
                        hsc[:, :], P1[:, :], AF.Silu),
                      waits=[(pe, gate_pe), (dv, dv.n)], inc=ac)
                    hb = hbuf[tt % 2]
                    E("vector", lambda e, hb=hb: e.tensor_tensor(
                        hb[:, :], P3[:, :], hsc[:, :], ALU.mult),
                      waits=[(ac, ac.n), (pe, up_pe)], inc=dv)
                    qdv = quant(hb, 512, q_out, after=[(dv, dv.n)],
                                dst_waits=[(pe, qout_free_pe)])
                    nb4 = 4 if ib < NIB - 1 else 2
                    tpe = transpose8(q_out, nb4,
                                     waits=[(dv, qdv), (pe, hq_free_pe)])
                    qout_free_pe = tpe
                    for m in range(nb4):
                        ic = 4 * ib + m
                        E("vector", lambda e, ic=ic, m=m, tt=tt: e.tensor_copy(
                            hqT[:, 1024 * ic + 128 * tt:
                                1024 * ic + 128 * (tt + 1)],
                            PT[:, 128 * m:128 * (m + 1)]),
                          waits=([(pe, tpe)] if m == 0 else ()), inc=dv)
                wgb_free[bi] = pe.n
            hq_dv = dv.n
            for hc in range(DCH):
                ldd = dma("sync", dS,
                          wdb.rearrange("p (c w) -> p c w", w=128),
                          wd4[hc, :, :, :],
                          waits=[(pe, pe.n)])
                for half in range(2):
                    for ic in range(ICH):
                        E("tensor", lambda e, ic=ic, half=half: e.matmul(
                            P2[:, :],
                            wdb[:, 128 * ic:128 * (ic + 1)],
                            hqT[:, 1024 * ic + 512 * half:
                                1024 * ic + 512 * (half + 1)],
                            start=(ic == 0), stop=(ic == ICH - 1)),
                          waits=([(dS, ldd), (dv, hq_dv), (ac, ac.n)]
                                 if ic == 0 else ()),
                          inc=(pe if ic == ICH - 1 else None))
                    E("scalar", lambda e: e.activation(
                        gst[:, :], P2[:, :], AF.Silu),
                      waits=[(pe, pe.n), (dA, gst_free_dA)], inc=ac)
                    gst_free_dA = dma(
                        "scalar", dA,
                        gT_d[hc, :, 1024 * tg + 512 * half:
                             1024 * tg + 512 * (half + 1)],
                        gst[:, :], waits=[(ac, ac.n)])
            hq_free_pe = pe.n
        ffn_dA = dA.n

        selA = t_sel[:, 0:1]
        selB = t_sel[:, 1:2]
        shs3 = [shs[p].rearrange("(c p2) w -> c p2 w", p2=128)
                for p in range(2)]
        fo_free_dS = 0
        for t in range(NTO):
            lda_ = dma("scalar", dA,
                       f_sA[:, :].rearrange("p (c w) -> p c w", w=128),
                       shs3[0][:, :, 128 * t:128 * (t + 1)].rearrange(
                           "c p w -> p c w"),
                       waits=[(gp, sbar_gp), (dv, dv.n), (dA, ffn_dA)])
            lda2_ = dma("scalar", dA,
                        f_sA2[:, :].rearrange("p (c w) -> p c w", w=128),
                        shs3[1][:, :, 128 * t:128 * (t + 1)].rearrange(
                            "c p w -> p c w"),
                        waits=[])
            ldb_ = dma("scalar", dA,
                       f_sB[:, :].rearrange("p (c w) -> p c w", w=128),
                       shs3[0][:, :, OWN + 128 * t:OWN + 128 * (t + 1)
                               ].rearrange("c p w -> p c w"),
                       waits=[])
            ldb2_ = dma("scalar", dA,
                        f_sB2[:, :].rearrange("p (c w) -> p c w", w=128),
                        shs3[1][:, :, OWN + 128 * t:OWN + 128 * (t + 1)
                                ].rearrange("c p w -> p c w"),
                        waits=[])
            ldg_ = dma("scalar", dA,
                       f_g[:, :].rearrange("p (c w) -> p c w", w=128),
                       gT_d[:, :, 128 * t:128 * (t + 1)].rearrange(
                           "c p w -> p c w"),
                       waits=[])
            ldn_ = dma("scalar", dA,
                       f_xn[:, :].rearrange("p (c w) -> p c w", w=128),
                       xnTo_d[:, :, 128 * t:128 * (t + 1)].rearrange(
                           "c p w -> p c w"),
                       waits=[])
            ldx_ = dma("sync", dS, t_x[t % 2][:, :],
                       x_own[128 * t:128 * (t + 1), :],
                       waits=[(dv, dv.n)])
            E("vector", lambda e: e.tensor_tensor(
                f_t[:, :], f_sB[:, :], f_sB2[:, :], ALU.add),
              waits=[(dA, ldb2_)], inc=dv)
            E("vector", lambda e: e.tensor_scalar(
                f_s[:, :], f_t[:, :], selB, None, ALU.mult),
              waits=[(dv, dv.n)], inc=dv)
            E("vector", lambda e: e.tensor_tensor(
                f_t[:, :], f_sA[:, :], f_sA2[:, :], ALU.add),
              waits=[(dA, lda2_), (dv, dv.n)], inc=dv)
            E("vector", lambda e: e.scalar_tensor_tensor(
                f_s2[:, :], f_t[:, :], selA, f_s[:, :], ALU.mult, ALU.add),
              waits=[(dv, dv.n)], inc=dv)
            E("vector", lambda e: e.tensor_tensor(
                f_t[:, :], f_s2[:, :], f_xn[:, :], ALU.add),
              waits=[(dA, ldn_), (dv, dv.n)], inc=dv)
            E("vector", lambda e: e.tensor_tensor(
                f_m[:, :], f_t[:, :], f_g[:, :], ALU.mult),
              waits=[(dA, ldg_), (dv, dv.n)], inc=dv)
            tpe = transpose8(f_m, 8, waits=[(dv, dv.n)])
            E("vector", lambda e, t=t: e.tensor_tensor(
                f_o[:, :], PT[:, :], t_x[t % 2][:, :], ALU.add),
              waits=[(pe, tpe), (dS, ldx_), (dS, fo_free_dS)], inc=dv)
            fo_free_dS = dma("sync", dS, out_own[128 * t:128 * (t + 1), :],
                             f_o[:, :], waits=[(dv, dv.n)])
        E("sync", lambda e: e.nop(), waits=[(dS, dS.n)], inc=None)

        with nc.Block() as block:
            def runner(engname):
                def _(e):
                    for waits, fn, inc, k in sch.steps[engname]:
                        for (s, v) in waits:
                            e.wait_ge(s.h, v)
                        instr = fn(e)
                        if inc is not None:
                            assert instr is not None, engname
                            instr.then_inc(inc.h, k)
                return _
            block.tensor(runner("tensor"))
            block.vector(runner("vector"))
            block.scalar(runner("scalar"))
            block.gpsimd(runner("gpsimd"))
            block.sync(runner("sync"))

    return nc


_cache = {}


def _get_nc():
    if "nc" not in _cache:
        _cache["nc"] = build_program(DEBUG)
    return _cache["nc"]


def kernel(L0_attn_x, L0_attn_Wq, L0_attn_Wk, L0_attn_Wv, L0_attn_Wo,
           L0_ffn_w_gate, L0_ffn_w_up, L0_ffn_w_down):
    x = np.ascontiguousarray(np.asarray(L0_attn_x, dtype=np.float32))
    Wq = np.ascontiguousarray(np.asarray(L0_attn_Wq, dtype=np.float32))
    Wk = np.ascontiguousarray(np.asarray(L0_attn_Wk, dtype=np.float32))
    Wv = np.ascontiguousarray(np.asarray(L0_attn_Wv, dtype=np.float32))
    Wo = np.ascontiguousarray(np.asarray(L0_attn_Wo, dtype=np.float32))
    wg = np.ascontiguousarray(np.asarray(L0_ffn_w_gate, dtype=np.float32))
    wu = np.ascontiguousarray(np.asarray(L0_ffn_w_up, dtype=np.float32))
    wd = np.ascontiguousarray(np.asarray(L0_ffn_w_down, dtype=np.float32))

    wg_p = np.zeros((IPAD, D), dtype=np.float32)
    wg_p[:IFF] = wg
    wu_p = np.zeros((IPAD, D), dtype=np.float32)
    wu_p[:IFF] = wu

    ident = np.eye(128, dtype=np.float32).astype(ml_dtypes.bfloat16)
    r_idx = np.arange(128)
    maskt = np.where(r_idx[:, None] <= r_idx[None, :], 0.0, NEG).astype(
        np.float32)

    epoch = int(np.random.randint(1, 2**30))
    in_maps = []
    for core in range(8):
        b = core // 2
        r = core % 2
        selv = np.zeros((128, 2), dtype=np.float32)
        selv[:, r] = 1.0
        in_maps.append({
            "xb": x[b],
            "x_own": x[b, r * OWN:(r + 1) * OWN],
            "wq_in": Wq[:, 512 * r:512 * (r + 1)],
            "wk_in": Wk[:, 512 * r:512 * (r + 1)],
            "wv_in": Wv[:, 512 * r:512 * (r + 1)],
            "wo_in": Wo[512 * r:512 * (r + 1), :],
            "wg_sh": wg_p[1536 * r:1536 * (r + 1)],
            "wu_sh": wu_p[1536 * r:1536 * (r + 1)],
            "wd_sh": wd[512 * r:512 * (r + 1)],
            "ident": ident,
            "maskt": maskt,
            "selv": selv,
            "rolei": np.array([[r, 0]], dtype=np.int32),
            "epochi": np.array([[epoch, 0]], dtype=np.int32),
        })

    nc = _get_nc()
    res = run_bass_kernel_spmd(nc, in_maps, core_ids=list(range(8)))
    out = np.empty((4, S, D), dtype=np.float32)
    for core in range(8):
        b = core // 2
        r = core % 2
        out[b, r * OWN:(r + 1) * OWN] = res.results[core]["out_own"]
    return out


# version 09
# speedup vs baseline: 1.1200x; 1.1200x over previous
import numpy as np
import ml_dtypes
from contextlib import ExitStack

import concourse.bass as bass
import concourse.mybir as mybir
from concourse.bass_utils import run_bass_kernel_spmd

F32 = mybir.dt.float32
BF16 = mybir.dt.bfloat16
I16 = mybir.dt.int16
ALU = mybir.AluOpType
AF = mybir.ActivationFunctionType
AX = mybir.AxisListType

DEBUG = False

D = 1024
DCH = 8
S = 4096
OWN = 2048
NT = S // 128
NTO = OWN // 128
HPC = 8
NHP = HPC // 2
NQG = S // 512
DK = 64
IFF = 2816
IPAD = 3072
ICH = 22
NIB = IPAD // 512
NEG = -1.0e30

KT_O, QT_O, VH_O = 0, 4096, 8192
XR_O = 12352
PTND_O = 20544
PTDG_O = 24640
WQ_O, WK_O, WV_O, WO_O = 26688, 30784, 34880, 38976
BIG_N = 43072
HQT_O = 0
WG_O = 22528
WU_O = 30720
WD_O = 38912
GST_O = 41728
AR_O = 0


class Sem:
    def __init__(self, h):
        self.h = h
        self.n = 0


class Sched:

    ENGS = ("tensor", "vector", "scalar", "gpsimd", "sync")

    def __init__(self):
        self.steps = {e: [] for e in self.ENGS}

    def emit(self, eng, fn, waits=(), inc=None, inc_by=1):
        w = [(s, v) for (s, v) in waits if v > 0]
        self.steps[eng].append((w, fn, inc, inc_by))
        if inc is not None:
            inc.n += inc_by
        return inc.n if inc is not None else 0


def build_program(debug=False):
    nc = bass.Bass("TRN2", target_bir_lowering=False)

    xb = nc.dram_tensor("xb", [S, D], F32, kind="ExternalInput").ap()
    x_own = nc.dram_tensor("x_own", [OWN, D], F32, kind="ExternalInput").ap()
    wq_in = nc.dram_tensor("wq_in", [D, 512], F32, kind="ExternalInput").ap()
    wk_in = nc.dram_tensor("wk_in", [D, 512], F32, kind="ExternalInput").ap()
    wv_in = nc.dram_tensor("wv_in", [D, 512], F32, kind="ExternalInput").ap()
    wo_in = nc.dram_tensor("wo_in", [512, D], F32, kind="ExternalInput").ap()
    wg_sh = nc.dram_tensor("wg_sh", [12 * 128, D], F32,
                           kind="ExternalInput").ap()
    wu_sh = nc.dram_tensor("wu_sh", [12 * 128, D], F32,
                           kind="ExternalInput").ap()
    wd_sh = nc.dram_tensor("wd_sh", [4 * 128, IFF], F32,
                           kind="ExternalInput").ap()
    rolei = nc.dram_tensor("rolei", [1, 2], mybir.dt.int32,
                           kind="ExternalInput").ap()
    epochi = nc.dram_tensor("epochi", [1, 2], mybir.dt.int32,
                            kind="ExternalInput").ap()
    ident = nc.dram_tensor("ident", [128, 128], BF16, kind="ExternalInput").ap()
    maskt = nc.dram_tensor("maskt", [128, 128], F32, kind="ExternalInput").ap()
    selv = nc.dram_tensor("selv", [128, 2], F32, kind="ExternalInput").ap()
    out_own = nc.dram_tensor("out_own", [OWN, D], F32, kind="ExternalOutput").ap()

    dkind = "ExternalOutput" if debug else "Internal"
    xnT_d = nc.dram_tensor("xnT_d", [DCH, 128, S], BF16, kind=dkind).ap()
    xnTo_d = nc.dram_tensor("xnTo_d", [DCH, 128, OWN], BF16, kind=dkind).ap()
    attnT_d = nc.dram_tensor("attnT_d", [NHP, 128, S], BF16, kind=dkind).ap()
    gT_d = nc.dram_tensor("gT_d", [DCH, 128, OWN], BF16, kind=dkind).ap()
    shwg = nc.dram_tensor("shwg", [24 * D, 128], BF16,
                          addr_space="Shared").ap()
    shwu = nc.dram_tensor("shwu", [24 * D, 128], BF16,
                          addr_space="Shared").ap()
    shwd = nc.dram_tensor("shwd", [8 * IFF, 128], BF16,
                          addr_space="Shared").ap()
    shs = nc.dram_tensor("shs", [2, DCH * 128, S], BF16,
                         addr_space="Shared").ap()
    shflag = nc.dram_tensor("shflag", [2, 2, 16], mybir.dt.int32,
                            addr_space="Shared").ap()
    if debug:
        xnq_dbg = nc.dram_tensor("xnq_dbg", [DCH, 128, OWN], BF16,
                                 kind="ExternalOutput").ap()
        sp_dbg = nc.dram_tensor("sp_dbg", [DCH * 128, S], BF16,
                                kind="ExternalOutput").ap()
        kqv_dbg = nc.dram_tensor("kqv_dbg", [3, 128, 4160], BF16,
                                 kind="ExternalOutput").ap()
        p_dbg = nc.dram_tensor("p_dbg", [6, 128, 2048], BF16,
                               kind="ExternalOutput").ap()
        av_dbg = nc.dram_tensor("av_dbg", [4, 128, 512], F32,
                                kind="ExternalOutput").ap()

    sch = Sched()
    es = ExitStack()

    def sb(name, shape, dtype):
        return es.enter_context(nc.sbuf_tensor(name, shape, dtype))

    def ps(name, shape, dtype):
        return es.enter_context(nc.psum_tensor(name, shape, dtype))

    def sem(name):
        return Sem(es.enter_context(nc.semaphore(name)))

    with es:
        big = sb("big", [128, BIG_N], BF16)
        xnqT = sb("xnqT", [128, DCH * OWN], BF16)
        t_x = [sb(f"t_x{i}", [128, D], F32) for i in range(2)]
        t_xn = sb("t_xn", [128, D], F32)
        t_xnb = sb("t_xnb", [128, D], BF16)
        bnb = sb("bnb", [128, 12], F32)
        mv = sb("mv", [128, 2], F32)
        ln_s = sb("ln_s", [128, 4], F32)
        lnsv = sb("lnsv", [128, 2 * NTO], F32)
        q_xn = sb("q_xn", [128, 1024], F32)
        q_a = sb("q_a", [128, 1024], F32)
        q_i1 = sb("q_i1", [128, 1024], I16)
        q_i2 = sb("q_i2", [128, 1024], I16)
        q_i3 = sb("q_i3", [128, 1024], I16)
        q_t1 = sb("q_t1", [128, 1024], BF16)
        q_t2 = sb("q_t2", [128, 1024], BF16)
        q_sg = sb("q_sg", [128, 1024], BF16)
        q_out = sb("q_out", [128, 1024], BF16)
        q_bm = sb("q_bm", [128, 64], F32)
        q_S = sb("q_S", [128, 64], F32)
        q_R = sb("q_R", [128, 64], F32)
        q_Sb = sb("q_Sb", [128, 64], BF16)
        st = [sb(f"st{i}", [128, 1024], BF16) for i in range(2)]
        t_mask = sb("t_mask", [128, 128], F32)
        t_id = sb("t_id", [128, 128], BF16)
        t_sel = sb("t_sel", [128, 2], F32)
        rden = [sb(f"rden{i}", [128, 512], F32) for i in range(2)]
        rb64 = [sb(f"rb64{i}", [128, 512], F32) for i in range(2)]
        outst = [sb(f"outst{i}", [128, 512], BF16) for i in range(2)]
        t_ones = sb("t_ones", [128, 64], F32)
        t_ri = sb("t_ri", [1, 2], mybir.dt.int32)
        t_ep = sb("t_ep", [1, 16], mybir.dt.int32)
        t_zz = sb("t_zz", [1, 16], mybir.dt.int32)
        regs = {}
        t_wf = None
        hsc = sb("hsc", [128, 512], F32)
        hbuf = [sb(f"hbuf{i}", [128, 512], F32) for i in range(2)]
        f_sA = sb("f_sA", [128, 1024], BF16)
        f_sA2 = sb("f_sA2", [128, 1024], BF16)
        f_sB = sb("f_sB", [128, 1024], BF16)
        f_sB2 = sb("f_sB2", [128, 1024], BF16)
        f_g = sb("f_g", [128, 1024], BF16)
        f_xn = sb("f_xn", [128, 1024], BF16)
        f_t = sb("f_t", [128, 1024], F32)
        f_s = sb("f_s", [128, 1024], F32)
        f_s2 = sb("f_s2", [128, 1024], F32)
        f_m = sb("f_m", [128, 1024], BF16)
        f_o = sb("f_o", [128, 1024], F32)

        P0 = ps("P0", [128, 2048], F32)
        P1 = ps("P1", [128, 512], F32)
        P2 = ps("P2", [128, 512], F32)
        P3 = ps("P3", [128, 512], F32)
        PT = ps("PT", [128, 1024], BF16)

        dS = sem("dS")
        dA = sem("dA")
        dP = sem("dP")
        pe = sem("pe")
        dv = sem("dv")
        ac = sem("ac")
        gp = sem("gp")

        E = sch.emit

        def dma(q, qs, out, in_, waits=()):
            return E(q, lambda e, o=out, i=in_: e.dma_start(out=o, in_=i),
                     waits=waits, inc=qs, inc_by=16)

        def quant(src_f32, n, dst_bf16, after=(), dst_waits=()):
            for _ in quant_gen(src_f32, n, dst_bf16, after, dst_waits):
                pass
            return dv.n

        def quant_gen(src_f32, n, dst_bf16, after=(), dst_waits=()):
            assert n % 32 == 0
            h = n // 2
            g = h // 16

            co = [0, h]
            go = [0, g]

            def stp(fn, waits=()):
                E("vector", fn, waits=waits, inc=dv)

            w0 = list(after)
            for i in range(2):
                c, gg = co[i], go[i]
                x3 = src_f32[:, c:c + h].rearrange("p (g w) -> p g w", w=16)
                stp(lambda e, x3=x3, gg=gg: e.tensor_reduce(
                    q_bm[:, gg:gg + g], x3, AX.X, ALU.max,
                    apply_absolute_value=True), waits=w0)
                w0 = ()
            yield
            for i in range(2):
                gg = go[i]
                stp(lambda e, gg=gg: e.tensor_scalar(
                    q_S[:, gg:gg + g], q_bm[:, gg:gg + g], 1.0 / 6.0, 1e-12,
                    ALU.mult, ALU.max))
            yield
            for i in range(2):
                gg = go[i]
                stp(lambda e, gg=gg: e.reciprocal(
                    q_R[:, gg:gg + g], q_S[:, gg:gg + g]))
            yield
            for i in range(2):
                gg = go[i]
                stp(lambda e, gg=gg: e.tensor_copy(
                    q_Sb[:, gg:gg + g], q_S[:, gg:gg + g]))
            yield
            for i in range(2):
                c, gg = co[i], go[i]
                x3 = src_f32[:, c:c + h].rearrange("p (g w) -> p g w", w=16)
                rb = q_R[:, gg:gg + g].unsqueeze(2).broadcast_to((128, g, 16))
                o3 = q_xn[:, c:c + h].rearrange("p (g w) -> p g w", w=16)
                stp(lambda e, x3=x3, rb=rb, o3=o3: e.tensor_tensor(
                    o3, x3, rb, ALU.mult))
            yield
            for i in range(2):
                c = co[i]
                stp(lambda e, c=c: e.tensor_scalar(
                    q_sg[:, c:c + h], q_xn[:, c:c + h], 0.0, 2.0,
                    ALU.is_gt, ALU.mult))
            yield
            for i in range(2):
                c = co[i]
                stp(lambda e, c=c: e.scalar_tensor_tensor(
                    q_a[:, c:c + h], q_sg[:, c:c + h], -1.0,
                    q_xn[:, c:c + h], ALU.add, ALU.mult))
            yield
            for i in range(2):
                c = co[i]
                stp(lambda e, c=c: e.tensor_scalar(
                    q_xn[:, c:c + h], q_a[:, c:c + h], 2.0, 4.0,
                    ALU.mult, ALU.min))
            yield
            for i in range(2):
                c = co[i]
                stp(lambda e, c=c: e.tensor_copy(
                    q_i1[:, c:c + h], q_xn[:, c:c + h]))
            yield
            for i in range(2):
                c = co[i]
                stp(lambda e, c=c: e.tensor_scalar(
                    q_xn[:, c:c + h], q_a[:, c:c + h], 2.0, 4.0,
                    ALU.max, ALU.min))
            yield
            for i in range(2):
                c = co[i]
                stp(lambda e, c=c: e.tensor_copy(
                    q_i2[:, c:c + h], q_xn[:, c:c + h]))
            yield
            for i in range(2):
                c = co[i]
                stp(lambda e, c=c: e.tensor_scalar(
                    q_xn[:, c:c + h], q_a[:, c:c + h], 0.5, 2.0,
                    ALU.mult, ALU.max))
            yield
            for i in range(2):
                c = co[i]
                stp(lambda e, c=c: e.tensor_copy(
                    q_i3[:, c:c + h], q_xn[:, c:c + h]))
            yield
            for i in range(2):
                c = co[i]
                stp(lambda e, c=c: e.scalar_tensor_tensor(
                    q_t1[:, c:c + h], q_i1[:, c:c + h], 0.5, q_i2[:, c:c + h],
                    ALU.mult, ALU.add))
            yield
            for i in range(2):
                c = co[i]
                stp(lambda e, c=c: e.scalar_tensor_tensor(
                    q_t2[:, c:c + h], q_i3[:, c:c + h], 2.0, q_t1[:, c:c + h],
                    ALU.mult, ALU.add))
            yield
            for i in range(2):
                c = co[i]
                stp(lambda e, c=c: e.tensor_scalar(
                    q_t2[:, c:c + h], q_t2[:, c:c + h], -6.0, None, ALU.add))
            yield
            for i in range(2):
                c, gg = co[i], go[i]
                sbb = q_Sb[:, gg:gg + g].unsqueeze(2).broadcast_to((128, g, 16))
                t23 = q_t2[:, c:c + h].rearrange("p (g w) -> p g w", w=16)
                o3 = q_t1[:, c:c + h].rearrange("p (g w) -> p g w", w=16)
                stp(lambda e, t23=t23, sbb=sbb, o3=o3: e.tensor_tensor(
                    o3, t23, sbb, ALU.mult))
            yield
            for i in range(2):
                c = co[i]
                stp(lambda e, c=c: e.scalar_tensor_tensor(
                    dst_bf16[:, c:c + h], q_sg[:, c:c + h], -1.0,
                    q_t1[:, c:c + h], ALU.add, ALU.mult),
                    waits=(list(dst_waits) if i == 0 else ()))
            yield

        def transpose8(src_bf16, nblk, pt_off=0, waits=()):
            for m in range(nblk):
                w = waits if m == 0 else ()
                E("tensor", lambda e, m=m: e.transpose(
                    PT[:, pt_off + 128 * m:pt_off + 128 * (m + 1)],
                    src_bf16[:, 128 * m:128 * (m + 1)], t_id[:, :]),
                    waits=w, inc=(pe if m == nblk - 1 else None))
            return pe.n

        dma("sync", dS, t_id[:, :], ident)
        dma("sync", dS, t_mask[:, :], maskt)
        dma("sync", dS, t_sel[:, :], selv)
        dma("sync", dS, t_ri[0:1, :], rolei)
        dma("sync", dS, t_ep[0:1, 0:2], epochi)
        a_small = dS.n

        def _init_regs_sync(e):
            regs["sy_role"] = e.alloc_register("sy_role")
            e.wait_ge(dS.h, a_small)
            return e.load(regs["sy_role"], t_ri[0:1, 0:1])
        E("sync", _init_regs_sync, waits=(), inc=None)

        def _init_regs_gp(e):
            e.memset(t_zz[0:1, :], 0)
            regs["gp_role"] = e.alloc_register("gp_role")
            regs["gp_f"] = e.alloc_register("gp_f")
            regs["gp_c"] = e.alloc_register("gp_c")
            regs["gp_k"] = e.alloc_register("gp_k")
            e.wait_ge(dS.h, a_small)
            e.load(regs["gp_role"], t_ri[0:1, 0:1])
            e.load(regs["gp_k"], t_ep[0:1, 0:1])
            with e.If_eq(regs["gp_role"], 0):
                e.dma_start(out=shflag[0, 0, 0:2],
                            in_=t_zz[0:1, 0:2]).then_inc(dP.h, 16)
                e.dma_start(out=shflag[0, 1, 0:2],
                            in_=t_zz[0:1, 0:2]).then_inc(dP.h, 16)
            with e.Else():
                e.dma_start(out=shflag[1, 0, 0:2],
                            in_=t_zz[0:1, 0:2]).then_inc(dP.h, 16)
                e.dma_start(out=shflag[1, 1, 0:2],
                            in_=t_zz[0:1, 0:2]).then_inc(dP.h, 16)
            return e.wait_ge(dP.h, 32)
        E("gpsimd", _init_regs_gp, waits=(), inc=None)
        dP.n += 32

        def _barrier_body(e, bar, after_dS, dp_wait):
            e.wait_ge(dS.h, after_dS)
            r = regs["gp_role"]
            f, c, k = regs["gp_f"], regs["gp_c"], regs["gp_k"]
            for me in (0, 1):
                other = 1 - me
                with (e.If_eq(r, 0) if me == 0 else e.Else()):
                    e.dma_start(out=shflag[me, bar, 0:2],
                                in_=t_ep[0:1, 0:2]).then_inc(dP.h, 16)
                    e.wait_ge(dP.h, dp_wait)
                    e.load(f, shflag[other, bar:bar + 1, 0:1])
                    e.reg_alu(c, f, k, ALU.not_equal)
                    with e.While(c):
                        e.load(f, shflag[other, bar:bar + 1, 0:1])
                        e.reg_alu(c, f, k, ALU.not_equal)
            return e.nop()

        def pair_barrier(bar, after_dS):
            dp_wait = dP.n + 16
            E("gpsimd", lambda e, bar=bar, a=after_dS, w=dp_wait:
              _barrier_body(e, bar, a, w), waits=(), inc=gp)
            dP.n += 16
            return gp.n

        for (w_in, w_off, scale, ncol) in [
                (wq_in, WQ_O, 0.125, 512), (wk_in, WK_O, 1.0, 512),
                (wv_in, WV_O, 1.0, 512), (wo_in, WO_O, 1.0, 1024)]:
            nch = w_in.shape[0] // 128
            for pc in range(nch):
                wv_ = dma("sync", dS, t_x[0][:, :ncol],
                          w_in[128 * pc:128 * (pc + 1), :],
                          waits=[(ac, ac.n)])
                E("scalar", lambda e, w_off=w_off, pc=pc, scale=scale,
                  ncol=ncol: e.activation(
                      big[:, w_off + ncol * pc:w_off + ncol * (pc + 1)],
                      t_x[0][:, :ncol], AF.Copy, scale=scale),
                  waits=[(dS, wv_)], inc=ac)
        aw_done_ac = ac.n

        st_dma_free = [0, 0]
        qout_free_pe = 0

        def prep_piece_gen(src_ap, cn, dst0_ap, dst1_ap, par):
            nonlocal qout_free_pe
            land = t_x[par % 2][:, :cn]
            ldw = dma("gpsimd", dP, land, src_ap,
                      waits=[(dv, dv.n), (ac, ac.n)])
            yield
            yield from quant_gen(land, cn, q_out, after=[(dP, ldw)],
                                 dst_waits=[(pe, qout_free_pe)])
            qdv = dv.n
            nb8 = cn // 128
            tpe = transpose8(q_out, nb8,
                             waits=[(dv, qdv), (ac, aw_done_ac)])
            qout_free_pe = tpe
            sti = st[par % 2]
            E("vector", lambda e, sti=sti, nb8=nb8: e.tensor_copy(
                sti[:, :128 * nb8], PT[:, :128 * nb8]),
              waits=[(pe, tpe), (dS, st_dma_free[par % 2])], inc=dv)
            src_st = sti[:, :128 * nb8].rearrange("p (m w) -> p m w", w=128)
            d0 = dst0_ap.rearrange("(m p) w -> p m w", p=128)
            d1 = dst1_ap.rearrange("(m p) w -> p m w", p=128)
            dvw = dv.n

            def _branched_store(e, d0=d0, d1=d1, src_st=src_st, dvw=dvw):
                e.wait_ge(dv.h, dvw)
                with e.If_eq(regs["sy_role"], 0):
                    e.dma_start(out=d0, in_=src_st).then_inc(dS.h, 16)
                with e.Else():
                    e.dma_start(out=d1, in_=src_st).then_inc(dS.h, 16)
                return None
            E("sync", _branched_store, waits=(), inc=None)
            dS.n += 16
            st_dma_free[par % 2] = dS.n
            yield

        def xnq_tile_gen(t, par):
            nonlocal qout_free_pe
            land = t_x[par % 2][:, :]
            ldw = dma("gpsimd", dP, land,
                      x_own[128 * t:128 * (t + 1), :],
                      waits=[(dv, dv.n), (ac, ac.n)])
            E("vector", lambda e, t=t: e.tensor_scalar(
                t_xn[:, :], land[:, :], lnsv[:, 2 * t:2 * t + 1],
                lnsv[:, 2 * t + 1:2 * t + 2], ALU.mult, ALU.add),
              waits=[(dP, ldw), (dv, dv.n)], inc=dv)
            yield
            yield from quant_gen(t_xn, D, q_out, after=[(dv, dv.n)],
                                 dst_waits=[(pe, qout_free_pe)])
            tpe = transpose8(q_out, 8, waits=[(dv, dv.n)])
            qout_free_pe = tpe
            for c in range(DCH):
                E("vector", lambda e, c=c, t=t: e.tensor_copy(
                    xnqT[:, OWN * c + 128 * t:OWN * c + 128 * (t + 1)],
                    PT[:, 128 * c:128 * (c + 1)]),
                  waits=([(pe, tpe)] if c == 0 else ()), inc=dv)
            if debug:
                dstq = xnq_dbg[:, :, 128 * t:128 * (t + 1)].rearrange(
                    "c p w -> p c w")
                dma("sync", dS, dstq,
                    q_out[:, :].rearrange("p (c w) -> p c w", w=128),
                    waits=[(dv, dv.n)])
            yield

        deferred = []

        def build_deferred():
            par = 0
            for (src, sh_dst) in [(wg_sh, shwg), (wu_sh, shwu)]:
                for lb in range(12):
                    deferred.append(prep_piece_gen(
                        src[128 * lb:128 * (lb + 1), :], D,
                        sh_dst[D * lb:D * (lb + 1), :],
                        sh_dst[D * (12 + lb):D * (13 + lb), :], par))
                    par += 1
            for t in range(NTO):
                deferred.append(xnq_tile_gen(t, par))
                par += 1
            for lb in range(4):
                for pc in range(3):
                    c0 = 1024 * pc
                    cn = min(1024, IFF - c0)
                    deferred.append(prep_piece_gen(
                        wd_sh[128 * lb:128 * (lb + 1), c0:c0 + cn], cn,
                        shwd[IFF * lb + c0:IFF * lb + c0 + cn, :],
                        shwd[IFF * (4 + lb) + c0:IFF * (4 + lb) + c0 + cn, :],
                        par))
                    par += 1

        def pull(n):
            k = 0
            while deferred and k < n:
                try:
                    next(deferred[0])
                    k += 1
                except StopIteration:
                    deferred.pop(0)

        def pull_all():
            while deferred:
                pull(1000)

        wg4 = shwg.rearrange("(b c p) i -> p c b i", c=8, p=128)
        wu4 = shwu.rearrange("(b c p) i -> p c b i", c=8, p=128)
        wd4 = shwd.rearrange("(h ic p) w -> h p ic w", h=8, p=128)

        def ln_tile(xsrc, ti, store_dst, save_ti=None):
            land = t_x[ti % 2][:, :]
            ldw = dma("sync", dS, land, xsrc[128 * ti:128 * (ti + 1), :],
                      waits=[(dv, dv.n), (ac, ac.n)])
            E("vector", lambda e: e.bn_stats(bnb[:, 0:6], land[:, 0:512]),
              waits=[(dS, ldw)], inc=dv)
            E("vector", lambda e: e.bn_stats(bnb[:, 6:12], land[:, 512:1024]),
              inc=dv)
            E("vector", lambda e: e.bn_aggr(mv[:, :], bnb[:, :]),
              waits=[(dv, dv.n)], inc=dv)
            E("vector", lambda e: e.tensor_scalar(
                ln_s[:, 0:1], mv[:, 1:2], 1e-5, None, ALU.add),
              waits=[(dv, dv.n)], inc=dv)
            E("scalar", lambda e: e.activation(
                ln_s[:, 1:2], ln_s[:, 0:1], AF.Sqrt),
              waits=[(dv, dv.n)], inc=ac)
            E("vector", lambda e: e.reciprocal(ln_s[:, 2:3], ln_s[:, 1:2]),
              waits=[(ac, ac.n)], inc=dv)
            E("vector", lambda e: e.scalar_tensor_tensor(
                ln_s[:, 3:4], mv[:, 0:1], -1.0, ln_s[:, 2:3],
                ALU.mult, ALU.mult), waits=[(dv, dv.n)], inc=dv)
            if save_ti is not None:
                E("vector", lambda e, t=save_ti: e.tensor_copy(
                    lnsv[:, 2 * t:2 * t + 2], ln_s[:, 2:4]),
                  waits=[(dv, dv.n)], inc=dv)
            E("scalar", lambda e: e.activation(
                t_xnb[:, :], land[:, :], AF.Identity,
                scale=ln_s[:, 2:3], bias=ln_s[:, 3:4]),
              waits=[(dv, dv.n)], inc=ac)
            tpe = transpose8(t_xnb, 8, waits=[(ac, ac.n), (dv, dv.n)])
            sti = st[ti % 2]
            E("vector", lambda e, sti=sti: e.tensor_copy(sti[:, :], PT[:, :]),
              waits=[(pe, tpe), (dS, st_dma_free[ti % 2])], inc=dv)
            dst = store_dst[:, :, 128 * ti:128 * (ti + 1)].rearrange(
                "c p w -> p c w")
            st_dma_free[ti % 2] = dma(
                "sync", dS, dst,
                sti[:, :].rearrange("p (c w) -> p c w", w=128),
                waits=[(dv, dv.n)])
            return st_dma_free[ti % 2]

        ln_dma_at = []
        for ti in range(NT):
            ln_dma_at.append(ln_tile(xb, ti, xnT_d))
        for ti in range(NTO):
            ln_tile(x_own, ti, xnTo_d, save_ti=ti)
        ln_done_dS = dS.n
        ln_done_dv = dv.n
        build_deferred()

        kT = big[:, KT_O:KT_O + 4096]
        qT = big[:, QT_O:QT_O + 4096]
        vh = big[:, VH_O:VH_O + 4160]
        xr = [big[:, XR_O + 4096 * i:XR_O + 4096 * (i + 1)] for i in range(2)]
        pnd = [big[:, PTND_O + 2048 * i:PTND_O + 2048 * (i + 1)]
               for i in range(2)]
        pdg = [big[:, PTDG_O + 512 * j:PTDG_O + 512 * (j + 1)]
               for j in range(4)]

        vh3 = vh.rearrange("p (c w) -> p c w", w=65)
        E("vector", lambda e: e.memset(vh3[:, :, 64:65], 1.0),
          waits=[], inc=dv)
        E("vector", lambda e: e.memset(t_ones[0:1, :], 1.0), inc=dv)
        for j in range(4):
            E("vector", lambda e, j=j: e.memset(pdg[j][:, :], 0.0), inc=dv)
        attn_init_dv = dv.n

        xr_free_pe = [0, 0]
        pnd_free_pe = [0, 0]
        pdg_free_pe = [0, 0, 0, 0]
        p1_free_dv = 0
        p0_free_ac = 0
        p2div_free_dv = 0
        outst_free_dS = [0, 0]
        rr_free_dv = [0, 0]
        dbg_gate_dv = 0
        ogi = 0

        for hp in range(NHP):
            kq_ready_ac = {}
            v_ready_dv = {}
            for tg in range(DCH):
                xbuf = xr[tg % 2]
                lw = [(pe, xr_free_pe[tg % 2]), (dS, ln_dma_at[4 * tg + 3])]
                src = xnT_d[:, :, 512 * tg:512 * (tg + 1)].rearrange(
                    "c p w -> p c w")
                ldx = dma("sync", dS,
                          xbuf.rearrange("p (c w) -> p c w", w=512),
                          src, waits=lw)
                for c in range(DCH):
                    E("tensor", lambda e, c=c, xbuf=xbuf, hp=hp: e.matmul(
                        P2[:, :],
                        big[:, WK_O + 512 * c + 128 * hp:
                             WK_O + 512 * c + 128 * (hp + 1)],
                        xbuf[:, 512 * c:512 * (c + 1)],
                        start=(c == 0), stop=(c == DCH - 1)),
                      waits=([(dS, ldx), (ac, ac.n), (dv, dv.n)]
                             if c == 0 else ()),
                      inc=(pe if c == DCH - 1 else None))
                E("scalar", lambda e, tg=tg: e.activation(
                    kT[:, 512 * tg:512 * (tg + 1)], P2[:, :], AF.Copy),
                  waits=[(pe, pe.n)], inc=ac)
                for c in range(DCH):
                    E("tensor", lambda e, c=c, xbuf=xbuf, hp=hp: e.matmul(
                        P2[:, :],
                        big[:, WQ_O + 512 * c + 128 * hp:
                             WQ_O + 512 * c + 128 * (hp + 1)],
                        xbuf[:, 512 * c:512 * (c + 1)],
                        start=(c == 0), stop=(c == DCH - 1)),
                      waits=([(ac, ac.n)] if c == 0 else ()),
                      inc=(pe if c == DCH - 1 else None))
                E("scalar", lambda e, tg=tg: e.activation(
                    qT[:, 512 * tg:512 * (tg + 1)], P2[:, :], AF.Copy),
                  waits=[(pe, pe.n)], inc=ac)
                kq_ready_ac[tg] = ac.n
                for tt in range(4):
                    for c in range(DCH):
                        E("tensor", lambda e, c=c, tt=tt, xbuf=xbuf, hp=hp:
                          e.matmul(
                              P2[:, 128 * tt:128 * (tt + 1)],
                              xbuf[:, 512 * c + 128 * tt:
                                   512 * c + 128 * (tt + 1)],
                              big[:, WV_O + 512 * c + 128 * hp:
                                   WV_O + 512 * c + 128 * (hp + 1)],
                              start=(c == 0), stop=(c == DCH - 1)),
                          waits=([(ac, kq_ready_ac[tg]),
                                  (dv, dv.n)] if (tt == 0 and c == 0) else ()),
                          inc=(pe if c == DCH - 1 else None))
                vpe = pe.n
                for tt in range(4):
                    c32 = 4 * tg + tt
                    for h2 in range(2):
                        E("vector", lambda e, tt=tt, h2=h2, c32=c32:
                          e.tensor_copy(
                              vh[:, 2080 * h2 + 65 * c32:
                                 2080 * h2 + 65 * c32 + 64],
                              P2[:, 128 * tt + 64 * h2:
                                 128 * tt + 64 * h2 + 64]),
                          waits=([(pe, vpe),
                                  (dv, attn_init_dv)] if tt == 0 and h2 == 0
                                 else ()), inc=dv)
                v_ready_dv[tg] = dv.n
                xr_free_pe[tg % 2] = pe.n
            proj_ac = ac.n
            proj_dv = dv.n
            if debug and hp == 0:
                dma("sync", dS, kqv_dbg[0, :, 0:4096], kT,
                    waits=[(ac, proj_ac), (dv, proj_dv)])
                dma("sync", dS, kqv_dbg[1, :, 0:4096], qT, waits=())
                dma("sync", dS, kqv_dbg[2, :, :], vh, waits=())

            for h2 in range(2):
                r0 = 64 * h2
                for g in range(NQG):
                    nb = g
                    scores_pe = {}
                    exp_ac = {}
                    for b in range(nb + 1):
                        is_diag = (b == nb)
                        if not is_diag:
                            for j4 in range(4):
                                kc = 4 * b + j4
                                w = []
                                if j4 == 0:
                                    w = [(ac, exp_ac.get(b - 1, p0_free_ac)),
                                         (ac, proj_ac), (dv, proj_dv)]
                                E("tensor", lambda e, kc=kc, j4=j4, r0=r0, g=g:
                                  e.matmul(
                                      P0[:, 512 * j4:512 * (j4 + 1)],
                                      kT[r0:r0 + 64,
                                         128 * kc:128 * (kc + 1)],
                                      qT[r0:r0 + 64, 512 * g:512 * (g + 1)],
                                      start=True, stop=True),
                                  waits=w, inc=(pe if j4 == 3 else None))
                            scores_pe[b] = pe.n
                        else:
                            for j in range(4):
                                kc = 4 * g + j
                                nw = 512 - 128 * j
                                w = [(ac, exp_ac.get(b - 1, p0_free_ac))] \
                                    if j == 0 else []
                                if j == 0:
                                    w += [(ac, proj_ac), (dv, proj_dv)]
                                E("tensor", lambda e, kc=kc, j=j, nw=nw,
                                  r0=r0, g=g: e.matmul(
                                      P0[:, 512 * j:512 * j + nw],
                                      kT[r0:r0 + 64,
                                         128 * kc:128 * (kc + 1)],
                                      qT[r0:r0 + 64,
                                         512 * g + 128 * j:512 * (g + 1)],
                                      start=True, stop=True),
                                  waits=w, inc=pe)
                                scores_pe[(b, j)] = pe.n
                        if b >= 1:
                            pb = pnd[(b - 1) % 2]
                            for j4 in range(4):
                                kc = 4 * (b - 1) + j4
                                w = []
                                if j4 == 0:
                                    w = [(ac, exp_ac[b - 1]),
                                         (dv, dbg_gate_dv)]
                                    if b == 1:
                                        w.append((dv, p1_free_dv))
                                E("tensor", lambda e, kc=kc, j4=j4, h2=h2,
                                  b=b, pb=pb: e.matmul(
                                      P1[0:65, :],
                                      vh[:, 2080 * h2 + 65 * kc:
                                         2080 * h2 + 65 * kc + 65],
                                      pb[:, 512 * j4:512 * (j4 + 1)],
                                      start=(b == 1 and j4 == 0), stop=False),
                                  waits=w, inc=(pe if j4 == 3 else None))
                            pnd_free_pe[(b - 1) % 2] = pe.n
                            if debug and hp == 0 and h2 == 1 and g == 0:
                                E("vector", lambda e: e.tensor_copy(
                                    hbuf[0][:, :], P1[:, :]),
                                  waits=[(pe, pe.n)], inc=dv)
                                dma("sync", dS, av_dbg[b], hbuf[0][:, :],
                                    waits=[(dv, dv.n)])
                                dbg_gate_dv = dv.n
                        if not is_diag:
                            E("scalar", lambda e, b=b: e.activation(
                                pnd[b % 2][:, :], P0[:, :], AF.Exp),
                              waits=[(pe, scores_pe[b]),
                                     (pe, pnd_free_pe[b % 2])], inc=ac)
                            exp_ac[b] = ac.n
                            pull(2)
                        else:
                            for j in range(4):
                                E("vector", lambda e, j=j: e.tensor_tensor(
                                    P0[:, 512 * j:512 * j + 128],
                                    P0[:, 512 * j:512 * j + 128],
                                    t_mask[:, :], ALU.add),
                                  waits=[(pe, scores_pe[(b, j)])], inc=dv)
                                mdv = dv.n
                                nw = 512 - 128 * j
                                E("scalar", lambda e, j=j, nw=nw:
                                  e.activation(
                                      pdg[j][:, 128 * j:128 * j + nw],
                                      P0[:, 512 * j:512 * j + nw], AF.Exp),
                                  waits=[(dv, mdv),
                                         (pe, pdg_free_pe[j])], inc=ac)
                                exp_ac[(b, j)] = ac.n
                    p0_free_ac = ac.n
                    if debug and hp == 0 and h2 == 1 and g == 0:
                        dma("sync", dS, p_dbg[0], pnd[0],
                            waits=[(ac, ac.n)])
                        dma("sync", dS, p_dbg[1], pnd[1], waits=())
                        for j in range(4):
                            dma("sync", dS, p_dbg[2 + j][:, 0:512], pdg[j],
                                waits=())
                    for j in range(4):
                        kc = 4 * g + j
                        w = [(ac, exp_ac[(nb, j)]), (dv, dbg_gate_dv)]
                        if nb == 0 and j == 0:
                            w.append((dv, p1_free_dv))
                        E("tensor", lambda e, kc=kc, j=j, h2=h2, g=g, nb=nb:
                          e.matmul(
                              P1[0:65, :],
                              vh[:, 2080 * h2 + 65 * kc:
                                 2080 * h2 + 65 * kc + 65],
                              pdg[j][:, :],
                              start=(nb == 0 and j == 0), stop=(j == 3)),
                          waits=w, inc=(pe if j == 3 else None))
                        pdg_free_pe[j] = pe.n
                    av_pe = pe.n
                    if debug and hp == 0 and h2 == 1 and g == 0:
                        E("vector", lambda e: e.tensor_copy(
                            hbuf[0][:, :], P1[:, :]),
                          waits=[(pe, av_pe)], inc=dv)
                        dma("sync", dS, av_dbg[3], hbuf[0][:, :],
                            waits=[(dv, dv.n)])
                    oi = ogi % 2
                    E("vector", lambda e, oi=oi: e.reciprocal(
                        rden[oi][0:1, :], P1[64:65, :]),
                      waits=[(pe, av_pe), (pe, rr_free_dv[oi])], inc=dv)
                    rdv = dv.n
                    E("tensor", lambda e, oi=oi: e.matmul(
                        P2[0:64, :], t_ones[0:1, 0:64], rden[oi][0:1, :],
                        start=True, stop=True),
                      waits=[(dv, max(rdv, p2div_free_dv, proj_dv))], inc=pe)
                    rr_free_dv[oi] = pe.n
                    E("vector", lambda e, oi=oi: e.tensor_copy(
                        rb64[oi][0:64, :], P2[0:64, :]),
                      waits=[(pe, pe.n)], inc=dv)
                    p2div_free_dv = dv.n
                    E("vector", lambda e, oi=oi: e.tensor_tensor(
                        outst[oi][0:64, :], P1[0:64, :], rb64[oi][0:64, :],
                        ALU.mult),
                      waits=[(dS, outst_free_dS[oi]), (dv, dv.n)], inc=dv)
                    p1_free_dv = dv.n
                    od = dma("sync", dS,
                             attnT_d[hp, r0:r0 + 64, 512 * g:512 * (g + 1)],
                             outst[oi][0:64, :], waits=[(dv, dv.n)])
                    outst_free_dS[oi] = od
                    ogi += 1
                    pull(10)

        pull_all()
        wprep_dS = dS.n
        wbar_gp = pair_barrier(0, wprep_dS)
        attn_done_dS = dS.n

        ar = [big[:, AR_O + 2048 * i:AR_O + 2048 * (i + 1)] for i in range(2)]
        ar_free_pe = [pe.n, pe.n]
        for tg in range(DCH):
            abuf = ar[tg % 2]
            src = attnT_d[:, :, 512 * tg:512 * (tg + 1)].rearrange(
                "c p w -> p c w")
            lda = dma("sync", dS, abuf.rearrange("p (c w) -> p c w", w=512),
                      src, waits=[(pe, ar_free_pe[tg % 2]),
                                  (dS, attn_done_dS)])
            for hc in range(DCH):
                for a in range(4):
                    E("tensor", lambda e, a=a, hc=hc, abuf=abuf: e.matmul(
                        P2[:, :],
                        big[:, WO_O + 1024 * a + 128 * hc:
                             WO_O + 1024 * a + 128 * (hc + 1)],
                        abuf[:, 512 * a:512 * (a + 1)],
                        start=(a == 0), stop=(a == 3)),
                      waits=([(dS, lda), (dv, dv.n)] if a == 0 else ()),
                      inc=(pe if a == 3 else None))
                sti = st[hc % 2]
                E("vector", lambda e, sti=sti: e.tensor_copy(
                    sti[:, 0:512], P2[:, :]),
                  waits=[(pe, pe.n), (dS, st_dma_free[hc % 2])], inc=dv)
                d0 = shs[0, 128 * hc:128 * (hc + 1), 512 * tg:512 * (tg + 1)]
                d1 = shs[1, 128 * hc:128 * (hc + 1), 512 * tg:512 * (tg + 1)]
                dvw = dv.n

                def _st_s(e, d0=d0, d1=d1, sti=sti, dvw=dvw):
                    e.wait_ge(dv.h, dvw)
                    with e.If_eq(regs["sy_role"], 0):
                        e.dma_start(out=d0, in_=sti[:, 0:512]).then_inc(
                            dS.h, 16)
                    with e.Else():
                        e.dma_start(out=d1, in_=sti[:, 0:512]).then_inc(
                            dS.h, 16)
                    return None
                E("sync", _st_s, waits=(), inc=None)
                dS.n += 16
                st_dma_free[hc % 2] = dS.n
            ar_free_pe[tg % 2] = pe.n
        wo_dS = dS.n
        if debug:
            dma("sync", dS, sp_dbg, shs[0], waits=[(dS, wo_dS)])
        sbar_gp = pair_barrier(1, wo_dS)

        hqT = big[:, HQT_O:HQT_O + ICH * 1024]
        wgb = [big[:, WG_O + 4096 * i:WG_O + 4096 * (i + 1)] for i in range(2)]
        wub = [big[:, WU_O + 4096 * i:WU_O + 4096 * (i + 1)] for i in range(2)]
        wdb = big[:, WD_O:WD_O + 2816]
        gst = big[:, GST_O:GST_O + 512]

        wgb_free = [pe.n, pe.n]
        hq_free_pe = pe.n
        gst_free_dA = 0
        ffn_start_pe = pe.n

        for tg in range(2):
            for ib in range(NIB):
                bi = ib % 2
                for hc in range(DCH):
                    ldg = dma("sync", dS,
                              wgb[bi][:, 512 * hc:512 * (hc + 1)].rearrange(
                                  "p (b i) -> p b i", i=128),
                              wg4[:, hc, 4 * ib:4 * (ib + 1), :],
                              waits=([(pe, wgb_free[bi]), (gp, wbar_gp)]
                                     if hc == 0 else ()))
                for hc in range(DCH):
                    ldu = dma("sync", dS,
                              wub[bi][:, 512 * hc:512 * (hc + 1)].rearrange(
                                  "p (b i) -> p b i", i=128),
                              wu4[:, hc, 4 * ib:4 * (ib + 1), :],
                              waits=())
                for tt in range(8):
                    t16 = 8 * tg + tt
                    for c in range(DCH):
                        E("tensor", lambda e, c=c, t16=t16, bi=bi: e.matmul(
                            P1[:, :],
                            xnqT[:, OWN * c + 128 * t16:
                                 OWN * c + 128 * (t16 + 1)],
                            wgb[bi][:, 512 * c:512 * (c + 1)],
                            start=(c == 0), stop=(c == DCH - 1)),
                          waits=([(dS, ldg), (ac, ac.n), (dv, dv.n)]
                                 if c == 0 else ()),
                          inc=(pe if c == DCH - 1 else None))
                    gate_pe = pe.n
                    for c in range(DCH):
                        E("tensor", lambda e, c=c, t16=t16, bi=bi: e.matmul(
                            P3[:, :],
                            xnqT[:, OWN * c + 128 * t16:
                                 OWN * c + 128 * (t16 + 1)],
                            wub[bi][:, 512 * c:512 * (c + 1)],
                            start=(c == 0), stop=(c == DCH - 1)),
                          waits=([(dS, ldu), (dv, dv.n)] if c == 0 else ()),
                          inc=(pe if c == DCH - 1 else None))
                    up_pe = pe.n
                    E("scalar", lambda e: e.activation(
                        hsc[:, :], P1[:, :], AF.Silu),
                      waits=[(pe, gate_pe), (dv, dv.n)], inc=ac)
                    hb = hbuf[tt % 2]
                    E("vector", lambda e, hb=hb: e.tensor_tensor(
                        hb[:, :], P3[:, :], hsc[:, :], ALU.mult),
                      waits=[(ac, ac.n), (pe, up_pe)], inc=dv)
                    qdv = quant(hb, 512, q_out, after=[(dv, dv.n)],
                                dst_waits=[(pe, qout_free_pe)])
                    nb4 = 4 if ib < NIB - 1 else 2
                    tpe = transpose8(q_out, nb4,
                                     waits=[(dv, qdv), (pe, hq_free_pe)])
                    qout_free_pe = tpe
                    for m in range(nb4):
                        ic = 4 * ib + m
                        E("vector", lambda e, ic=ic, m=m, tt=tt: e.tensor_copy(
                            hqT[:, 1024 * ic + 128 * tt:
                                1024 * ic + 128 * (tt + 1)],
                            PT[:, 128 * m:128 * (m + 1)]),
                          waits=([(pe, tpe)] if m == 0 else ()), inc=dv)
                wgb_free[bi] = pe.n
            hq_dv = dv.n
            for hc in range(DCH):
                ldd = dma("sync", dS,
                          wdb.rearrange("p (c w) -> p c w", w=128),
                          wd4[hc, :, :, :],
                          waits=[(pe, pe.n)])
                for half in range(2):
                    for ic in range(ICH):
                        E("tensor", lambda e, ic=ic, half=half: e.matmul(
                            P2[:, :],
                            wdb[:, 128 * ic:128 * (ic + 1)],
                            hqT[:, 1024 * ic + 512 * half:
                                1024 * ic + 512 * (half + 1)],
                            start=(ic == 0), stop=(ic == ICH - 1)),
                          waits=([(dS, ldd), (dv, hq_dv), (ac, ac.n)]
                                 if ic == 0 else ()),
                          inc=(pe if ic == ICH - 1 else None))
                    E("scalar", lambda e: e.activation(
                        gst[:, :], P2[:, :], AF.Silu),
                      waits=[(pe, pe.n), (dA, gst_free_dA)], inc=ac)
                    gst_free_dA = dma(
                        "scalar", dA,
                        gT_d[hc, :, 1024 * tg + 512 * half:
                             1024 * tg + 512 * (half + 1)],
                        gst[:, :], waits=[(ac, ac.n)])
            hq_free_pe = pe.n
        ffn_dA = dA.n

        selA = t_sel[:, 0:1]
        selB = t_sel[:, 1:2]
        shs3 = [shs[p].rearrange("(c p2) w -> c p2 w", p2=128)
                for p in range(2)]
        fo_free_dS = 0
        for t in range(NTO):
            lda_ = dma("scalar", dA,
                       f_sA[:, :].rearrange("p (c w) -> p c w", w=128),
                       shs3[0][:, :, 128 * t:128 * (t + 1)].rearrange(
                           "c p w -> p c w"),
                       waits=[(gp, sbar_gp), (dv, dv.n), (dA, ffn_dA)])
            lda2_ = dma("scalar", dA,
                        f_sA2[:, :].rearrange("p (c w) -> p c w", w=128),
                        shs3[1][:, :, 128 * t:128 * (t + 1)].rearrange(
                            "c p w -> p c w"),
                        waits=[])
            ldb_ = dma("scalar", dA,
                       f_sB[:, :].rearrange("p (c w) -> p c w", w=128),
                       shs3[0][:, :, OWN + 128 * t:OWN + 128 * (t + 1)
                               ].rearrange("c p w -> p c w"),
                       waits=[])
            ldb2_ = dma("scalar", dA,
                        f_sB2[:, :].rearrange("p (c w) -> p c w", w=128),
                        shs3[1][:, :, OWN + 128 * t:OWN + 128 * (t + 1)
                                ].rearrange("c p w -> p c w"),
                        waits=[])
            ldg_ = dma("scalar", dA,
                       f_g[:, :].rearrange("p (c w) -> p c w", w=128),
                       gT_d[:, :, 128 * t:128 * (t + 1)].rearrange(
                           "c p w -> p c w"),
                       waits=[])
            ldn_ = dma("scalar", dA,
                       f_xn[:, :].rearrange("p (c w) -> p c w", w=128),
                       xnTo_d[:, :, 128 * t:128 * (t + 1)].rearrange(
                           "c p w -> p c w"),
                       waits=[])
            ldx_ = dma("sync", dS, t_x[t % 2][:, :],
                       x_own[128 * t:128 * (t + 1), :],
                       waits=[(dv, dv.n)])
            E("vector", lambda e: e.tensor_tensor(
                f_t[:, :], f_sB[:, :], f_sB2[:, :], ALU.add),
              waits=[(dA, ldb2_)], inc=dv)
            E("vector", lambda e: e.tensor_scalar(
                f_s[:, :], f_t[:, :], selB, None, ALU.mult),
              waits=[(dv, dv.n)], inc=dv)
            E("vector", lambda e: e.tensor_tensor(
                f_t[:, :], f_sA[:, :], f_sA2[:, :], ALU.add),
              waits=[(dA, lda2_), (dv, dv.n)], inc=dv)
            E("vector", lambda e: e.scalar_tensor_tensor(
                f_s2[:, :], f_t[:, :], selA, f_s[:, :], ALU.mult, ALU.add),
              waits=[(dv, dv.n)], inc=dv)
            E("vector", lambda e: e.tensor_tensor(
                f_t[:, :], f_s2[:, :], f_xn[:, :], ALU.add),
              waits=[(dA, ldn_), (dv, dv.n)], inc=dv)
            E("vector", lambda e: e.tensor_tensor(
                f_m[:, :], f_t[:, :], f_g[:, :], ALU.mult),
              waits=[(dA, ldg_), (dv, dv.n)], inc=dv)
            tpe = transpose8(f_m, 8, waits=[(dv, dv.n)])
            E("vector", lambda e, t=t: e.tensor_tensor(
                f_o[:, :], PT[:, :], t_x[t % 2][:, :], ALU.add),
              waits=[(pe, tpe), (dS, ldx_), (dS, fo_free_dS)], inc=dv)
            fo_free_dS = dma("sync", dS, out_own[128 * t:128 * (t + 1), :],
                             f_o[:, :], waits=[(dv, dv.n)])
        E("sync", lambda e: e.nop(), waits=[(dS, dS.n)], inc=None)

        with nc.Block() as block:
            def runner(engname):
                def _(e):
                    for waits, fn, inc, k in sch.steps[engname]:
                        for (s, v) in waits:
                            e.wait_ge(s.h, v)
                        instr = fn(e)
                        if inc is not None:
                            assert instr is not None, engname
                            instr.then_inc(inc.h, k)
                return _
            block.tensor(runner("tensor"))
            block.vector(runner("vector"))
            block.scalar(runner("scalar"))
            block.gpsimd(runner("gpsimd"))
            block.sync(runner("sync"))

    return nc


_cache = {}


def _get_nc():
    if "nc" not in _cache:
        _cache["nc"] = build_program(DEBUG)
    return _cache["nc"]


def kernel(L0_attn_x, L0_attn_Wq, L0_attn_Wk, L0_attn_Wv, L0_attn_Wo,
           L0_ffn_w_gate, L0_ffn_w_up, L0_ffn_w_down):
    x = np.ascontiguousarray(np.asarray(L0_attn_x, dtype=np.float32))
    Wq = np.ascontiguousarray(np.asarray(L0_attn_Wq, dtype=np.float32))
    Wk = np.ascontiguousarray(np.asarray(L0_attn_Wk, dtype=np.float32))
    Wv = np.ascontiguousarray(np.asarray(L0_attn_Wv, dtype=np.float32))
    Wo = np.ascontiguousarray(np.asarray(L0_attn_Wo, dtype=np.float32))
    wg = np.ascontiguousarray(np.asarray(L0_ffn_w_gate, dtype=np.float32))
    wu = np.ascontiguousarray(np.asarray(L0_ffn_w_up, dtype=np.float32))
    wd = np.ascontiguousarray(np.asarray(L0_ffn_w_down, dtype=np.float32))

    wg_p = np.zeros((IPAD, D), dtype=np.float32)
    wg_p[:IFF] = wg
    wu_p = np.zeros((IPAD, D), dtype=np.float32)
    wu_p[:IFF] = wu

    ident = np.eye(128, dtype=np.float32).astype(ml_dtypes.bfloat16)
    r_idx = np.arange(128)
    maskt = np.where(r_idx[:, None] <= r_idx[None, :], 0.0, NEG).astype(
        np.float32)

    epoch = int(np.random.randint(1, 2**30))
    in_maps = []
    for core in range(8):
        b = core // 2
        r = core % 2
        selv = np.zeros((128, 2), dtype=np.float32)
        selv[:, r] = 1.0
        in_maps.append({
            "xb": x[b],
            "x_own": x[b, r * OWN:(r + 1) * OWN],
            "wq_in": Wq[:, 512 * r:512 * (r + 1)],
            "wk_in": Wk[:, 512 * r:512 * (r + 1)],
            "wv_in": Wv[:, 512 * r:512 * (r + 1)],
            "wo_in": Wo[512 * r:512 * (r + 1), :],
            "wg_sh": wg_p[1536 * r:1536 * (r + 1)],
            "wu_sh": wu_p[1536 * r:1536 * (r + 1)],
            "wd_sh": wd[512 * r:512 * (r + 1)],
            "ident": ident,
            "maskt": maskt,
            "selv": selv,
            "rolei": np.array([[r, 0]], dtype=np.int32),
            "epochi": np.array([[epoch, 0]], dtype=np.int32),
        })

    nc = _get_nc()
    res = run_bass_kernel_spmd(nc, in_maps, core_ids=list(range(8)))
    out = np.empty((4, S, D), dtype=np.float32)
    for core in range(8):
        b = core // 2
        r = core % 2
        out[b, r * OWN:(r + 1) * OWN] = res.results[core]["out_own"]
    return out


# version 10
# speedup vs baseline: 1.2648x; 1.1293x over previous
import numpy as np
import ml_dtypes
from contextlib import ExitStack

import concourse.bass as bass
import concourse.mybir as mybir
from concourse.bass_utils import run_bass_kernel_spmd

F32 = mybir.dt.float32
BF16 = mybir.dt.bfloat16
I16 = mybir.dt.int16
ALU = mybir.AluOpType
AF = mybir.ActivationFunctionType
AX = mybir.AxisListType

DEBUG = False

D = 1024
DCH = 8
S = 4096
OWN = 2048
NT = S // 128
NTO = OWN // 128
HPC = 8
NHP = HPC // 2
NQG = S // 512
DK = 64
IFF = 2816
IPAD = 3072
ICH = 22
NIB = IPAD // 512
NEG = -1.0e30

KT_O, QT_O, VH_O = 0, 4096, 8192
XR_O = 12352
PTND_O = 20544
PTDG_O = 24640
WQ_O, WK_O, WV_O, WO_O = 26688, 30784, 34880, 38976
BIG_N = 43072
HQT_O = 0
WG_O = 22528
WU_O = 30720
WD_O = 38912
GST_O = 41728
AR_O = 0


class Sem:
    def __init__(self, h):
        self.h = h
        self.n = 0


class Sched:

    ENGS = ("tensor", "vector", "scalar", "gpsimd", "sync")

    def __init__(self):
        self.steps = {e: [] for e in self.ENGS}

    def emit(self, eng, fn, waits=(), inc=None, inc_by=1):
        w = [(s, v) for (s, v) in waits if v > 0]
        self.steps[eng].append((w, fn, inc, inc_by))
        if inc is not None:
            inc.n += inc_by
        return inc.n if inc is not None else 0


def build_program(debug=False):
    nc = bass.Bass("TRN2", target_bir_lowering=False)

    xb = nc.dram_tensor("xb", [S, D], F32, kind="ExternalInput").ap()
    x_own = nc.dram_tensor("x_own", [OWN, D], F32, kind="ExternalInput").ap()
    wq_in = nc.dram_tensor("wq_in", [D, 512], F32, kind="ExternalInput").ap()
    wk_in = nc.dram_tensor("wk_in", [D, 512], F32, kind="ExternalInput").ap()
    wv_in = nc.dram_tensor("wv_in", [D, 512], F32, kind="ExternalInput").ap()
    wo_in = nc.dram_tensor("wo_in", [512, D], F32, kind="ExternalInput").ap()
    wg_sh = nc.dram_tensor("wg_sh", [12 * 128, D], F32,
                           kind="ExternalInput").ap()
    wu_sh = nc.dram_tensor("wu_sh", [12 * 128, D], F32,
                           kind="ExternalInput").ap()
    wd_sh = nc.dram_tensor("wd_sh", [4 * 128, IFF], F32,
                           kind="ExternalInput").ap()
    rolei = nc.dram_tensor("rolei", [1, 2], mybir.dt.int32,
                           kind="ExternalInput").ap()
    epochi = nc.dram_tensor("epochi", [1, 2], mybir.dt.int32,
                            kind="ExternalInput").ap()
    ident = nc.dram_tensor("ident", [128, 128], BF16, kind="ExternalInput").ap()
    maskt = nc.dram_tensor("maskt", [128, 128], F32, kind="ExternalInput").ap()
    selv = nc.dram_tensor("selv", [128, 2], F32, kind="ExternalInput").ap()
    out_own = nc.dram_tensor("out_own", [OWN, D], F32, kind="ExternalOutput").ap()

    dkind = "ExternalOutput" if debug else "Internal"
    xnT_d = nc.dram_tensor("xnT_d", [DCH, 128, S], BF16, kind=dkind).ap()
    xnTo_d = nc.dram_tensor("xnTo_d", [DCH, 128, OWN], BF16, kind=dkind).ap()
    attnT_d = nc.dram_tensor("attnT_d", [NHP, 128, S], BF16, kind=dkind).ap()
    gT_d = nc.dram_tensor("gT_d", [DCH, 128, OWN], BF16, kind=dkind).ap()
    shwg = nc.dram_tensor("shwg", [24 * D, 128], BF16,
                          addr_space="Shared").ap()
    shwu = nc.dram_tensor("shwu", [24 * D, 128], BF16,
                          addr_space="Shared").ap()
    shwd = nc.dram_tensor("shwd", [8 * IFF, 128], BF16,
                          addr_space="Shared").ap()
    shs = nc.dram_tensor("shs", [2, DCH * 128, S], BF16,
                         addr_space="Shared").ap()
    shflag = nc.dram_tensor("shflag", [2, 2, 16], mybir.dt.int32,
                            addr_space="Shared").ap()
    if debug:
        xnq_dbg = nc.dram_tensor("xnq_dbg", [DCH, 128, OWN], BF16,
                                 kind="ExternalOutput").ap()
        sp_dbg = nc.dram_tensor("sp_dbg", [DCH * 128, S], BF16,
                                kind="ExternalOutput").ap()
        kqv_dbg = nc.dram_tensor("kqv_dbg", [3, 128, 4160], BF16,
                                 kind="ExternalOutput").ap()
        p_dbg = nc.dram_tensor("p_dbg", [6, 128, 2048], BF16,
                               kind="ExternalOutput").ap()
        av_dbg = nc.dram_tensor("av_dbg", [4, 128, 512], F32,
                                kind="ExternalOutput").ap()

    sch = Sched()
    es = ExitStack()

    def sb(name, shape, dtype):
        return es.enter_context(nc.sbuf_tensor(name, shape, dtype))

    def ps(name, shape, dtype):
        return es.enter_context(nc.psum_tensor(name, shape, dtype))

    def sem(name):
        return Sem(es.enter_context(nc.semaphore(name)))

    with es:
        big = sb("big", [128, BIG_N], BF16)
        xnqT = sb("xnqT", [128, DCH * OWN], BF16)
        t_x = [sb(f"t_x{i}", [128, D], F32) for i in range(2)]
        t_xn = sb("t_xn", [128, D], F32)
        t_xnb = sb("t_xnb", [128, D], BF16)
        bnb = sb("bnb", [128, 12], F32)
        mv = sb("mv", [128, 2], F32)
        ln_s = sb("ln_s", [128, 4], F32)
        lnsv = sb("lnsv", [128, 2 * NTO], F32)
        q_xn = sb("q_xn", [128, 1024], F32)
        q_a = sb("q_a", [128, 1024], F32)
        q_i1 = sb("q_i1", [128, 1024], I16)
        q_i2 = sb("q_i2", [128, 1024], I16)
        q_i3 = sb("q_i3", [128, 1024], I16)
        q_t1 = sb("q_t1", [128, 1024], BF16)
        q_t2 = sb("q_t2", [128, 1024], BF16)
        q_sg = sb("q_sg", [128, 1024], BF16)
        q_out = sb("q_out", [128, 1024], BF16)
        q_bm = sb("q_bm", [128, 64], F32)
        q_S = sb("q_S", [128, 64], F32)
        q_R = sb("q_R", [128, 64], F32)
        q_Sb = sb("q_Sb", [128, 64], BF16)
        st = [sb(f"st{i}", [128, 1024], BF16) for i in range(2)]
        t_mask = sb("t_mask", [128, 128], F32)
        t_id = sb("t_id", [128, 128], BF16)
        t_sel = sb("t_sel", [128, 2], F32)
        rden = [sb(f"rden{i}", [128, 512], F32) for i in range(2)]
        rb64 = [sb(f"rb64{i}", [128, 512], F32) for i in range(2)]
        outst = [sb(f"outst{i}", [128, 512], BF16) for i in range(2)]
        t_ones = sb("t_ones", [128, 64], F32)
        t_ri = sb("t_ri", [1, 2], mybir.dt.int32)
        t_ep = sb("t_ep", [1, 16], mybir.dt.int32)
        t_zz = sb("t_zz", [1, 16], mybir.dt.int32)
        regs = {}
        t_wf = None
        hsc = sb("hsc", [128, 512], F32)
        hbuf = [sb(f"hbuf{i}", [128, 512], F32) for i in range(2)]
        f_t = sb("f_t", [128, 1024], F32)
        f_s = sb("f_s", [128, 1024], F32)
        f_s2 = sb("f_s2", [128, 1024], F32)
        f_m = sb("f_m", [128, 1024], BF16)
        f_o = sb("f_o", [128, 1024], F32)

        P0 = ps("P0", [128, 2048], F32)
        P1 = ps("P1", [128, 512], F32)
        P2 = ps("P2", [128, 512], F32)
        P3 = ps("P3", [128, 512], F32)
        PT = ps("PT", [128, 1024], BF16)

        dS = sem("dS")
        dA = sem("dA")
        dP = sem("dP")
        pe = sem("pe")
        dv = sem("dv")
        ac = sem("ac")
        gp = sem("gp")

        E = sch.emit

        def dma(q, qs, out, in_, waits=()):
            return E(q, lambda e, o=out, i=in_: e.dma_start(out=o, in_=i),
                     waits=waits, inc=qs, inc_by=16)

        def quant(src_f32, n, dst_bf16, after=(), dst_waits=()):
            for _ in quant_gen(src_f32, n, dst_bf16, after, dst_waits):
                pass
            return dv.n

        def quant_gen(src_f32, n, dst_bf16, after=(), dst_waits=()):
            assert n % 32 == 0
            h = n // 2
            g = h // 16

            co = [0, h]
            go = [0, g]

            def stp(fn, waits=()):
                E("vector", fn, waits=waits, inc=dv)

            w0 = list(after)
            for i in range(2):
                c, gg = co[i], go[i]
                x3 = src_f32[:, c:c + h].rearrange("p (g w) -> p g w", w=16)
                stp(lambda e, x3=x3, gg=gg: e.tensor_reduce(
                    q_bm[:, gg:gg + g], x3, AX.X, ALU.max,
                    apply_absolute_value=True), waits=w0)
                w0 = ()
            yield
            for i in range(2):
                gg = go[i]
                stp(lambda e, gg=gg: e.tensor_scalar(
                    q_S[:, gg:gg + g], q_bm[:, gg:gg + g], 1.0 / 6.0, 1e-12,
                    ALU.mult, ALU.max))
            yield
            for i in range(2):
                gg = go[i]
                stp(lambda e, gg=gg: e.reciprocal(
                    q_R[:, gg:gg + g], q_S[:, gg:gg + g]))
            yield
            for i in range(2):
                gg = go[i]
                stp(lambda e, gg=gg: e.tensor_copy(
                    q_Sb[:, gg:gg + g], q_S[:, gg:gg + g]))
            yield
            for i in range(2):
                c, gg = co[i], go[i]
                x3 = src_f32[:, c:c + h].rearrange("p (g w) -> p g w", w=16)
                rb = q_R[:, gg:gg + g].unsqueeze(2).broadcast_to((128, g, 16))
                o3 = q_xn[:, c:c + h].rearrange("p (g w) -> p g w", w=16)
                stp(lambda e, x3=x3, rb=rb, o3=o3: e.tensor_tensor(
                    o3, x3, rb, ALU.mult))
            yield
            for i in range(2):
                c = co[i]
                stp(lambda e, c=c: e.tensor_scalar(
                    q_sg[:, c:c + h], q_xn[:, c:c + h], 0.0, 2.0,
                    ALU.is_gt, ALU.mult))
            yield
            for i in range(2):
                c = co[i]
                stp(lambda e, c=c: e.scalar_tensor_tensor(
                    q_a[:, c:c + h], q_sg[:, c:c + h], -1.0,
                    q_xn[:, c:c + h], ALU.add, ALU.mult))
            yield
            for i in range(2):
                c = co[i]
                stp(lambda e, c=c: e.tensor_scalar(
                    q_xn[:, c:c + h], q_a[:, c:c + h], 2.0, 4.0,
                    ALU.mult, ALU.min))
            yield
            for i in range(2):
                c = co[i]
                stp(lambda e, c=c: e.tensor_copy(
                    q_i1[:, c:c + h], q_xn[:, c:c + h]))
            yield
            for i in range(2):
                c = co[i]
                stp(lambda e, c=c: e.tensor_scalar(
                    q_xn[:, c:c + h], q_a[:, c:c + h], 2.0, 4.0,
                    ALU.max, ALU.min))
            yield
            for i in range(2):
                c = co[i]
                stp(lambda e, c=c: e.tensor_copy(
                    q_i2[:, c:c + h], q_xn[:, c:c + h]))
            yield
            for i in range(2):
                c = co[i]
                stp(lambda e, c=c: e.tensor_scalar(
                    q_xn[:, c:c + h], q_a[:, c:c + h], 0.5, 2.0,
                    ALU.mult, ALU.max))
            yield
            for i in range(2):
                c = co[i]
                stp(lambda e, c=c: e.tensor_copy(
                    q_i3[:, c:c + h], q_xn[:, c:c + h]))
            yield
            for i in range(2):
                c = co[i]
                stp(lambda e, c=c: e.scalar_tensor_tensor(
                    q_t1[:, c:c + h], q_i1[:, c:c + h], 0.5, q_i2[:, c:c + h],
                    ALU.mult, ALU.add))
            yield
            for i in range(2):
                c = co[i]
                stp(lambda e, c=c: e.scalar_tensor_tensor(
                    q_t2[:, c:c + h], q_i3[:, c:c + h], 2.0, q_t1[:, c:c + h],
                    ALU.mult, ALU.add))
            yield
            for i in range(2):
                c = co[i]
                stp(lambda e, c=c: e.tensor_scalar(
                    q_t2[:, c:c + h], q_t2[:, c:c + h], -6.0, None, ALU.add))
            yield
            for i in range(2):
                c, gg = co[i], go[i]
                sbb = q_Sb[:, gg:gg + g].unsqueeze(2).broadcast_to((128, g, 16))
                t23 = q_t2[:, c:c + h].rearrange("p (g w) -> p g w", w=16)
                o3 = q_t1[:, c:c + h].rearrange("p (g w) -> p g w", w=16)
                stp(lambda e, t23=t23, sbb=sbb, o3=o3: e.tensor_tensor(
                    o3, t23, sbb, ALU.mult))
            yield
            for i in range(2):
                c = co[i]
                stp(lambda e, c=c: e.scalar_tensor_tensor(
                    dst_bf16[:, c:c + h], q_sg[:, c:c + h], -1.0,
                    q_t1[:, c:c + h], ALU.add, ALU.mult),
                    waits=(list(dst_waits) if i == 0 else ()))
            yield

        def transpose8(src_bf16, nblk, pt_off=0, waits=()):
            for m in range(nblk):
                w = waits if m == 0 else ()
                E("tensor", lambda e, m=m: e.transpose(
                    PT[:, pt_off + 128 * m:pt_off + 128 * (m + 1)],
                    src_bf16[:, 128 * m:128 * (m + 1)], t_id[:, :]),
                    waits=w, inc=(pe if m == nblk - 1 else None))
            return pe.n

        dma("sync", dS, t_id[:, :], ident)
        dma("sync", dS, t_mask[:, :], maskt)
        dma("sync", dS, t_sel[:, :], selv)
        dma("sync", dS, t_ri[0:1, :], rolei)
        dma("sync", dS, t_ep[0:1, 0:2], epochi)
        a_small = dS.n

        def _init_regs_sync(e):
            regs["sy_role"] = e.alloc_register("sy_role")
            e.wait_ge(dS.h, a_small)
            return e.load(regs["sy_role"], t_ri[0:1, 0:1])
        E("sync", _init_regs_sync, waits=(), inc=None)

        def _init_regs_gp(e):
            e.memset(t_zz[0:1, :], 0)
            regs["gp_role"] = e.alloc_register("gp_role")
            regs["gp_f"] = e.alloc_register("gp_f")
            regs["gp_c"] = e.alloc_register("gp_c")
            regs["gp_k"] = e.alloc_register("gp_k")
            e.wait_ge(dS.h, a_small)
            e.load(regs["gp_role"], t_ri[0:1, 0:1])
            e.load(regs["gp_k"], t_ep[0:1, 0:1])
            with e.If_eq(regs["gp_role"], 0):
                e.dma_start(out=shflag[0, 0, 0:2],
                            in_=t_zz[0:1, 0:2]).then_inc(dP.h, 16)
                e.dma_start(out=shflag[0, 1, 0:2],
                            in_=t_zz[0:1, 0:2]).then_inc(dP.h, 16)
            with e.Else():
                e.dma_start(out=shflag[1, 0, 0:2],
                            in_=t_zz[0:1, 0:2]).then_inc(dP.h, 16)
                e.dma_start(out=shflag[1, 1, 0:2],
                            in_=t_zz[0:1, 0:2]).then_inc(dP.h, 16)
            return e.wait_ge(dP.h, 32)
        E("gpsimd", _init_regs_gp, waits=(), inc=None)
        dP.n += 32

        def _barrier_body(e, bar, after_dS, dp_wait):
            e.wait_ge(dS.h, after_dS)
            r = regs["gp_role"]
            f, c, k = regs["gp_f"], regs["gp_c"], regs["gp_k"]
            for me in (0, 1):
                other = 1 - me
                with (e.If_eq(r, 0) if me == 0 else e.Else()):
                    e.dma_start(out=shflag[me, bar, 0:2],
                                in_=t_ep[0:1, 0:2]).then_inc(dP.h, 16)
                    e.wait_ge(dP.h, dp_wait)
                    e.load(f, shflag[other, bar:bar + 1, 0:1])
                    e.reg_alu(c, f, k, ALU.not_equal)
                    with e.While(c):
                        e.load(f, shflag[other, bar:bar + 1, 0:1])
                        e.reg_alu(c, f, k, ALU.not_equal)
            return e.nop()

        def pair_barrier(bar, after_dS):
            dp_wait = dP.n + 16
            E("gpsimd", lambda e, bar=bar, a=after_dS, w=dp_wait:
              _barrier_body(e, bar, a, w), waits=(), inc=gp)
            dP.n += 16
            return gp.n

        for (w_in, w_off, scale, ncol) in [
                (wq_in, WQ_O, 0.125, 512), (wk_in, WK_O, 1.0, 512),
                (wv_in, WV_O, 1.0, 512), (wo_in, WO_O, 1.0, 1024)]:
            nch = w_in.shape[0] // 128
            for pc in range(nch):
                wv_ = dma("sync", dS, t_x[0][:, :ncol],
                          w_in[128 * pc:128 * (pc + 1), :],
                          waits=[(ac, ac.n)])
                E("scalar", lambda e, w_off=w_off, pc=pc, scale=scale,
                  ncol=ncol: e.activation(
                      big[:, w_off + ncol * pc:w_off + ncol * (pc + 1)],
                      t_x[0][:, :ncol], AF.Copy, scale=scale),
                  waits=[(dS, wv_)], inc=ac)
        aw_done_ac = ac.n

        st_dma_free = [0, 0]
        qout_free_pe = 0

        def prep_piece_gen(src_ap, cn, dst0_ap, dst1_ap, par):
            nonlocal qout_free_pe
            land = t_x[par % 2][:, :cn]
            ldw = dma("gpsimd", dP, land, src_ap,
                      waits=[(dv, dv.n), (ac, ac.n)])
            yield
            yield from quant_gen(land, cn, q_out, after=[(dP, ldw)],
                                 dst_waits=[(pe, qout_free_pe)])
            qdv = dv.n
            nb8 = cn // 128
            tpe = transpose8(q_out, nb8,
                             waits=[(dv, qdv), (ac, aw_done_ac)])
            qout_free_pe = tpe
            sti = st[par % 2]
            E("vector", lambda e, sti=sti, nb8=nb8: e.tensor_copy(
                sti[:, :128 * nb8], PT[:, :128 * nb8]),
              waits=[(pe, tpe), (dS, st_dma_free[par % 2])], inc=dv)
            src_st = sti[:, :128 * nb8].rearrange("p (m w) -> p m w", w=128)
            d0 = dst0_ap.rearrange("(m p) w -> p m w", p=128)
            d1 = dst1_ap.rearrange("(m p) w -> p m w", p=128)
            dvw = dv.n

            def _branched_store(e, d0=d0, d1=d1, src_st=src_st, dvw=dvw):
                e.wait_ge(dv.h, dvw)
                with e.If_eq(regs["sy_role"], 0):
                    e.dma_start(out=d0, in_=src_st).then_inc(dS.h, 16)
                with e.Else():
                    e.dma_start(out=d1, in_=src_st).then_inc(dS.h, 16)
                return None
            E("sync", _branched_store, waits=(), inc=None)
            dS.n += 16
            st_dma_free[par % 2] = dS.n
            yield

        def xnq_tile_gen(t, par):
            nonlocal qout_free_pe
            land = t_x[par % 2][:, :]
            ldw = dma("gpsimd", dP, land,
                      x_own[128 * t:128 * (t + 1), :],
                      waits=[(dv, dv.n), (ac, ac.n)])
            E("vector", lambda e, t=t: e.tensor_scalar(
                t_xn[:, :], land[:, :], lnsv[:, 2 * t:2 * t + 1],
                lnsv[:, 2 * t + 1:2 * t + 2], ALU.mult, ALU.add),
              waits=[(dP, ldw), (dv, dv.n)], inc=dv)
            yield
            yield from quant_gen(t_xn, D, q_out, after=[(dv, dv.n)],
                                 dst_waits=[(pe, qout_free_pe)])
            tpe = transpose8(q_out, 8, waits=[(dv, dv.n)])
            qout_free_pe = tpe
            for c in range(DCH):
                E("vector", lambda e, c=c, t=t: e.tensor_copy(
                    xnqT[:, OWN * c + 128 * t:OWN * c + 128 * (t + 1)],
                    PT[:, 128 * c:128 * (c + 1)]),
                  waits=([(pe, tpe)] if c == 0 else ()), inc=dv)
            if debug:
                dstq = xnq_dbg[:, :, 128 * t:128 * (t + 1)].rearrange(
                    "c p w -> p c w")
                dma("sync", dS, dstq,
                    q_out[:, :].rearrange("p (c w) -> p c w", w=128),
                    waits=[(dv, dv.n)])
            yield

        deferred = []

        def build_deferred():
            par = 0
            for (src, sh_dst) in [(wg_sh, shwg), (wu_sh, shwu)]:
                for lb in range(12):
                    deferred.append(prep_piece_gen(
                        src[128 * lb:128 * (lb + 1), :], D,
                        sh_dst[D * lb:D * (lb + 1), :],
                        sh_dst[D * (12 + lb):D * (13 + lb), :], par))
                    par += 1
            for t in range(NTO):
                deferred.append(xnq_tile_gen(t, par))
                par += 1
            for lb in range(4):
                for pc in range(3):
                    c0 = 1024 * pc
                    cn = min(1024, IFF - c0)
                    deferred.append(prep_piece_gen(
                        wd_sh[128 * lb:128 * (lb + 1), c0:c0 + cn], cn,
                        shwd[IFF * lb + c0:IFF * lb + c0 + cn, :],
                        shwd[IFF * (4 + lb) + c0:IFF * (4 + lb) + c0 + cn, :],
                        par))
                    par += 1

        def pull(n):
            k = 0
            while deferred and k < n:
                try:
                    next(deferred[0])
                    k += 1
                except StopIteration:
                    deferred.pop(0)

        def pull_all():
            while deferred:
                pull(1000)

        wg4 = shwg.rearrange("(b c p) i -> p c b i", c=8, p=128)
        wu4 = shwu.rearrange("(b c p) i -> p c b i", c=8, p=128)
        wd4 = shwd.rearrange("(h ic p) w -> h p ic w", h=8, p=128)

        def ln_tile(xsrc, ti, store_dst, save_ti=None):
            land = t_x[ti % 2][:, :]
            ldw = dma("sync", dS, land, xsrc[128 * ti:128 * (ti + 1), :],
                      waits=[(dv, dv.n), (ac, ac.n)])
            E("vector", lambda e: e.bn_stats(bnb[:, 0:6], land[:, 0:512]),
              waits=[(dS, ldw)], inc=dv)
            E("vector", lambda e: e.bn_stats(bnb[:, 6:12], land[:, 512:1024]),
              inc=dv)
            E("vector", lambda e: e.bn_aggr(mv[:, :], bnb[:, :]),
              waits=[(dv, dv.n)], inc=dv)
            E("vector", lambda e: e.tensor_scalar(
                ln_s[:, 0:1], mv[:, 1:2], 1e-5, None, ALU.add),
              waits=[(dv, dv.n)], inc=dv)
            E("scalar", lambda e: e.activation(
                ln_s[:, 1:2], ln_s[:, 0:1], AF.Sqrt),
              waits=[(dv, dv.n)], inc=ac)
            E("vector", lambda e: e.reciprocal(ln_s[:, 2:3], ln_s[:, 1:2]),
              waits=[(ac, ac.n)], inc=dv)
            E("vector", lambda e: e.scalar_tensor_tensor(
                ln_s[:, 3:4], mv[:, 0:1], -1.0, ln_s[:, 2:3],
                ALU.mult, ALU.mult), waits=[(dv, dv.n)], inc=dv)
            if save_ti is not None:
                E("vector", lambda e, t=save_ti: e.tensor_copy(
                    lnsv[:, 2 * t:2 * t + 2], ln_s[:, 2:4]),
                  waits=[(dv, dv.n)], inc=dv)
            E("scalar", lambda e: e.activation(
                t_xnb[:, :], land[:, :], AF.Identity,
                scale=ln_s[:, 2:3], bias=ln_s[:, 3:4]),
              waits=[(dv, dv.n)], inc=ac)
            tpe = transpose8(t_xnb, 8, waits=[(ac, ac.n), (dv, dv.n)])
            sti = st[ti % 2]
            E("vector", lambda e, sti=sti: e.tensor_copy(sti[:, :], PT[:, :]),
              waits=[(pe, tpe), (dS, st_dma_free[ti % 2])], inc=dv)
            dst = store_dst[:, :, 128 * ti:128 * (ti + 1)].rearrange(
                "c p w -> p c w")
            st_dma_free[ti % 2] = dma(
                "sync", dS, dst,
                sti[:, :].rearrange("p (c w) -> p c w", w=128),
                waits=[(dv, dv.n)])
            return st_dma_free[ti % 2]

        ln_dma_at = []
        for ti in range(NT):
            ln_dma_at.append(ln_tile(xb, ti, xnT_d))
        for ti in range(NTO):
            ln_tile(x_own, ti, xnTo_d, save_ti=ti)
        ln_done_dS = dS.n
        ln_done_dv = dv.n
        build_deferred()

        kT = big[:, KT_O:KT_O + 4096]
        qT = big[:, QT_O:QT_O + 4096]
        vh = big[:, VH_O:VH_O + 4160]
        xr = [big[:, XR_O + 4096 * i:XR_O + 4096 * (i + 1)] for i in range(2)]
        pnd = [big[:, PTND_O + 2048 * i:PTND_O + 2048 * (i + 1)]
               for i in range(2)]
        pdg = [big[:, PTDG_O + 512 * j:PTDG_O + 512 * (j + 1)]
               for j in range(4)]

        vh3 = vh.rearrange("p (c w) -> p c w", w=65)
        E("vector", lambda e: e.memset(vh3[:, :, 64:65], 1.0),
          waits=[], inc=dv)
        E("vector", lambda e: e.memset(t_ones[0:1, :], 1.0), inc=dv)
        for j in range(4):
            E("vector", lambda e, j=j: e.memset(pdg[j][:, :], 0.0), inc=dv)
        attn_init_dv = dv.n

        xr_free_pe = [0, 0]
        pnd_free_pe = [0, 0]
        pdg_free_pe = [0, 0, 0, 0]
        p1b_free_dv = [0, 0]
        p0_free_ac = 0
        p2div_free_dv = 0
        outst_free_dS = [0, 0]
        rr_free_dv = [0, 0]
        dbg_gate_dv = 0
        ogi = 0

        for hp in range(NHP):
            kq_ready_ac = {}
            v_ready_dv = {}
            for tg in range(DCH):
                xbuf = xr[tg % 2]
                lw = [(pe, xr_free_pe[tg % 2]), (dS, ln_dma_at[4 * tg + 3])]
                src = xnT_d[:, :, 512 * tg:512 * (tg + 1)].rearrange(
                    "c p w -> p c w")
                ldx = dma("sync", dS,
                          xbuf.rearrange("p (c w) -> p c w", w=512),
                          src, waits=lw)
                for c in range(DCH):
                    E("tensor", lambda e, c=c, xbuf=xbuf, hp=hp: e.matmul(
                        P2[:, :],
                        big[:, WK_O + 512 * c + 128 * hp:
                             WK_O + 512 * c + 128 * (hp + 1)],
                        xbuf[:, 512 * c:512 * (c + 1)],
                        start=(c == 0), stop=(c == DCH - 1)),
                      waits=([(dS, ldx), (ac, ac.n), (dv, dv.n)]
                             if c == 0 else ()),
                      inc=(pe if c == DCH - 1 else None))
                E("scalar", lambda e, tg=tg: e.activation(
                    kT[:, 512 * tg:512 * (tg + 1)], P2[:, :], AF.Copy),
                  waits=[(pe, pe.n)], inc=ac)
                for c in range(DCH):
                    E("tensor", lambda e, c=c, xbuf=xbuf, hp=hp: e.matmul(
                        P2[:, :],
                        big[:, WQ_O + 512 * c + 128 * hp:
                             WQ_O + 512 * c + 128 * (hp + 1)],
                        xbuf[:, 512 * c:512 * (c + 1)],
                        start=(c == 0), stop=(c == DCH - 1)),
                      waits=([(ac, ac.n)] if c == 0 else ()),
                      inc=(pe if c == DCH - 1 else None))
                E("scalar", lambda e, tg=tg: e.activation(
                    qT[:, 512 * tg:512 * (tg + 1)], P2[:, :], AF.Copy),
                  waits=[(pe, pe.n)], inc=ac)
                kq_ready_ac[tg] = ac.n
                for tt in range(4):
                    for c in range(DCH):
                        E("tensor", lambda e, c=c, tt=tt, xbuf=xbuf, hp=hp:
                          e.matmul(
                              P2[:, 128 * tt:128 * (tt + 1)],
                              xbuf[:, 512 * c + 128 * tt:
                                   512 * c + 128 * (tt + 1)],
                              big[:, WV_O + 512 * c + 128 * hp:
                                   WV_O + 512 * c + 128 * (hp + 1)],
                              start=(c == 0), stop=(c == DCH - 1)),
                          waits=([(ac, kq_ready_ac[tg]),
                                  (dv, dv.n)] if (tt == 0 and c == 0) else ()),
                          inc=(pe if c == DCH - 1 else None))
                vpe = pe.n
                for tt in range(4):
                    c32 = 4 * tg + tt
                    for h2 in range(2):
                        E("vector", lambda e, tt=tt, h2=h2, c32=c32:
                          e.tensor_copy(
                              vh[:, 2080 * h2 + 65 * c32:
                                 2080 * h2 + 65 * c32 + 64],
                              P2[:, 128 * tt + 64 * h2:
                                 128 * tt + 64 * h2 + 64]),
                          waits=([(pe, vpe),
                                  (dv, attn_init_dv)] if tt == 0 and h2 == 0
                                 else ()), inc=dv)
                v_ready_dv[tg] = dv.n
                xr_free_pe[tg % 2] = pe.n
            proj_ac = ac.n
            proj_dv = dv.n
            if debug and hp == 0:
                dma("sync", dS, kqv_dbg[0, :, 0:4096], kT,
                    waits=[(ac, proj_ac), (dv, proj_dv)])
                dma("sync", dS, kqv_dbg[1, :, 0:4096], qT, waits=())
                dma("sync", dS, kqv_dbg[2, :, :], vh, waits=())

            for h2 in range(2):
                r0 = 64 * h2
                for g in range(NQG):
                    nb = g
                    PB = [P1, P3][ogi % 2]
                    scores_pe = {}
                    exp_ac = {}
                    for b in range(nb + 1):
                        is_diag = (b == nb)
                        if not is_diag:
                            for j4 in range(4):
                                kc = 4 * b + j4
                                w = []
                                if j4 == 0:
                                    w = [(ac, exp_ac.get(b - 1, p0_free_ac)),
                                         (ac, proj_ac), (dv, proj_dv)]
                                E("tensor", lambda e, kc=kc, j4=j4, r0=r0, g=g:
                                  e.matmul(
                                      P0[:, 512 * j4:512 * (j4 + 1)],
                                      kT[r0:r0 + 64,
                                         128 * kc:128 * (kc + 1)],
                                      qT[r0:r0 + 64, 512 * g:512 * (g + 1)],
                                      start=True, stop=True),
                                  waits=w, inc=(pe if j4 == 3 else None))
                            scores_pe[b] = pe.n
                        else:
                            for j in range(4):
                                kc = 4 * g + j
                                nw = 512 - 128 * j
                                w = [(ac, exp_ac.get(b - 1, p0_free_ac))] \
                                    if j == 0 else []
                                if j == 0:
                                    w += [(ac, proj_ac), (dv, proj_dv)]
                                E("tensor", lambda e, kc=kc, j=j, nw=nw,
                                  r0=r0, g=g: e.matmul(
                                      P0[:, 512 * j:512 * j + nw],
                                      kT[r0:r0 + 64,
                                         128 * kc:128 * (kc + 1)],
                                      qT[r0:r0 + 64,
                                         512 * g + 128 * j:512 * (g + 1)],
                                      start=True, stop=True),
                                  waits=w, inc=pe)
                                scores_pe[(b, j)] = pe.n
                        if b >= 1:
                            pb = pnd[(b - 1) % 2]
                            for j4 in range(4):
                                kc = 4 * (b - 1) + j4
                                w = []
                                if j4 == 0:
                                    w = [(ac, exp_ac[b - 1]),
                                         (dv, dbg_gate_dv)]
                                    if b == 1:
                                        w.append((dv, p1b_free_dv[ogi % 2]))
                                E("tensor", lambda e, kc=kc, j4=j4, h2=h2,
                                  b=b, pb=pb, PB=PB: e.matmul(
                                      PB[0:65, :],
                                      vh[:, 2080 * h2 + 65 * kc:
                                         2080 * h2 + 65 * kc + 65],
                                      pb[:, 512 * j4:512 * (j4 + 1)],
                                      start=(b == 1 and j4 == 0), stop=False),
                                  waits=w, inc=(pe if j4 == 3 else None))
                            pnd_free_pe[(b - 1) % 2] = pe.n
                            if debug and hp == 0 and h2 == 1 and g == 0:
                                E("vector", lambda e: e.tensor_copy(
                                    hbuf[0][:, :], P1[:, :]),
                                  waits=[(pe, pe.n)], inc=dv)
                                dma("sync", dS, av_dbg[b], hbuf[0][:, :],
                                    waits=[(dv, dv.n)])
                                dbg_gate_dv = dv.n
                        if not is_diag:
                            E("scalar", lambda e, b=b: e.activation(
                                pnd[b % 2][:, :], P0[:, :], AF.Exp),
                              waits=[(pe, scores_pe[b]),
                                     (pe, pnd_free_pe[b % 2])], inc=ac)
                            exp_ac[b] = ac.n
                            pull(2)
                        else:
                            for j in range(4):
                                E("vector", lambda e, j=j: e.tensor_tensor(
                                    P0[:, 512 * j:512 * j + 128],
                                    P0[:, 512 * j:512 * j + 128],
                                    t_mask[:, :], ALU.add),
                                  waits=[(pe, scores_pe[(b, j)])], inc=dv)
                                mdv = dv.n
                                nw = 512 - 128 * j
                                E("scalar", lambda e, j=j, nw=nw:
                                  e.activation(
                                      pdg[j][:, 128 * j:128 * j + nw],
                                      P0[:, 512 * j:512 * j + nw], AF.Exp),
                                  waits=[(dv, mdv),
                                         (pe, pdg_free_pe[j])], inc=ac)
                                exp_ac[(b, j)] = ac.n
                    p0_free_ac = ac.n
                    if debug and hp == 0 and h2 == 1 and g == 0:
                        dma("sync", dS, p_dbg[0], pnd[0],
                            waits=[(ac, ac.n)])
                        dma("sync", dS, p_dbg[1], pnd[1], waits=())
                        for j in range(4):
                            dma("sync", dS, p_dbg[2 + j][:, 0:512], pdg[j],
                                waits=())
                    for j in range(4):
                        kc = 4 * g + j
                        w = [(ac, exp_ac[(nb, j)]), (dv, dbg_gate_dv)]
                        if nb == 0 and j == 0:
                            w.append((dv, p1b_free_dv[ogi % 2]))
                        E("tensor", lambda e, kc=kc, j=j, h2=h2, g=g, nb=nb,
                          PB=PB: e.matmul(
                              PB[0:65, :],
                              vh[:, 2080 * h2 + 65 * kc:
                                 2080 * h2 + 65 * kc + 65],
                              pdg[j][:, :],
                              start=(nb == 0 and j == 0), stop=(j == 3)),
                          waits=w, inc=(pe if j == 3 else None))
                        pdg_free_pe[j] = pe.n
                    av_pe = pe.n
                    if debug and hp == 0 and h2 == 1 and g == 0:
                        E("vector", lambda e, PB=PB: e.tensor_copy(
                            hbuf[0][:, :], PB[:, :]),
                          waits=[(pe, av_pe)], inc=dv)
                        dma("sync", dS, av_dbg[3], hbuf[0][:, :],
                            waits=[(dv, dv.n)])
                    oi = ogi % 2
                    E("vector", lambda e, oi=oi, PB=PB: e.reciprocal(
                        rden[oi][0:1, :], PB[64:65, :]),
                      waits=[(pe, av_pe), (pe, rr_free_dv[oi])], inc=dv)
                    rdv = dv.n
                    E("tensor", lambda e, oi=oi: e.matmul(
                        P2[0:64, :], t_ones[0:1, 0:64], rden[oi][0:1, :],
                        start=True, stop=True),
                      waits=[(dv, max(rdv, p2div_free_dv, proj_dv))], inc=pe)
                    rr_free_dv[oi] = pe.n
                    E("vector", lambda e, oi=oi: e.tensor_copy(
                        rb64[oi][0:64, :], P2[0:64, :]),
                      waits=[(pe, pe.n)], inc=dv)
                    p2div_free_dv = dv.n
                    E("vector", lambda e, oi=oi, PB=PB: e.tensor_tensor(
                        outst[oi][0:64, :], PB[0:64, :], rb64[oi][0:64, :],
                        ALU.mult),
                      waits=[(dS, outst_free_dS[oi]), (dv, dv.n)], inc=dv)
                    p1b_free_dv[ogi % 2] = dv.n
                    od = dma("sync", dS,
                             attnT_d[hp, r0:r0 + 64, 512 * g:512 * (g + 1)],
                             outst[oi][0:64, :], waits=[(dv, dv.n)])
                    outst_free_dS[oi] = od
                    ogi += 1
                    pull(10)

        pull_all()
        wprep_dS = dS.n
        wbar_gp = pair_barrier(0, wprep_dS)
        attn_done_dS = dS.n

        ar = [big[:, AR_O + 2048 * i:AR_O + 2048 * (i + 1)] for i in range(2)]
        ar_free_pe = [pe.n, pe.n]
        for tg in range(DCH):
            abuf = ar[tg % 2]
            src = attnT_d[:, :, 512 * tg:512 * (tg + 1)].rearrange(
                "c p w -> p c w")
            lda = dma("sync", dS, abuf.rearrange("p (c w) -> p c w", w=512),
                      src, waits=[(pe, ar_free_pe[tg % 2]),
                                  (dS, attn_done_dS)])
            for hc in range(DCH):
                for a in range(4):
                    E("tensor", lambda e, a=a, hc=hc, abuf=abuf: e.matmul(
                        P2[:, :],
                        big[:, WO_O + 1024 * a + 128 * hc:
                             WO_O + 1024 * a + 128 * (hc + 1)],
                        abuf[:, 512 * a:512 * (a + 1)],
                        start=(a == 0), stop=(a == 3)),
                      waits=([(dS, lda), (dv, dv.n)] if a == 0 else ()),
                      inc=(pe if a == 3 else None))
                sti = st[hc % 2]
                E("vector", lambda e, sti=sti: e.tensor_copy(
                    sti[:, 0:512], P2[:, :]),
                  waits=[(pe, pe.n), (dS, st_dma_free[hc % 2])], inc=dv)
                d0 = shs[0, 128 * hc:128 * (hc + 1), 512 * tg:512 * (tg + 1)]
                d1 = shs[1, 128 * hc:128 * (hc + 1), 512 * tg:512 * (tg + 1)]
                dvw = dv.n

                def _st_s(e, d0=d0, d1=d1, sti=sti, dvw=dvw):
                    e.wait_ge(dv.h, dvw)
                    with e.If_eq(regs["sy_role"], 0):
                        e.dma_start(out=d0, in_=sti[:, 0:512]).then_inc(
                            dS.h, 16)
                    with e.Else():
                        e.dma_start(out=d1, in_=sti[:, 0:512]).then_inc(
                            dS.h, 16)
                    return None
                E("sync", _st_s, waits=(), inc=None)
                dS.n += 16
                st_dma_free[hc % 2] = dS.n
            ar_free_pe[tg % 2] = pe.n
        wo_dS = dS.n
        if debug:
            dma("sync", dS, sp_dbg, shs[0], waits=[(dS, wo_dS)])
        sbar_gp = pair_barrier(1, wo_dS)

        hqT = big[:, HQT_O:HQT_O + ICH * 1024]
        wgb = [big[:, WG_O + 4096 * i:WG_O + 4096 * (i + 1)] for i in range(2)]
        wub = [big[:, WU_O + 4096 * i:WU_O + 4096 * (i + 1)] for i in range(2)]
        wdb = big[:, WD_O:WD_O + 2816]
        gst = big[:, GST_O:GST_O + 512]

        wgb_free = [pe.n, pe.n]
        hq_free_pe = pe.n
        gst_free_dA = 0
        ffn_first = True
        h_prev_dv = dv.n
        hq_pending = None

        for tg in range(2):
            for ib in range(NIB):
                bi = ib % 2
                for hc in range(DCH):
                    ldg = dma("sync", dS,
                              wgb[bi][:, 512 * hc:512 * (hc + 1)].rearrange(
                                  "p (b i) -> p b i", i=128),
                              wg4[:, hc, 4 * ib:4 * (ib + 1), :],
                              waits=([(pe, wgb_free[bi]), (gp, wbar_gp)]
                                     if hc == 0 else ()))
                for hc in range(DCH):
                    ldu = dma("sync", dS,
                              wub[bi][:, 512 * hc:512 * (hc + 1)].rearrange(
                                  "p (b i) -> p b i", i=128),
                              wu4[:, hc, 4 * ib:4 * (ib + 1), :],
                              waits=())
                for tt in range(8):
                    t16 = 8 * tg + tt
                    for c in range(DCH):
                        E("tensor", lambda e, c=c, t16=t16, bi=bi: e.matmul(
                            P1[:, :],
                            xnqT[:, OWN * c + 128 * t16:
                                 OWN * c + 128 * (t16 + 1)],
                            wgb[bi][:, 512 * c:512 * (c + 1)],
                            start=(c == 0), stop=(c == DCH - 1)),
                          waits=(([(dS, ldg), (ac, ac.n)]
                                  + ([(dv, dv.n)] if ffn_first else []))
                                 if c == 0 else ()),
                          inc=(pe if c == DCH - 1 else None))
                    gate_pe = pe.n
                    for c in range(DCH):
                        E("tensor", lambda e, c=c, t16=t16, bi=bi: e.matmul(
                            P3[:, :],
                            xnqT[:, OWN * c + 128 * t16:
                                 OWN * c + 128 * (t16 + 1)],
                            wub[bi][:, 512 * c:512 * (c + 1)],
                            start=(c == 0), stop=(c == DCH - 1)),
                          waits=([(dS, ldu), (dv, h_prev_dv)]
                                 if c == 0 else ()),
                          inc=(pe if c == DCH - 1 else None))
                    up_pe = pe.n
                    ffn_first = False
                    if hq_pending is not None:
                        hq_pending()
                        hq_pending = None
                    E("scalar", lambda e: e.activation(
                        hsc[:, :], P1[:, :], AF.Silu),
                      waits=[(pe, gate_pe), (dv, dv.n)], inc=ac)
                    hb = hbuf[tt % 2]
                    E("vector", lambda e, hb=hb: e.tensor_tensor(
                        hb[:, :], P3[:, :], hsc[:, :], ALU.mult),
                      waits=[(ac, ac.n), (pe, up_pe)], inc=dv)
                    h_prev_dv = dv.n
                    qdv = quant(hb, 512, q_out, after=[(dv, dv.n)],
                                dst_waits=[(pe, qout_free_pe)])
                    nb4 = 4 if ib < NIB - 1 else 2

                    def _flush(qdv=qdv, nb4=nb4, ib=ib, tt=tt):
                        nonlocal qout_free_pe
                        tpe = transpose8(q_out, nb4,
                                         waits=[(dv, qdv), (pe, hq_free_pe)])
                        qout_free_pe = tpe
                        for m in range(nb4):
                            ic = 4 * ib + m
                            E("vector", lambda e, ic=ic, m=m, tt=tt:
                              e.tensor_copy(
                                  hqT[:, 1024 * ic + 128 * tt:
                                      1024 * ic + 128 * (tt + 1)],
                                  PT[:, 128 * m:128 * (m + 1)]),
                              waits=([(pe, tpe)] if m == 0 else ()), inc=dv)
                    hq_pending = _flush
                wgb_free[bi] = pe.n
            if hq_pending is not None:
                hq_pending()
                hq_pending = None
            hq_dv = dv.n
            for hc in range(DCH):
                ldd = dma("sync", dS,
                          wdb.rearrange("p (c w) -> p c w", w=128),
                          wd4[hc, :, :, :],
                          waits=[(pe, pe.n)])
                for half in range(2):
                    for ic in range(ICH):
                        E("tensor", lambda e, ic=ic, half=half: e.matmul(
                            P2[:, :],
                            wdb[:, 128 * ic:128 * (ic + 1)],
                            hqT[:, 1024 * ic + 512 * half:
                                1024 * ic + 512 * (half + 1)],
                            start=(ic == 0), stop=(ic == ICH - 1)),
                          waits=([(dS, ldd), (dv, hq_dv), (ac, ac.n)]
                                 if ic == 0 else ()),
                          inc=(pe if ic == ICH - 1 else None))
                    E("scalar", lambda e: e.activation(
                        gst[:, :], P2[:, :], AF.Silu),
                      waits=[(pe, pe.n), (dA, gst_free_dA)], inc=ac)
                    gst_free_dA = dma(
                        "scalar", dA,
                        gT_d[hc, :, 1024 * tg + 512 * half:
                             1024 * tg + 512 * (half + 1)],
                        gst[:, :], waits=[(ac, ac.n)])
            hq_free_pe = pe.n
        ffn_dA = dA.n

        selA = t_sel[:, 0:1]
        selB = t_sel[:, 1:2]
        shs3 = [shs[p].rearrange("(c p2) w -> c p2 w", p2=128)
                for p in range(2)]
        fb = [[big[:, 1024 * (6 * i + j):1024 * (6 * i + j + 1)]
               for j in range(6)] for i in range(2)]
        fb_free_dv = [0, 0]
        tx_free_dv = [dv.n, dv.n]
        fo_free_dS = 0
        ld_at = {}
        for t in range(NTO + 1):
            if t < NTO:
                par = t % 2
                w0 = [(gp, sbar_gp), (dv, fb_free_dv[par]), (dA, ffn_dA)]
                if t == 0:
                    w0.append((pe, pe.n))
                l1 = dma("scalar", dA,
                         fb[par][0].rearrange("p (c w) -> p c w", w=128),
                         shs3[0][:, :, 128 * t:128 * (t + 1)].rearrange(
                             "c p w -> p c w"), waits=w0)
                l2 = dma("scalar", dA,
                         fb[par][1].rearrange("p (c w) -> p c w", w=128),
                         shs3[1][:, :, 128 * t:128 * (t + 1)].rearrange(
                             "c p w -> p c w"), waits=())
                l3 = dma("scalar", dA,
                         fb[par][2].rearrange("p (c w) -> p c w", w=128),
                         shs3[0][:, :, OWN + 128 * t:OWN + 128 * (t + 1)
                                 ].rearrange("c p w -> p c w"), waits=())
                l4 = dma("scalar", dA,
                         fb[par][3].rearrange("p (c w) -> p c w", w=128),
                         shs3[1][:, :, OWN + 128 * t:OWN + 128 * (t + 1)
                                 ].rearrange("c p w -> p c w"), waits=())
                l5 = dma("scalar", dA,
                         fb[par][4].rearrange("p (c w) -> p c w", w=128),
                         gT_d[:, :, 128 * t:128 * (t + 1)].rearrange(
                             "c p w -> p c w"), waits=())
                l6 = dma("scalar", dA,
                         fb[par][5].rearrange("p (c w) -> p c w", w=128),
                         xnTo_d[:, :, 128 * t:128 * (t + 1)].rearrange(
                             "c p w -> p c w"), waits=())
                lx = dma("sync", dS, t_x[par][:, :],
                         x_own[128 * t:128 * (t + 1), :],
                         waits=[(dv, tx_free_dv[par])])
                ld_at[t] = (l6, lx)
            if t == 0:
                continue
            tc = t - 1
            par = tc % 2
            l6, lx = ld_at[tc]
            E("vector", lambda e, par=par: e.tensor_tensor(
                f_t[:, :], fb[par][2][:, :], fb[par][3][:, :], ALU.add),
              waits=[(dA, l6), (dv, dv.n)], inc=dv)
            E("vector", lambda e: e.tensor_scalar(
                f_s[:, :], f_t[:, :], selB, None, ALU.mult),
              waits=[(dv, dv.n)], inc=dv)
            E("vector", lambda e, par=par: e.tensor_tensor(
                f_t[:, :], fb[par][0][:, :], fb[par][1][:, :], ALU.add),
              waits=[(dv, dv.n)], inc=dv)
            E("vector", lambda e: e.scalar_tensor_tensor(
                f_s2[:, :], f_t[:, :], selA, f_s[:, :], ALU.mult, ALU.add),
              waits=[(dv, dv.n)], inc=dv)
            E("vector", lambda e, par=par: e.tensor_tensor(
                f_t[:, :], f_s2[:, :], fb[par][5][:, :], ALU.add),
              waits=[(dv, dv.n)], inc=dv)
            E("vector", lambda e, par=par: e.tensor_tensor(
                f_m[:, :], f_t[:, :], fb[par][4][:, :], ALU.mult),
              waits=[(dv, dv.n)], inc=dv)
            fb_free_dv[par] = dv.n
            tpe = transpose8(f_m, 8, waits=[(dv, dv.n)])
            E("vector", lambda e, par=par: e.tensor_tensor(
                f_o[:, :], PT[:, :], t_x[par][:, :], ALU.add),
              waits=[(pe, tpe), (dS, lx), (dS, fo_free_dS)], inc=dv)
            tx_free_dv[par] = dv.n
            fo_free_dS = dma("sync", dS,
                             out_own[128 * tc:128 * (tc + 1), :],
                             f_o[:, :], waits=[(dv, dv.n)])
        E("sync", lambda e: e.nop(), waits=[(dS, dS.n)], inc=None)

        with nc.Block() as block:
            def runner(engname):
                def _(e):
                    for waits, fn, inc, k in sch.steps[engname]:
                        for (s, v) in waits:
                            e.wait_ge(s.h, v)
                        instr = fn(e)
                        if inc is not None:
                            assert instr is not None, engname
                            instr.then_inc(inc.h, k)
                return _
            block.tensor(runner("tensor"))
            block.vector(runner("vector"))
            block.scalar(runner("scalar"))
            block.gpsimd(runner("gpsimd"))
            block.sync(runner("sync"))

    return nc


_cache = {}


def _get_nc():
    if "nc" not in _cache:
        _cache["nc"] = build_program(DEBUG)
    return _cache["nc"]


def kernel(L0_attn_x, L0_attn_Wq, L0_attn_Wk, L0_attn_Wv, L0_attn_Wo,
           L0_ffn_w_gate, L0_ffn_w_up, L0_ffn_w_down):
    x = np.ascontiguousarray(np.asarray(L0_attn_x, dtype=np.float32))
    Wq = np.ascontiguousarray(np.asarray(L0_attn_Wq, dtype=np.float32))
    Wk = np.ascontiguousarray(np.asarray(L0_attn_Wk, dtype=np.float32))
    Wv = np.ascontiguousarray(np.asarray(L0_attn_Wv, dtype=np.float32))
    Wo = np.ascontiguousarray(np.asarray(L0_attn_Wo, dtype=np.float32))
    wg = np.ascontiguousarray(np.asarray(L0_ffn_w_gate, dtype=np.float32))
    wu = np.ascontiguousarray(np.asarray(L0_ffn_w_up, dtype=np.float32))
    wd = np.ascontiguousarray(np.asarray(L0_ffn_w_down, dtype=np.float32))

    wg_p = np.zeros((IPAD, D), dtype=np.float32)
    wg_p[:IFF] = wg
    wu_p = np.zeros((IPAD, D), dtype=np.float32)
    wu_p[:IFF] = wu

    ident = np.eye(128, dtype=np.float32).astype(ml_dtypes.bfloat16)
    r_idx = np.arange(128)
    maskt = np.where(r_idx[:, None] <= r_idx[None, :], 0.0, NEG).astype(
        np.float32)

    epoch = int(np.random.randint(1, 2**30))
    in_maps = []
    for core in range(8):
        b = core // 2
        r = core % 2
        selv = np.zeros((128, 2), dtype=np.float32)
        selv[:, r] = 1.0
        in_maps.append({
            "xb": x[b],
            "x_own": x[b, r * OWN:(r + 1) * OWN],
            "wq_in": Wq[:, 512 * r:512 * (r + 1)],
            "wk_in": Wk[:, 512 * r:512 * (r + 1)],
            "wv_in": Wv[:, 512 * r:512 * (r + 1)],
            "wo_in": Wo[512 * r:512 * (r + 1), :],
            "wg_sh": wg_p[1536 * r:1536 * (r + 1)],
            "wu_sh": wu_p[1536 * r:1536 * (r + 1)],
            "wd_sh": wd[512 * r:512 * (r + 1)],
            "ident": ident,
            "maskt": maskt,
            "selv": selv,
            "rolei": np.array([[r, 0]], dtype=np.int32),
            "epochi": np.array([[epoch, 0]], dtype=np.int32),
        })

    nc = _get_nc()
    res = run_bass_kernel_spmd(nc, in_maps, core_ids=list(range(8)))
    out = np.empty((4, S, D), dtype=np.float32)
    for core in range(8):
        b = core // 2
        r = core % 2
        out[b, r * OWN:(r + 1) * OWN] = res.results[core]["out_own"]
    return out


# version 12
# speedup vs baseline: 1.2825x; 1.0140x over previous
import numpy as np
import ml_dtypes
from contextlib import ExitStack

import concourse.bass as bass
import concourse.mybir as mybir
from concourse.bass_utils import run_bass_kernel_spmd

F32 = mybir.dt.float32
BF16 = mybir.dt.bfloat16
I16 = mybir.dt.int16
ALU = mybir.AluOpType
AF = mybir.ActivationFunctionType
AX = mybir.AxisListType

DEBUG = False

D = 1024
DCH = 8
S = 4096
OWN = 2048
NT = S // 128
NTO = OWN // 128
HPC = 8
NHP = HPC // 2
NQG = S // 512
DK = 64
IFF = 2816
IPAD = 3072
ICH = 22
NIB = IPAD // 512
NEG = -1.0e30

KT_O, QT_O, VH_O = 0, 4096, 8192
XR_O = 12352
PTND_O = 20544
PTDG_O = 24640
WQ_O, WK_O, WV_O, WO_O = 26688, 30784, 34880, 38976
BIG_N = 43072
HQT_O = 0
WG_O = 22528
WU_O = 30720
WD_O = 38912
GST_O = 41728
AR_O = 0


class Sem:
    def __init__(self, h):
        self.h = h
        self.n = 0


class Sched:

    ENGS = ("tensor", "vector", "scalar", "gpsimd", "sync")

    def __init__(self):
        self.steps = {e: [] for e in self.ENGS}

    def emit(self, eng, fn, waits=(), inc=None, inc_by=1):
        w = [(s, v) for (s, v) in waits if v > 0]
        self.steps[eng].append((w, fn, inc, inc_by))
        if inc is not None:
            inc.n += inc_by
        return inc.n if inc is not None else 0


def build_program(debug=False):
    nc = bass.Bass("TRN2", target_bir_lowering=False)

    xb = nc.dram_tensor("xb", [S, D], F32, kind="ExternalInput").ap()
    x_own = nc.dram_tensor("x_own", [OWN, D], F32, kind="ExternalInput").ap()
    wq_in = nc.dram_tensor("wq_in", [D, 512], F32, kind="ExternalInput").ap()
    wk_in = nc.dram_tensor("wk_in", [D, 512], F32, kind="ExternalInput").ap()
    wv_in = nc.dram_tensor("wv_in", [D, 512], F32, kind="ExternalInput").ap()
    wo_in = nc.dram_tensor("wo_in", [512, D], F32, kind="ExternalInput").ap()
    wg_sh = nc.dram_tensor("wg_sh", [12 * 128, D], F32,
                           kind="ExternalInput").ap()
    wu_sh = nc.dram_tensor("wu_sh", [12 * 128, D], F32,
                           kind="ExternalInput").ap()
    wd_sh = nc.dram_tensor("wd_sh", [4 * 128, IFF], F32,
                           kind="ExternalInput").ap()
    rolei = nc.dram_tensor("rolei", [1, 2], mybir.dt.int32,
                           kind="ExternalInput").ap()
    epochi = nc.dram_tensor("epochi", [1, 2], mybir.dt.int32,
                            kind="ExternalInput").ap()
    ident = nc.dram_tensor("ident", [128, 128], BF16, kind="ExternalInput").ap()
    maskt = nc.dram_tensor("maskt", [128, 128], F32, kind="ExternalInput").ap()
    selv = nc.dram_tensor("selv", [128, 2], F32, kind="ExternalInput").ap()
    out_own = nc.dram_tensor("out_own", [OWN, D], F32, kind="ExternalOutput").ap()

    dkind = "ExternalOutput" if debug else "Internal"
    xnT_d = nc.dram_tensor("xnT_d", [DCH, 128, S], BF16, kind=dkind).ap()
    xnTo_d = nc.dram_tensor("xnTo_d", [DCH, 128, OWN], BF16, kind=dkind).ap()
    attnT_d = nc.dram_tensor("attnT_d", [NHP, 128, S], BF16, kind=dkind).ap()
    gT_d = nc.dram_tensor("gT_d", [DCH, 128, OWN], BF16, kind=dkind).ap()
    shwg = nc.dram_tensor("shwg", [24 * D, 128], BF16,
                          addr_space="Shared").ap()
    shwu = nc.dram_tensor("shwu", [24 * D, 128], BF16,
                          addr_space="Shared").ap()
    shwd = nc.dram_tensor("shwd", [8 * IFF, 128], BF16,
                          addr_space="Shared").ap()
    shs = nc.dram_tensor("shs", [2, DCH * 128, S], BF16,
                         addr_space="Shared").ap()
    shflag = nc.dram_tensor("shflag", [2, 2, 16], mybir.dt.int32,
                            addr_space="Shared").ap()
    if debug:
        xnq_dbg = nc.dram_tensor("xnq_dbg", [DCH, 128, OWN], BF16,
                                 kind="ExternalOutput").ap()
        sp_dbg = nc.dram_tensor("sp_dbg", [DCH * 128, S], BF16,
                                kind="ExternalOutput").ap()
        kqv_dbg = nc.dram_tensor("kqv_dbg", [3, 128, 4160], BF16,
                                 kind="ExternalOutput").ap()
        p_dbg = nc.dram_tensor("p_dbg", [6, 128, 2048], BF16,
                               kind="ExternalOutput").ap()
        av_dbg = nc.dram_tensor("av_dbg", [4, 128, 512], F32,
                                kind="ExternalOutput").ap()

    sch = Sched()
    es = ExitStack()

    def sb(name, shape, dtype):
        return es.enter_context(nc.sbuf_tensor(name, shape, dtype))

    def ps(name, shape, dtype):
        return es.enter_context(nc.psum_tensor(name, shape, dtype))

    def sem(name):
        return Sem(es.enter_context(nc.semaphore(name)))

    with es:
        big = sb("big", [128, BIG_N], BF16)
        xnqT = sb("xnqT", [128, DCH * OWN], BF16)
        t_x = [sb(f"t_x{i}", [128, D], F32) for i in range(2)]
        t_xn = sb("t_xn", [128, D], F32)
        t_xnb = sb("t_xnb", [128, D], BF16)
        bnb = sb("bnb", [128, 12], F32)
        mv = sb("mv", [128, 2], F32)
        ln_s = sb("ln_s", [128, 4], F32)
        lnsv = sb("lnsv", [128, 2 * NTO], F32)
        q_xn = sb("q_xn", [128, 1024], F32)
        q_a = sb("q_a", [128, 1024], F32)
        q_i1 = sb("q_i1", [128, 1024], I16)
        q_i2 = sb("q_i2", [128, 1024], I16)
        q_i3 = sb("q_i3", [128, 1024], I16)
        q_t1 = sb("q_t1", [128, 1024], BF16)
        q_t2 = sb("q_t2", [128, 1024], BF16)
        q_sg = sb("q_sg", [128, 1024], BF16)
        q_out = sb("q_out", [128, 1024], BF16)
        q_bm = sb("q_bm", [128, 64], F32)
        q_S = sb("q_S", [128, 64], F32)
        q_R = sb("q_R", [128, 64], F32)
        q_Sb = sb("q_Sb", [128, 64], BF16)
        st = [sb(f"st{i}", [128, 1024], BF16) for i in range(2)]
        t_mask = sb("t_mask", [128, 128], F32)
        t_id = sb("t_id", [128, 128], BF16)
        t_sel = sb("t_sel", [128, 2], F32)
        rden = [sb(f"rden{i}", [128, 512], F32) for i in range(2)]
        rb64 = [sb(f"rb64{i}", [128, 512], F32) for i in range(2)]
        outst = [sb(f"outst{i}", [128, 512], BF16) for i in range(2)]
        t_ones = sb("t_ones", [128, 64], F32)
        t_ri = sb("t_ri", [1, 2], mybir.dt.int32)
        t_ep = sb("t_ep", [1, 16], mybir.dt.int32)
        t_zz = sb("t_zz", [1, 16], mybir.dt.int32)
        regs = {}
        t_wf = None
        hsc = sb("hsc", [128, 512], F32)
        hbuf = [sb(f"hbuf{i}", [128, 512], F32) for i in range(2)]
        hbuf2 = [sb(f"hbuf2{i}", [128, 1024], F32) for i in range(2)]
        f_t = sb("f_t", [128, 1024], F32)
        f_s = sb("f_s", [128, 1024], F32)
        f_s2 = sb("f_s2", [128, 1024], F32)
        f_m = sb("f_m", [128, 1024], BF16)
        f_o = sb("f_o", [128, 1024], F32)

        P0 = ps("P0", [128, 2048], F32)
        P1 = ps("P1", [128, 512], F32)
        P2 = ps("P2", [128, 512], F32)
        P3 = ps("P3", [128, 512], F32)
        PT = ps("PT", [128, 1024], BF16)

        dS = sem("dS")
        dA = sem("dA")
        dP = sem("dP")
        pe = sem("pe")
        dv = sem("dv")
        ac = sem("ac")
        gp = sem("gp")

        E = sch.emit

        def dma(q, qs, out, in_, waits=()):
            return E(q, lambda e, o=out, i=in_: e.dma_start(out=o, in_=i),
                     waits=waits, inc=qs, inc_by=16)

        def quant(src_f32, n, dst_bf16, after=(), dst_waits=()):
            for _ in quant_gen(src_f32, n, dst_bf16, after, dst_waits):
                pass
            return dv.n

        def quant_gen(src_f32, n, dst_bf16, after=(), dst_waits=()):
            assert n % 32 == 0
            h = n // 2
            g = h // 16

            co = [0, h]
            go = [0, g]

            def stp(fn, waits=()):
                E("vector", fn, waits=waits, inc=dv)

            w0 = list(after)
            for i in range(2):
                c, gg = co[i], go[i]
                x3 = src_f32[:, c:c + h].rearrange("p (g w) -> p g w", w=16)
                stp(lambda e, x3=x3, gg=gg: e.tensor_reduce(
                    q_bm[:, gg:gg + g], x3, AX.X, ALU.max,
                    apply_absolute_value=True), waits=w0)
                w0 = ()
            yield
            for i in range(2):
                gg = go[i]
                stp(lambda e, gg=gg: e.tensor_scalar(
                    q_S[:, gg:gg + g], q_bm[:, gg:gg + g], 1.0 / 6.0, 1e-12,
                    ALU.mult, ALU.max))
            yield
            for i in range(2):
                gg = go[i]
                stp(lambda e, gg=gg: e.reciprocal(
                    q_R[:, gg:gg + g], q_S[:, gg:gg + g]))
            yield
            for i in range(2):
                gg = go[i]
                stp(lambda e, gg=gg: e.tensor_copy(
                    q_Sb[:, gg:gg + g], q_S[:, gg:gg + g]))
            yield
            for i in range(2):
                c, gg = co[i], go[i]
                x3 = src_f32[:, c:c + h].rearrange("p (g w) -> p g w", w=16)
                rb = q_R[:, gg:gg + g].unsqueeze(2).broadcast_to((128, g, 16))
                o3 = q_xn[:, c:c + h].rearrange("p (g w) -> p g w", w=16)
                stp(lambda e, x3=x3, rb=rb, o3=o3: e.tensor_tensor(
                    o3, x3, rb, ALU.mult))
            yield
            for i in range(2):
                c = co[i]
                stp(lambda e, c=c: e.tensor_scalar(
                    q_sg[:, c:c + h], q_xn[:, c:c + h], 0.0, 2.0,
                    ALU.is_gt, ALU.mult))
            yield
            for i in range(2):
                c = co[i]
                stp(lambda e, c=c: e.scalar_tensor_tensor(
                    q_a[:, c:c + h], q_sg[:, c:c + h], -1.0,
                    q_xn[:, c:c + h], ALU.add, ALU.mult))
            yield
            for i in range(2):
                c = co[i]
                stp(lambda e, c=c: e.tensor_scalar(
                    q_xn[:, c:c + h], q_a[:, c:c + h], 2.0, 4.0,
                    ALU.mult, ALU.min))
            yield
            for i in range(2):
                c = co[i]
                stp(lambda e, c=c: e.tensor_copy(
                    q_i1[:, c:c + h], q_xn[:, c:c + h]))
            yield
            for i in range(2):
                c = co[i]
                stp(lambda e, c=c: e.tensor_scalar(
                    q_xn[:, c:c + h], q_a[:, c:c + h], 2.0, 4.0,
                    ALU.max, ALU.min))
            yield
            for i in range(2):
                c = co[i]
                stp(lambda e, c=c: e.tensor_copy(
                    q_i2[:, c:c + h], q_xn[:, c:c + h]))
            yield
            for i in range(2):
                c = co[i]
                stp(lambda e, c=c: e.tensor_scalar(
                    q_xn[:, c:c + h], q_a[:, c:c + h], 0.5, 2.0,
                    ALU.mult, ALU.max))
            yield
            for i in range(2):
                c = co[i]
                stp(lambda e, c=c: e.tensor_copy(
                    q_i3[:, c:c + h], q_xn[:, c:c + h]))
            yield
            for i in range(2):
                c = co[i]
                stp(lambda e, c=c: e.scalar_tensor_tensor(
                    q_t1[:, c:c + h], q_i1[:, c:c + h], 0.5, q_i2[:, c:c + h],
                    ALU.mult, ALU.add))
            yield
            for i in range(2):
                c = co[i]
                stp(lambda e, c=c: e.scalar_tensor_tensor(
                    q_t2[:, c:c + h], q_i3[:, c:c + h], 2.0, q_t1[:, c:c + h],
                    ALU.mult, ALU.add))
            yield
            for i in range(2):
                c = co[i]
                stp(lambda e, c=c: e.tensor_scalar(
                    q_t2[:, c:c + h], q_t2[:, c:c + h], -6.0, None, ALU.add))
            yield
            for i in range(2):
                c, gg = co[i], go[i]
                sbb = q_Sb[:, gg:gg + g].unsqueeze(2).broadcast_to((128, g, 16))
                t23 = q_t2[:, c:c + h].rearrange("p (g w) -> p g w", w=16)
                o3 = q_t1[:, c:c + h].rearrange("p (g w) -> p g w", w=16)
                stp(lambda e, t23=t23, sbb=sbb, o3=o3: e.tensor_tensor(
                    o3, t23, sbb, ALU.mult))
            yield
            for i in range(2):
                c = co[i]
                stp(lambda e, c=c: e.scalar_tensor_tensor(
                    dst_bf16[:, c:c + h], q_sg[:, c:c + h], -1.0,
                    q_t1[:, c:c + h], ALU.add, ALU.mult),
                    waits=(list(dst_waits) if i == 0 else ()))
            yield

        def transpose8(src_bf16, nblk, pt_off=0, waits=()):
            for m in range(nblk):
                w = waits if m == 0 else ()
                E("tensor", lambda e, m=m: e.transpose(
                    PT[:, pt_off + 128 * m:pt_off + 128 * (m + 1)],
                    src_bf16[:, 128 * m:128 * (m + 1)], t_id[:, :]),
                    waits=w, inc=(pe if m == nblk - 1 else None))
            return pe.n

        dma("sync", dS, t_id[:, :], ident)
        dma("sync", dS, t_mask[:, :], maskt)
        dma("sync", dS, t_sel[:, :], selv)
        dma("sync", dS, t_ri[0:1, :], rolei)
        dma("sync", dS, t_ep[0:1, 0:2], epochi)
        a_small = dS.n

        def _init_regs_sync(e):
            regs["sy_role"] = e.alloc_register("sy_role")
            e.wait_ge(dS.h, a_small)
            return e.load(regs["sy_role"], t_ri[0:1, 0:1])
        E("sync", _init_regs_sync, waits=(), inc=None)

        def _init_regs_gp(e):
            e.memset(t_zz[0:1, :], 0)
            regs["gp_role"] = e.alloc_register("gp_role")
            regs["gp_f"] = e.alloc_register("gp_f")
            regs["gp_c"] = e.alloc_register("gp_c")
            regs["gp_k"] = e.alloc_register("gp_k")
            e.wait_ge(dS.h, a_small)
            e.load(regs["gp_role"], t_ri[0:1, 0:1])
            e.load(regs["gp_k"], t_ep[0:1, 0:1])
            with e.If_eq(regs["gp_role"], 0):
                e.dma_start(out=shflag[0, 0, 0:2],
                            in_=t_zz[0:1, 0:2]).then_inc(dP.h, 16)
                e.dma_start(out=shflag[0, 1, 0:2],
                            in_=t_zz[0:1, 0:2]).then_inc(dP.h, 16)
            with e.Else():
                e.dma_start(out=shflag[1, 0, 0:2],
                            in_=t_zz[0:1, 0:2]).then_inc(dP.h, 16)
                e.dma_start(out=shflag[1, 1, 0:2],
                            in_=t_zz[0:1, 0:2]).then_inc(dP.h, 16)
            return e.wait_ge(dP.h, 32)
        E("gpsimd", _init_regs_gp, waits=(), inc=None)
        dP.n += 32

        def _barrier_body(e, bar, after_dS, dp_wait):
            e.wait_ge(dS.h, after_dS)
            r = regs["gp_role"]
            f, c, k = regs["gp_f"], regs["gp_c"], regs["gp_k"]
            for me in (0, 1):
                other = 1 - me
                with (e.If_eq(r, 0) if me == 0 else e.Else()):
                    e.dma_start(out=shflag[me, bar, 0:2],
                                in_=t_ep[0:1, 0:2]).then_inc(dP.h, 16)
                    e.wait_ge(dP.h, dp_wait)
                    e.load(f, shflag[other, bar:bar + 1, 0:1])
                    e.reg_alu(c, f, k, ALU.not_equal)
                    with e.While(c):
                        e.load(f, shflag[other, bar:bar + 1, 0:1])
                        e.reg_alu(c, f, k, ALU.not_equal)
            return e.nop()

        def pair_barrier(bar, after_dS):
            dp_wait = dP.n + 16
            E("gpsimd", lambda e, bar=bar, a=after_dS, w=dp_wait:
              _barrier_body(e, bar, a, w), waits=(), inc=gp)
            dP.n += 16
            return gp.n

        for (w_in, w_off, scale, ncol) in [
                (wq_in, WQ_O, 0.125, 512), (wk_in, WK_O, 1.0, 512),
                (wv_in, WV_O, 1.0, 512), (wo_in, WO_O, 1.0, 1024)]:
            nch = w_in.shape[0] // 128
            for pc in range(nch):
                wv_ = dma("sync", dS, t_x[0][:, :ncol],
                          w_in[128 * pc:128 * (pc + 1), :],
                          waits=[(ac, ac.n)])
                E("scalar", lambda e, w_off=w_off, pc=pc, scale=scale,
                  ncol=ncol: e.activation(
                      big[:, w_off + ncol * pc:w_off + ncol * (pc + 1)],
                      t_x[0][:, :ncol], AF.Copy, scale=scale),
                  waits=[(dS, wv_)], inc=ac)
        aw_done_ac = ac.n

        st_dma_free = [0, 0]
        qout_free_pe = 0
        land_free_dv = [0, 0]
        ln_done_ac_ref = [0]

        def prep_piece_gen(src_ap, cn, dst0_ap, dst1_ap, par):
            nonlocal qout_free_pe
            land = t_x[par % 2][:, :cn]
            ldw = dma("gpsimd", dP, land, src_ap,
                      waits=[(dv, land_free_dv[par % 2]),
                             (ac, ln_done_ac_ref[0])])
            yield
            yield from quant_gen(land, cn, q_out, after=[(dP, ldw)],
                                 dst_waits=[(pe, qout_free_pe)])
            qdv = dv.n
            land_free_dv[par % 2] = qdv
            nb8 = cn // 128
            tpe = transpose8(q_out, nb8,
                             waits=[(dv, qdv), (ac, aw_done_ac)])
            qout_free_pe = tpe
            sti = st[par % 2]
            E("vector", lambda e, sti=sti, nb8=nb8: e.tensor_copy(
                sti[:, :128 * nb8], PT[:, :128 * nb8]),
              waits=[(pe, tpe), (dS, st_dma_free[par % 2])], inc=dv)
            src_st = sti[:, :128 * nb8].rearrange("p (m w) -> p m w", w=128)
            d0 = dst0_ap.rearrange("(m p) w -> p m w", p=128)
            d1 = dst1_ap.rearrange("(m p) w -> p m w", p=128)
            dvw = dv.n

            def _branched_store(e, d0=d0, d1=d1, src_st=src_st, dvw=dvw):
                e.wait_ge(dv.h, dvw)
                with e.If_eq(regs["sy_role"], 0):
                    e.dma_start(out=d0, in_=src_st).then_inc(dS.h, 16)
                with e.Else():
                    e.dma_start(out=d1, in_=src_st).then_inc(dS.h, 16)
                return None
            E("sync", _branched_store, waits=(), inc=None)
            dS.n += 16
            st_dma_free[par % 2] = dS.n
            yield

        def xnq_tile_gen(t, par):
            nonlocal qout_free_pe
            land = t_x[par % 2][:, :]
            ldw = dma("gpsimd", dP, land,
                      x_own[128 * t:128 * (t + 1), :],
                      waits=[(dv, land_free_dv[par % 2]),
                             (ac, ln_done_ac_ref[0])])
            E("vector", lambda e, t=t: e.tensor_scalar(
                t_xn[:, :], land[:, :], lnsv[:, 2 * t:2 * t + 1],
                lnsv[:, 2 * t + 1:2 * t + 2], ALU.mult, ALU.add),
              waits=[(dP, ldw), (dv, dv.n)], inc=dv)
            land_free_dv[par % 2] = dv.n
            yield
            yield from quant_gen(t_xn, D, q_out, after=[(dv, dv.n)],
                                 dst_waits=[(pe, qout_free_pe)])
            tpe = transpose8(q_out, 8, waits=[(dv, dv.n)])
            qout_free_pe = tpe
            for c in range(DCH):
                E("vector", lambda e, c=c, t=t: e.tensor_copy(
                    xnqT[:, OWN * c + 128 * t:OWN * c + 128 * (t + 1)],
                    PT[:, 128 * c:128 * (c + 1)]),
                  waits=([(pe, tpe)] if c == 0 else ()), inc=dv)
            if debug:
                dstq = xnq_dbg[:, :, 128 * t:128 * (t + 1)].rearrange(
                    "c p w -> p c w")
                dma("sync", dS, dstq,
                    q_out[:, :].rearrange("p (c w) -> p c w", w=128),
                    waits=[(dv, dv.n)])
            yield

        deferred = []

        def build_deferred():
            par = 0
            for (src, sh_dst) in [(wg_sh, shwg), (wu_sh, shwu)]:
                for lb in range(12):
                    deferred.append(prep_piece_gen(
                        src[128 * lb:128 * (lb + 1), :], D,
                        sh_dst[D * lb:D * (lb + 1), :],
                        sh_dst[D * (12 + lb):D * (13 + lb), :], par))
                    par += 1
            for t in range(NTO):
                deferred.append(xnq_tile_gen(t, par))
                par += 1
            for lb in range(4):
                for pc in range(3):
                    c0 = 1024 * pc
                    cn = min(1024, IFF - c0)
                    deferred.append(prep_piece_gen(
                        wd_sh[128 * lb:128 * (lb + 1), c0:c0 + cn], cn,
                        shwd[IFF * lb + c0:IFF * lb + c0 + cn, :],
                        shwd[IFF * (4 + lb) + c0:IFF * (4 + lb) + c0 + cn, :],
                        par))
                    par += 1

        def pull(n):
            k = 0
            while deferred and k < n:
                try:
                    next(deferred[0])
                    k += 1
                except StopIteration:
                    deferred.pop(0)

        def pull_all():
            while deferred:
                pull(1000)

        wg4 = shwg.rearrange("(b c p) i -> p c b i", c=8, p=128)
        wu4 = shwu.rearrange("(b c p) i -> p c b i", c=8, p=128)
        wd4 = shwd.rearrange("(h ic p) w -> h p ic w", h=8, p=128)

        def ln_tile(xsrc, ti, store_dst, save_ti=None):
            land = t_x[ti % 2][:, :]
            ldw = dma("sync", dS, land, xsrc[128 * ti:128 * (ti + 1), :],
                      waits=[(dv, dv.n), (ac, ac.n)])
            E("vector", lambda e: e.bn_stats(bnb[:, 0:6], land[:, 0:512]),
              waits=[(dS, ldw)], inc=dv)
            E("vector", lambda e: e.bn_stats(bnb[:, 6:12], land[:, 512:1024]),
              inc=dv)
            E("vector", lambda e: e.bn_aggr(mv[:, :], bnb[:, :]),
              waits=[(dv, dv.n)], inc=dv)
            E("vector", lambda e: e.tensor_scalar(
                ln_s[:, 0:1], mv[:, 1:2], 1e-5, None, ALU.add),
              waits=[(dv, dv.n)], inc=dv)
            E("scalar", lambda e: e.activation(
                ln_s[:, 1:2], ln_s[:, 0:1], AF.Sqrt),
              waits=[(dv, dv.n)], inc=ac)
            E("vector", lambda e: e.reciprocal(ln_s[:, 2:3], ln_s[:, 1:2]),
              waits=[(ac, ac.n)], inc=dv)
            E("vector", lambda e: e.scalar_tensor_tensor(
                ln_s[:, 3:4], mv[:, 0:1], -1.0, ln_s[:, 2:3],
                ALU.mult, ALU.mult), waits=[(dv, dv.n)], inc=dv)
            if save_ti is not None:
                E("vector", lambda e, t=save_ti: e.tensor_copy(
                    lnsv[:, 2 * t:2 * t + 2], ln_s[:, 2:4]),
                  waits=[(dv, dv.n)], inc=dv)
            E("scalar", lambda e: e.activation(
                t_xnb[:, :], land[:, :], AF.Identity,
                scale=ln_s[:, 2:3], bias=ln_s[:, 3:4]),
              waits=[(dv, dv.n)], inc=ac)
            tpe = transpose8(t_xnb, 8, waits=[(ac, ac.n), (dv, dv.n)])
            sti = st[ti % 2]
            E("vector", lambda e, sti=sti: e.tensor_copy(sti[:, :], PT[:, :]),
              waits=[(pe, tpe), (dS, st_dma_free[ti % 2])], inc=dv)
            dst = store_dst[:, :, 128 * ti:128 * (ti + 1)].rearrange(
                "c p w -> p c w")
            st_dma_free[ti % 2] = dma(
                "sync", dS, dst,
                sti[:, :].rearrange("p (c w) -> p c w", w=128),
                waits=[(dv, dv.n)])
            return st_dma_free[ti % 2]

        ln_dma_at = []
        for ti in range(NT):
            ln_dma_at.append(ln_tile(xb, ti, xnT_d))
        for ti in range(NTO):
            ln_tile(x_own, ti, xnTo_d, save_ti=ti)
        ln_done_dS = dS.n
        ln_done_dv = dv.n
        ln_done_ac_ref[0] = ac.n
        land_free_dv[0] = dv.n
        land_free_dv[1] = dv.n
        build_deferred()

        kT = big[:, KT_O:KT_O + 4096]
        qT = big[:, QT_O:QT_O + 4096]
        vh = big[:, VH_O:VH_O + 4160]
        xr = [big[:, XR_O + 4096 * i:XR_O + 4096 * (i + 1)] for i in range(2)]
        pnd = [big[:, PTND_O + 2048 * i:PTND_O + 2048 * (i + 1)]
               for i in range(2)]
        pdg = [big[:, PTDG_O + 512 * j:PTDG_O + 512 * (j + 1)]
               for j in range(4)]

        vh3 = vh.rearrange("p (c w) -> p c w", w=65)
        E("vector", lambda e: e.memset(vh3[:, :, 64:65], 1.0),
          waits=[], inc=dv)
        E("vector", lambda e: e.memset(t_ones[0:1, :], 1.0), inc=dv)
        for j in range(4):
            E("vector", lambda e, j=j: e.memset(pdg[j][:, :], 0.0), inc=dv)
        attn_init_dv = dv.n

        xr_free_pe = [0, 0]
        pnd_free_pe = [0, 0]
        pdg_free_pe = [0, 0, 0, 0]
        p1b_free_dv = [0, 0]
        p0_free_ac = 0
        p2div_free_dv = 0
        outst_free_dS = [0, 0]
        rr_free_dv = [0, 0]
        dbg_gate_dv = 0
        ogi = 0

        for hp in range(NHP):
            kq_ready_ac = {}
            v_ready_dv = {}
            for tg in range(DCH):
                xbuf = xr[tg % 2]
                lw = [(pe, xr_free_pe[tg % 2]), (dS, ln_dma_at[4 * tg + 3])]
                src = xnT_d[:, :, 512 * tg:512 * (tg + 1)].rearrange(
                    "c p w -> p c w")
                ldx = dma("sync", dS,
                          xbuf.rearrange("p (c w) -> p c w", w=512),
                          src, waits=lw)
                for c in range(DCH):
                    E("tensor", lambda e, c=c, xbuf=xbuf, hp=hp: e.matmul(
                        P2[:, :],
                        big[:, WK_O + 512 * c + 128 * hp:
                             WK_O + 512 * c + 128 * (hp + 1)],
                        xbuf[:, 512 * c:512 * (c + 1)],
                        start=(c == 0), stop=(c == DCH - 1)),
                      waits=([(dS, ldx), (ac, ac.n), (dv, dv.n)]
                             if c == 0 else ()),
                      inc=(pe if c == DCH - 1 else None))
                E("scalar", lambda e, tg=tg: e.activation(
                    kT[:, 512 * tg:512 * (tg + 1)], P2[:, :], AF.Copy),
                  waits=[(pe, pe.n)], inc=ac)
                for c in range(DCH):
                    E("tensor", lambda e, c=c, xbuf=xbuf, hp=hp: e.matmul(
                        P2[:, :],
                        big[:, WQ_O + 512 * c + 128 * hp:
                             WQ_O + 512 * c + 128 * (hp + 1)],
                        xbuf[:, 512 * c:512 * (c + 1)],
                        start=(c == 0), stop=(c == DCH - 1)),
                      waits=([(ac, ac.n)] if c == 0 else ()),
                      inc=(pe if c == DCH - 1 else None))
                E("scalar", lambda e, tg=tg: e.activation(
                    qT[:, 512 * tg:512 * (tg + 1)], P2[:, :], AF.Copy),
                  waits=[(pe, pe.n)], inc=ac)
                kq_ready_ac[tg] = ac.n
                for tt in range(4):
                    for c in range(DCH):
                        E("tensor", lambda e, c=c, tt=tt, xbuf=xbuf, hp=hp:
                          e.matmul(
                              P2[:, 128 * tt:128 * (tt + 1)],
                              xbuf[:, 512 * c + 128 * tt:
                                   512 * c + 128 * (tt + 1)],
                              big[:, WV_O + 512 * c + 128 * hp:
                                   WV_O + 512 * c + 128 * (hp + 1)],
                              start=(c == 0), stop=(c == DCH - 1)),
                          waits=([(ac, kq_ready_ac[tg]),
                                  (dv, dv.n)] if (tt == 0 and c == 0) else ()),
                          inc=(pe if c == DCH - 1 else None))
                vpe = pe.n
                for tt in range(4):
                    c32 = 4 * tg + tt
                    for h2 in range(2):
                        E("vector", lambda e, tt=tt, h2=h2, c32=c32:
                          e.tensor_copy(
                              vh[:, 2080 * h2 + 65 * c32:
                                 2080 * h2 + 65 * c32 + 64],
                              P2[:, 128 * tt + 64 * h2:
                                 128 * tt + 64 * h2 + 64]),
                          waits=([(pe, vpe),
                                  (dv, attn_init_dv)] if tt == 0 and h2 == 0
                                 else ()), inc=dv)
                v_ready_dv[tg] = dv.n
                xr_free_pe[tg % 2] = pe.n
            proj_ac = ac.n
            proj_dv = dv.n
            if debug and hp == 0:
                dma("sync", dS, kqv_dbg[0, :, 0:4096], kT,
                    waits=[(ac, proj_ac), (dv, proj_dv)])
                dma("sync", dS, kqv_dbg[1, :, 0:4096], qT, waits=())
                dma("sync", dS, kqv_dbg[2, :, :], vh, waits=())

            for h2 in range(2):
                r0 = 64 * h2
                for g in range(NQG):
                    nb2 = 2 * g
                    PB = [P1, P3][ogi % 2]
                    exp_ac = {}
                    scores_pe = {}
                    for b in range(nb2 + 1):
                        is_diag = (b == nb2)
                        if not is_diag:
                            off = 1024 * (b % 2)
                            for j2 in range(2):
                                kc = 2 * b + j2
                                w = []
                                if j2 == 0:
                                    w = [(ac, exp_ac.get(b - 2, p0_free_ac)),
                                         (ac, proj_ac), (dv, proj_dv)]
                                E("tensor", lambda e, kc=kc, j2=j2, r0=r0,
                                  g=g, off=off: e.matmul(
                                      P0[:, off + 512 * j2:
                                         off + 512 * (j2 + 1)],
                                      kT[r0:r0 + 64,
                                         128 * kc:128 * (kc + 1)],
                                      qT[r0:r0 + 64, 512 * g:512 * (g + 1)],
                                      start=True, stop=True),
                                  waits=w, inc=(pe if j2 == 1 else None))
                            scores_pe[b] = pe.n
                        else:
                            for j in range(4):
                                kc = 4 * g + j
                                nw = 512 - 128 * j
                                w = []
                                if j == 0:
                                    w = [(ac, exp_ac.get(nb2 - 1,
                                                         p0_free_ac)),
                                         (ac, proj_ac), (dv, proj_dv)]
                                E("tensor", lambda e, kc=kc, j=j, nw=nw,
                                  r0=r0, g=g: e.matmul(
                                      P0[:, 512 * j:512 * j + nw],
                                      kT[r0:r0 + 64,
                                         128 * kc:128 * (kc + 1)],
                                      qT[r0:r0 + 64,
                                         512 * g + 128 * j:512 * (g + 1)],
                                      start=True, stop=True),
                                  waits=w, inc=pe)
                                scores_pe[(b, j)] = pe.n
                        if b >= 1:
                            bb = b - 1
                            pb = pnd[bb % 2]
                            for j2 in range(2):
                                kc = 2 * bb + j2
                                w = []
                                if j2 == 0:
                                    w = [(ac, exp_ac[bb])]
                                    if b == 1:
                                        w.append((dv, p1b_free_dv[ogi % 2]))
                                E("tensor", lambda e, kc=kc, j2=j2, h2=h2,
                                  b=b, pb=pb, PB=PB: e.matmul(
                                      PB[0:65, :],
                                      vh[:, 2080 * h2 + 65 * kc:
                                         2080 * h2 + 65 * kc + 65],
                                      pb[:, 512 * j2:512 * (j2 + 1)],
                                      start=(b == 1 and j2 == 0), stop=False),
                                  waits=w, inc=(pe if j2 == 1 else None))
                            pnd_free_pe[bb % 2] = pe.n
                        if not is_diag:
                            off = 1024 * (b % 2)
                            E("scalar", lambda e, b=b, off=off: e.activation(
                                pnd[b % 2][:, 0:1024],
                                P0[:, off:off + 1024], AF.Exp),
                              waits=[(pe, scores_pe[b]),
                                     (pe, pnd_free_pe[b % 2])], inc=ac)
                            exp_ac[b] = ac.n
                            pull(1)
                        else:
                            for j in range(4):
                                E("vector", lambda e, j=j: e.tensor_tensor(
                                    P0[:, 512 * j:512 * j + 128],
                                    P0[:, 512 * j:512 * j + 128],
                                    t_mask[:, :], ALU.add),
                                  waits=[(pe, scores_pe[(b, j)])], inc=dv)
                                mdv = dv.n
                                nw = 512 - 128 * j
                                E("scalar", lambda e, j=j, nw=nw:
                                  e.activation(
                                      pdg[j][:, 128 * j:128 * j + nw],
                                      P0[:, 512 * j:512 * j + nw], AF.Exp),
                                  waits=[(dv, mdv),
                                         (pe, pdg_free_pe[j])], inc=ac)
                                exp_ac[(b, j)] = ac.n
                    p0_free_ac = ac.n
                    for j in range(4):
                        kc = 4 * g + j
                        w = [(ac, exp_ac[(nb2, j)])]
                        if nb2 == 0 and j == 0:
                            w.append((dv, p1b_free_dv[ogi % 2]))
                        E("tensor", lambda e, kc=kc, j=j, h2=h2, g=g,
                          nb2=nb2, PB=PB: e.matmul(
                              PB[0:65, :],
                              vh[:, 2080 * h2 + 65 * kc:
                                 2080 * h2 + 65 * kc + 65],
                              pdg[j][:, :],
                              start=(nb2 == 0 and j == 0), stop=(j == 3)),
                          waits=w, inc=(pe if j == 3 else None))
                        pdg_free_pe[j] = pe.n
                    av_pe = pe.n
                    oi = ogi % 2
                    E("vector", lambda e, oi=oi, PB=PB: e.reciprocal(
                        rden[oi][0:1, :], PB[64:65, :]),
                      waits=[(pe, av_pe), (pe, rr_free_dv[oi])], inc=dv)
                    rdv = dv.n
                    E("tensor", lambda e, oi=oi: e.matmul(
                        P2[0:64, :], t_ones[0:1, 0:64], rden[oi][0:1, :],
                        start=True, stop=True),
                      waits=[(dv, max(rdv, p2div_free_dv, proj_dv))], inc=pe)
                    rr_free_dv[oi] = pe.n
                    E("vector", lambda e, oi=oi: e.tensor_copy(
                        rb64[oi][0:64, :], P2[0:64, :]),
                      waits=[(pe, pe.n)], inc=dv)
                    p2div_free_dv = dv.n
                    E("vector", lambda e, oi=oi, PB=PB: e.tensor_tensor(
                        outst[oi][0:64, :], PB[0:64, :], rb64[oi][0:64, :],
                        ALU.mult),
                      waits=[(dS, outst_free_dS[oi]), (dv, dv.n)], inc=dv)
                    p1b_free_dv[ogi % 2] = dv.n
                    od = dma("sync", dS,
                             attnT_d[hp, r0:r0 + 64, 512 * g:512 * (g + 1)],
                             outst[oi][0:64, :], waits=[(dv, dv.n)])
                    outst_free_dS[oi] = od
                    ogi += 1
                    pull(8)

        pull_all()
        wprep_dS = dS.n
        wbar_gp = pair_barrier(0, wprep_dS)
        attn_done_dS = dS.n

        ar = [big[:, AR_O + 2048 * i:AR_O + 2048 * (i + 1)] for i in range(2)]
        ar_free_pe = [pe.n, pe.n]
        for tg in range(DCH):
            abuf = ar[tg % 2]
            src = attnT_d[:, :, 512 * tg:512 * (tg + 1)].rearrange(
                "c p w -> p c w")
            lda = dma("sync", dS, abuf.rearrange("p (c w) -> p c w", w=512),
                      src, waits=[(pe, ar_free_pe[tg % 2]),
                                  (dS, attn_done_dS)])
            for hc in range(DCH):
                for a in range(4):
                    E("tensor", lambda e, a=a, hc=hc, abuf=abuf: e.matmul(
                        P2[:, :],
                        big[:, WO_O + 1024 * a + 128 * hc:
                             WO_O + 1024 * a + 128 * (hc + 1)],
                        abuf[:, 512 * a:512 * (a + 1)],
                        start=(a == 0), stop=(a == 3)),
                      waits=([(dS, lda), (dv, dv.n)] if a == 0 else ()),
                      inc=(pe if a == 3 else None))
                sti = st[hc % 2]
                E("vector", lambda e, sti=sti: e.tensor_copy(
                    sti[:, 0:512], P2[:, :]),
                  waits=[(pe, pe.n), (dS, st_dma_free[hc % 2])], inc=dv)
                d0 = shs[0, 128 * hc:128 * (hc + 1), 512 * tg:512 * (tg + 1)]
                d1 = shs[1, 128 * hc:128 * (hc + 1), 512 * tg:512 * (tg + 1)]
                dvw = dv.n

                def _st_s(e, d0=d0, d1=d1, sti=sti, dvw=dvw):
                    e.wait_ge(dv.h, dvw)
                    with e.If_eq(regs["sy_role"], 0):
                        e.dma_start(out=d0, in_=sti[:, 0:512]).then_inc(
                            dS.h, 16)
                    with e.Else():
                        e.dma_start(out=d1, in_=sti[:, 0:512]).then_inc(
                            dS.h, 16)
                    return None
                E("sync", _st_s, waits=(), inc=None)
                dS.n += 16
                st_dma_free[hc % 2] = dS.n
            ar_free_pe[tg % 2] = pe.n
        wo_dS = dS.n
        if debug:
            dma("sync", dS, sp_dbg, shs[0], waits=[(dS, wo_dS)])
        sbar_gp = pair_barrier(1, wo_dS)

        hqT = big[:, HQT_O:HQT_O + ICH * 1024]
        wgb = [big[:, WG_O + 4096 * i:WG_O + 4096 * (i + 1)] for i in range(2)]
        wub = [big[:, WU_O + 4096 * i:WU_O + 4096 * (i + 1)] for i in range(2)]
        wdb = big[:, WD_O:WD_O + 2816]
        gst = big[:, GST_O:GST_O + 512]

        wgb_free = [pe.n, pe.n]
        hq_free_pe = pe.n
        gst_free_dA = 0
        ffn_first = True
        h_prev_dv = dv.n
        hq_pending = None

        for tg in range(2):
            for ib in range(NIB):
                bi = ib % 2
                for hc in range(DCH):
                    ldg = dma("sync", dS,
                              wgb[bi][:, 512 * hc:512 * (hc + 1)].rearrange(
                                  "p (b i) -> p b i", i=128),
                              wg4[:, hc, 4 * ib:4 * (ib + 1), :],
                              waits=([(pe, wgb_free[bi]), (gp, wbar_gp)]
                                     if hc == 0 else ()))
                for hc in range(DCH):
                    ldu = dma("sync", dS,
                              wub[bi][:, 512 * hc:512 * (hc + 1)].rearrange(
                                  "p (b i) -> p b i", i=128),
                              wu4[:, hc, 4 * ib:4 * (ib + 1), :],
                              waits=())
                for tt in range(8):
                    t16 = 8 * tg + tt
                    for c in range(DCH):
                        E("tensor", lambda e, c=c, t16=t16, bi=bi: e.matmul(
                            P1[:, :],
                            xnqT[:, OWN * c + 128 * t16:
                                 OWN * c + 128 * (t16 + 1)],
                            wgb[bi][:, 512 * c:512 * (c + 1)],
                            start=(c == 0), stop=(c == DCH - 1)),
                          waits=(([(dS, ldg), (ac, ac.n)]
                                  + ([(dv, dv.n)] if ffn_first else []))
                                 if c == 0 else ()),
                          inc=(pe if c == DCH - 1 else None))
                    gate_pe = pe.n
                    for c in range(DCH):
                        E("tensor", lambda e, c=c, t16=t16, bi=bi: e.matmul(
                            P3[:, :],
                            xnqT[:, OWN * c + 128 * t16:
                                 OWN * c + 128 * (t16 + 1)],
                            wub[bi][:, 512 * c:512 * (c + 1)],
                            start=(c == 0), stop=(c == DCH - 1)),
                          waits=([(dS, ldu), (dv, h_prev_dv)]
                                 if c == 0 else ()),
                          inc=(pe if c == DCH - 1 else None))
                    up_pe = pe.n
                    ffn_first = False
                    if hq_pending is not None:
                        hq_pending()
                        hq_pending = None
                    E("scalar", lambda e: e.activation(
                        hsc[:, :], P1[:, :], AF.Silu),
                      waits=[(pe, gate_pe), (dv, dv.n)], inc=ac)
                    hb = hbuf2[(tt // 2) % 2]
                    hcol = 512 * (tt % 2)
                    E("vector", lambda e, hb=hb, hcol=hcol: e.tensor_tensor(
                        hb[:, hcol:hcol + 512], P3[:, :], hsc[:, :],
                        ALU.mult),
                      waits=[(ac, ac.n), (pe, up_pe)], inc=dv)
                    h_prev_dv = dv.n
                    if tt % 2 == 0:
                        continue
                    qdv = quant(hb, 1024, q_out, after=[(dv, dv.n)],
                                dst_waits=[(pe, qout_free_pe)])
                    nb4 = 4 if ib < NIB - 1 else 2

                    def _flush(qdv=qdv, nb4=nb4, ib=ib, tt=tt):
                        nonlocal qout_free_pe
                        mlist = list(range(nb4)) + [4 + m for m in range(nb4)]
                        first = True
                        for m in mlist:
                            w = ([(dv, qdv), (pe, hq_free_pe)]
                                 if first else ())
                            first = False
                            E("tensor", lambda e, m=m: e.transpose(
                                PT[:, 128 * m:128 * (m + 1)],
                                q_out[:, 128 * m:128 * (m + 1)], t_id[:, :]),
                              waits=w, inc=(pe if m == mlist[-1] else None))
                        tpe = pe.n
                        qout_free_pe = tpe
                        first = True
                        for m in mlist:
                            ic = 4 * ib + (m % 4)
                            tcol = tt - 1 + (m // 4)
                            E("vector", lambda e, ic=ic, m=m, tcol=tcol:
                              e.tensor_copy(
                                  hqT[:, 1024 * ic + 128 * tcol:
                                      1024 * ic + 128 * (tcol + 1)],
                                  PT[:, 128 * m:128 * (m + 1)]),
                              waits=([(pe, tpe)] if first else ()), inc=dv)
                            first = False
                    hq_pending = _flush
                wgb_free[bi] = pe.n
            if hq_pending is not None:
                hq_pending()
                hq_pending = None
            hq_dv = dv.n
            for hc in range(DCH):
                ldd = dma("sync", dS,
                          wdb.rearrange("p (c w) -> p c w", w=128),
                          wd4[hc, :, :, :],
                          waits=[(pe, pe.n)])
                for half in range(2):
                    for ic in range(ICH):
                        E("tensor", lambda e, ic=ic, half=half: e.matmul(
                            P2[:, :],
                            wdb[:, 128 * ic:128 * (ic + 1)],
                            hqT[:, 1024 * ic + 512 * half:
                                1024 * ic + 512 * (half + 1)],
                            start=(ic == 0), stop=(ic == ICH - 1)),
                          waits=([(dS, ldd), (dv, hq_dv), (ac, ac.n)]
                                 if ic == 0 else ()),
                          inc=(pe if ic == ICH - 1 else None))
                    E("scalar", lambda e: e.activation(
                        gst[:, :], P2[:, :], AF.Silu),
                      waits=[(pe, pe.n), (dA, gst_free_dA)], inc=ac)
                    gst_free_dA = dma(
                        "scalar", dA,
                        gT_d[hc, :, 1024 * tg + 512 * half:
                             1024 * tg + 512 * (half + 1)],
                        gst[:, :], waits=[(ac, ac.n)])
            hq_free_pe = pe.n
        ffn_dA = dA.n

        selA = t_sel[:, 0:1]
        selB = t_sel[:, 1:2]
        shs3 = [shs[p].rearrange("(c p2) w -> c p2 w", p2=128)
                for p in range(2)]
        fb = [[big[:, 1024 * (6 * i + j):1024 * (6 * i + j + 1)]
               for j in range(6)] for i in range(2)]
        fb_free_dv = [0, 0]
        tx_free_dv = [dv.n, dv.n]
        fo_free_dS = 0
        ld_at = {}
        for t in range(NTO + 1):
            if t < NTO:
                par = t % 2
                w0 = [(gp, sbar_gp), (dv, fb_free_dv[par]), (dA, ffn_dA)]
                if t == 0:
                    w0.append((pe, pe.n))
                l1 = dma("scalar", dA,
                         fb[par][0].rearrange("p (c w) -> p c w", w=128),
                         shs3[0][:, :, 128 * t:128 * (t + 1)].rearrange(
                             "c p w -> p c w"), waits=w0)
                l2 = dma("scalar", dA,
                         fb[par][1].rearrange("p (c w) -> p c w", w=128),
                         shs3[1][:, :, 128 * t:128 * (t + 1)].rearrange(
                             "c p w -> p c w"), waits=())
                l3 = dma("scalar", dA,
                         fb[par][2].rearrange("p (c w) -> p c w", w=128),
                         shs3[0][:, :, OWN + 128 * t:OWN + 128 * (t + 1)
                                 ].rearrange("c p w -> p c w"), waits=())
                l4 = dma("scalar", dA,
                         fb[par][3].rearrange("p (c w) -> p c w", w=128),
                         shs3[1][:, :, OWN + 128 * t:OWN + 128 * (t + 1)
                                 ].rearrange("c p w -> p c w"), waits=())
                l5 = dma("scalar", dA,
                         fb[par][4].rearrange("p (c w) -> p c w", w=128),
                         gT_d[:, :, 128 * t:128 * (t + 1)].rearrange(
                             "c p w -> p c w"), waits=())
                l6 = dma("scalar", dA,
                         fb[par][5].rearrange("p (c w) -> p c w", w=128),
                         xnTo_d[:, :, 128 * t:128 * (t + 1)].rearrange(
                             "c p w -> p c w"), waits=())
                lx = dma("sync", dS, t_x[par][:, :],
                         x_own[128 * t:128 * (t + 1), :],
                         waits=[(dv, tx_free_dv[par])])
                ld_at[t] = (l6, lx)
            if t == 0:
                continue
            tc = t - 1
            par = tc % 2
            l6, lx = ld_at[tc]
            E("vector", lambda e, par=par: e.tensor_tensor(
                f_t[:, :], fb[par][2][:, :], fb[par][3][:, :], ALU.add),
              waits=[(dA, l6), (dv, dv.n)], inc=dv)
            E("vector", lambda e: e.tensor_scalar(
                f_s[:, :], f_t[:, :], selB, None, ALU.mult),
              waits=[(dv, dv.n)], inc=dv)
            E("vector", lambda e, par=par: e.tensor_tensor(
                f_t[:, :], fb[par][0][:, :], fb[par][1][:, :], ALU.add),
              waits=[(dv, dv.n)], inc=dv)
            E("vector", lambda e: e.scalar_tensor_tensor(
                f_s2[:, :], f_t[:, :], selA, f_s[:, :], ALU.mult, ALU.add),
              waits=[(dv, dv.n)], inc=dv)
            E("vector", lambda e, par=par: e.tensor_tensor(
                f_t[:, :], f_s2[:, :], fb[par][5][:, :], ALU.add),
              waits=[(dv, dv.n)], inc=dv)
            E("vector", lambda e, par=par: e.tensor_tensor(
                f_m[:, :], f_t[:, :], fb[par][4][:, :], ALU.mult),
              waits=[(dv, dv.n)], inc=dv)
            fb_free_dv[par] = dv.n
            tpe = transpose8(f_m, 8, waits=[(dv, dv.n)])
            E("vector", lambda e, par=par: e.tensor_tensor(
                f_o[:, :], PT[:, :], t_x[par][:, :], ALU.add),
              waits=[(pe, tpe), (dS, lx), (dS, fo_free_dS)], inc=dv)
            tx_free_dv[par] = dv.n
            fo_free_dS = dma("sync", dS,
                             out_own[128 * tc:128 * (tc + 1), :],
                             f_o[:, :], waits=[(dv, dv.n)])
        E("sync", lambda e: e.nop(), waits=[(dS, dS.n)], inc=None)

        with nc.Block() as block:
            def runner(engname):
                def _(e):
                    for waits, fn, inc, k in sch.steps[engname]:
                        for (s, v) in waits:
                            e.wait_ge(s.h, v)
                        instr = fn(e)
                        if inc is not None:
                            assert instr is not None, engname
                            instr.then_inc(inc.h, k)
                return _
            block.tensor(runner("tensor"))
            block.vector(runner("vector"))
            block.scalar(runner("scalar"))
            block.gpsimd(runner("gpsimd"))
            block.sync(runner("sync"))

    return nc


_cache = {}


def _get_nc():
    if "nc" not in _cache:
        _cache["nc"] = build_program(DEBUG)
    return _cache["nc"]


def kernel(L0_attn_x, L0_attn_Wq, L0_attn_Wk, L0_attn_Wv, L0_attn_Wo,
           L0_ffn_w_gate, L0_ffn_w_up, L0_ffn_w_down):
    x = np.ascontiguousarray(np.asarray(L0_attn_x, dtype=np.float32))
    Wq = np.ascontiguousarray(np.asarray(L0_attn_Wq, dtype=np.float32))
    Wk = np.ascontiguousarray(np.asarray(L0_attn_Wk, dtype=np.float32))
    Wv = np.ascontiguousarray(np.asarray(L0_attn_Wv, dtype=np.float32))
    Wo = np.ascontiguousarray(np.asarray(L0_attn_Wo, dtype=np.float32))
    wg = np.ascontiguousarray(np.asarray(L0_ffn_w_gate, dtype=np.float32))
    wu = np.ascontiguousarray(np.asarray(L0_ffn_w_up, dtype=np.float32))
    wd = np.ascontiguousarray(np.asarray(L0_ffn_w_down, dtype=np.float32))

    wg_p = np.zeros((IPAD, D), dtype=np.float32)
    wg_p[:IFF] = wg
    wu_p = np.zeros((IPAD, D), dtype=np.float32)
    wu_p[:IFF] = wu

    ident = np.eye(128, dtype=np.float32).astype(ml_dtypes.bfloat16)
    r_idx = np.arange(128)
    maskt = np.where(r_idx[:, None] <= r_idx[None, :], 0.0, NEG).astype(
        np.float32)

    epoch = int(np.random.randint(1, 2**30))
    in_maps = []
    for core in range(8):
        b = core // 2
        r = core % 2
        selv = np.zeros((128, 2), dtype=np.float32)
        selv[:, r] = 1.0
        in_maps.append({
            "xb": x[b],
            "x_own": x[b, r * OWN:(r + 1) * OWN],
            "wq_in": Wq[:, 512 * r:512 * (r + 1)],
            "wk_in": Wk[:, 512 * r:512 * (r + 1)],
            "wv_in": Wv[:, 512 * r:512 * (r + 1)],
            "wo_in": Wo[512 * r:512 * (r + 1), :],
            "wg_sh": wg_p[1536 * r:1536 * (r + 1)],
            "wu_sh": wu_p[1536 * r:1536 * (r + 1)],
            "wd_sh": wd[512 * r:512 * (r + 1)],
            "ident": ident,
            "maskt": maskt,
            "selv": selv,
            "rolei": np.array([[r, 0]], dtype=np.int32),
            "epochi": np.array([[epoch, 0]], dtype=np.int32),
        })

    nc = _get_nc()
    res = run_bass_kernel_spmd(nc, in_maps, core_ids=list(range(8)))
    out = np.empty((4, S, D), dtype=np.float32)
    for core in range(8):
        b = core // 2
        r = core % 2
        out[b, r * OWN:(r + 1) * OWN] = res.results[core]["out_own"]
    return out


# version 13
# speedup vs baseline: 1.3158x; 1.0260x over previous
import numpy as np
import ml_dtypes
from contextlib import ExitStack

import concourse.bass as bass
import concourse.mybir as mybir
from concourse.bass_utils import run_bass_kernel_spmd

F32 = mybir.dt.float32
BF16 = mybir.dt.bfloat16
I16 = mybir.dt.int16
ALU = mybir.AluOpType
AF = mybir.ActivationFunctionType
AX = mybir.AxisListType

DEBUG = False

D = 1024
DCH = 8
S = 4096
OWN = 2048
NT = S // 128
NTO = OWN // 128
HPC = 8
NHP = HPC // 2
NQG = S // 512
DK = 64
IFF = 2816
IPAD = 3072
ICH = 22
NIB = IPAD // 512
NEG = -1.0e30

KT_O, QT_O, VH_O = 0, 4096, 8192
XR_O = 12352
PTND_O = 20544
PTDG_O = 24640
WQ_O, WK_O, WV_O, WO_O = 26688, 30784, 34880, 38976
BIG_N = 43072
HQT_O = 0
WG_O = 22528
WU_O = 30720
WD_O = 38912
GST_O = 41728
AR_O = 0


class Sem:
    def __init__(self, h):
        self.h = h
        self.n = 0


class Sched:

    ENGS = ("tensor", "vector", "scalar", "gpsimd", "sync")

    def __init__(self):
        self.steps = {e: [] for e in self.ENGS}

    def emit(self, eng, fn, waits=(), inc=None, inc_by=1):
        w = [(s, v) for (s, v) in waits if v > 0]
        self.steps[eng].append((w, fn, inc, inc_by))
        if inc is not None:
            inc.n += inc_by
        return inc.n if inc is not None else 0


def build_program(debug=False):
    nc = bass.Bass("TRN2", target_bir_lowering=False)

    xb = nc.dram_tensor("xb", [S, D], F32, kind="ExternalInput").ap()
    x_own = nc.dram_tensor("x_own", [OWN, D], F32, kind="ExternalInput").ap()
    wq_in = nc.dram_tensor("wq_in", [D, 512], F32, kind="ExternalInput").ap()
    wk_in = nc.dram_tensor("wk_in", [D, 512], F32, kind="ExternalInput").ap()
    wv_in = nc.dram_tensor("wv_in", [D, 512], F32, kind="ExternalInput").ap()
    wo_in = nc.dram_tensor("wo_in", [512, D], F32, kind="ExternalInput").ap()
    wg_sh = nc.dram_tensor("wg_sh", [12 * 128, D], F32,
                           kind="ExternalInput").ap()
    wu_sh = nc.dram_tensor("wu_sh", [12 * 128, D], F32,
                           kind="ExternalInput").ap()
    wd_sh = nc.dram_tensor("wd_sh", [4 * 128, IFF], F32,
                           kind="ExternalInput").ap()
    rolei = nc.dram_tensor("rolei", [1, 2], mybir.dt.int32,
                           kind="ExternalInput").ap()
    epochi = nc.dram_tensor("epochi", [1, 2], mybir.dt.int32,
                            kind="ExternalInput").ap()
    ident = nc.dram_tensor("ident", [128, 128], BF16, kind="ExternalInput").ap()
    maskt = nc.dram_tensor("maskt", [128, 128], F32, kind="ExternalInput").ap()
    selv = nc.dram_tensor("selv", [128, 2], F32, kind="ExternalInput").ap()
    out_own = nc.dram_tensor("out_own", [OWN, D], F32, kind="ExternalOutput").ap()

    dkind = "ExternalOutput" if debug else "Internal"
    xnT_d = nc.dram_tensor("xnT_d", [DCH, 128, S], BF16, kind=dkind).ap()
    xnTo_d = nc.dram_tensor("xnTo_d", [DCH, 128, OWN], BF16, kind=dkind).ap()
    attnT_d = nc.dram_tensor("attnT_d", [NHP, 128, S], BF16, kind=dkind).ap()
    gT_d = nc.dram_tensor("gT_d", [DCH, 128, OWN], BF16, kind=dkind).ap()
    shwg = nc.dram_tensor("shwg", [24 * D, 128], BF16,
                          addr_space="Shared").ap()
    shwu = nc.dram_tensor("shwu", [24 * D, 128], BF16,
                          addr_space="Shared").ap()
    shwd = nc.dram_tensor("shwd", [8 * IFF, 128], BF16,
                          addr_space="Shared").ap()
    shs = nc.dram_tensor("shs", [2, DCH * 128, S], BF16,
                         addr_space="Shared").ap()
    shflag = nc.dram_tensor("shflag", [2, 2, 16], mybir.dt.int32,
                            addr_space="Shared").ap()
    if debug:
        xnq_dbg = nc.dram_tensor("xnq_dbg", [DCH, 128, OWN], BF16,
                                 kind="ExternalOutput").ap()
        sp_dbg = nc.dram_tensor("sp_dbg", [DCH * 128, S], BF16,
                                kind="ExternalOutput").ap()
        kqv_dbg = nc.dram_tensor("kqv_dbg", [3, 128, 4160], BF16,
                                 kind="ExternalOutput").ap()
        p_dbg = nc.dram_tensor("p_dbg", [6, 128, 2048], BF16,
                               kind="ExternalOutput").ap()
        av_dbg = nc.dram_tensor("av_dbg", [4, 128, 512], F32,
                                kind="ExternalOutput").ap()

    sch = Sched()
    es = ExitStack()

    def sb(name, shape, dtype):
        return es.enter_context(nc.sbuf_tensor(name, shape, dtype))

    def ps(name, shape, dtype):
        return es.enter_context(nc.psum_tensor(name, shape, dtype))

    def sem(name):
        return Sem(es.enter_context(nc.semaphore(name)))

    with es:
        big = sb("big", [128, BIG_N], BF16)
        xnqT = sb("xnqT", [128, DCH * OWN], BF16)
        t_x = [sb(f"t_x{i}", [128, D], F32) for i in range(2)]
        t_xn = sb("t_xn", [128, D], F32)
        t_xnb = sb("t_xnb", [128, D], BF16)
        bnb = sb("bnb", [128, 12], F32)
        mv = sb("mv", [128, 2], F32)
        ln_s = sb("ln_s", [128, 4], F32)
        lnsv = sb("lnsv", [128, 2 * NTO], F32)
        q_xn = sb("q_xn", [128, 1024], F32)
        q_a = sb("q_a", [128, 1024], F32)
        q_i1 = sb("q_i1", [128, 1024], I16)
        q_i2 = sb("q_i2", [128, 1024], I16)
        q_i3 = sb("q_i3", [128, 1024], I16)
        q_t1 = sb("q_t1", [128, 1024], BF16)
        q_t2 = sb("q_t2", [128, 1024], BF16)
        q_sg = sb("q_sg", [128, 1024], BF16)
        q_out = sb("q_out", [128, 1024], BF16)
        q_bm = sb("q_bm", [128, 64], F32)
        q_S = sb("q_S", [128, 64], F32)
        q_R = sb("q_R", [128, 64], F32)
        q_Sb = sb("q_Sb", [128, 64], BF16)
        st = [sb(f"st{i}", [128, 1024], BF16) for i in range(2)]
        t_mask = sb("t_mask", [128, 128], F32)
        t_id = sb("t_id", [128, 128], BF16)
        t_sel = sb("t_sel", [128, 2], F32)
        rden = [sb(f"rden{i}", [128, 512], F32) for i in range(2)]
        rb64 = [sb(f"rb64{i}", [128, 512], F32) for i in range(2)]
        outst = [sb(f"outst{i}", [128, 512], BF16) for i in range(2)]
        t_ones = sb("t_ones", [128, 64], F32)
        t_ri = sb("t_ri", [1, 2], mybir.dt.int32)
        t_ep = sb("t_ep", [1, 16], mybir.dt.int32)
        t_zz = sb("t_zz", [1, 16], mybir.dt.int32)
        regs = {}
        t_wf = None
        hsc = sb("hsc", [128, 512], F32)
        hbuf = [sb(f"hbuf{i}", [128, 512], F32) for i in range(2)]
        hbuf2 = [sb(f"hbuf2{i}", [128, 1024], F32) for i in range(2)]
        f_t = sb("f_t", [128, 1024], F32)
        f_s = sb("f_s", [128, 1024], F32)
        f_s2 = sb("f_s2", [128, 1024], F32)
        f_m = sb("f_m", [128, 1024], BF16)
        f_o = sb("f_o", [128, 1024], F32)

        P0 = ps("P0", [128, 2048], F32)
        P1 = ps("P1", [128, 512], F32)
        P2 = ps("P2", [128, 512], F32)
        P3 = ps("P3", [128, 512], F32)
        PT = ps("PT", [128, 1024], BF16)

        dS = sem("dS")
        dA = sem("dA")
        dP = sem("dP")
        pe = sem("pe")
        dv = sem("dv")
        ac = sem("ac")
        gp = sem("gp")

        E = sch.emit

        def dma(q, qs, out, in_, waits=()):
            return E(q, lambda e, o=out, i=in_: e.dma_start(out=o, in_=i),
                     waits=waits, inc=qs, inc_by=16)

        def quant(src_f32, n, dst_bf16, after=(), dst_waits=()):
            for _ in quant_gen(src_f32, n, dst_bf16, after, dst_waits):
                pass
            return dv.n

        def quant_gen(src_f32, n, dst_bf16, after=(), dst_waits=()):
            assert n % 32 == 0
            h = n // 2
            g = h // 16

            co = [0, h]
            go = [0, g]

            def stp(fn, waits=()):
                E("vector", fn, waits=waits, inc=dv)

            w0 = list(after)
            for i in range(2):
                c, gg = co[i], go[i]
                x3 = src_f32[:, c:c + h].rearrange("p (g w) -> p g w", w=16)
                stp(lambda e, x3=x3, gg=gg: e.tensor_reduce(
                    q_bm[:, gg:gg + g], x3, AX.X, ALU.max,
                    apply_absolute_value=True), waits=w0)
                w0 = ()
            yield
            for i in range(2):
                gg = go[i]
                stp(lambda e, gg=gg: e.tensor_scalar(
                    q_S[:, gg:gg + g], q_bm[:, gg:gg + g], 1.0 / 6.0, 1e-12,
                    ALU.mult, ALU.max))
            yield
            for i in range(2):
                gg = go[i]
                stp(lambda e, gg=gg: e.reciprocal(
                    q_R[:, gg:gg + g], q_S[:, gg:gg + g]))
            yield
            for i in range(2):
                gg = go[i]
                stp(lambda e, gg=gg: e.tensor_copy(
                    q_Sb[:, gg:gg + g], q_S[:, gg:gg + g]))
            yield
            for i in range(2):
                c, gg = co[i], go[i]
                x3 = src_f32[:, c:c + h].rearrange("p (g w) -> p g w", w=16)
                rb = q_R[:, gg:gg + g].unsqueeze(2).broadcast_to((128, g, 16))
                o3 = q_xn[:, c:c + h].rearrange("p (g w) -> p g w", w=16)
                stp(lambda e, x3=x3, rb=rb, o3=o3: e.tensor_tensor(
                    o3, x3, rb, ALU.mult))
            yield
            for i in range(2):
                c = co[i]
                stp(lambda e, c=c: e.tensor_scalar(
                    q_sg[:, c:c + h], q_xn[:, c:c + h], 0.0, 2.0,
                    ALU.is_gt, ALU.mult))
            yield
            for i in range(2):
                c = co[i]
                stp(lambda e, c=c: e.scalar_tensor_tensor(
                    q_a[:, c:c + h], q_sg[:, c:c + h], -1.0,
                    q_xn[:, c:c + h], ALU.add, ALU.mult))
            yield
            for i in range(2):
                c = co[i]
                stp(lambda e, c=c: e.tensor_scalar(
                    q_xn[:, c:c + h], q_a[:, c:c + h], 2.0, 4.0,
                    ALU.mult, ALU.min))
            yield
            for i in range(2):
                c = co[i]
                stp(lambda e, c=c: e.tensor_copy(
                    q_i1[:, c:c + h], q_xn[:, c:c + h]))
            yield
            for i in range(2):
                c = co[i]
                stp(lambda e, c=c: e.tensor_scalar(
                    q_xn[:, c:c + h], q_a[:, c:c + h], 2.0, 4.0,
                    ALU.max, ALU.min))
            yield
            for i in range(2):
                c = co[i]
                stp(lambda e, c=c: e.tensor_copy(
                    q_i2[:, c:c + h], q_xn[:, c:c + h]))
            yield
            for i in range(2):
                c = co[i]
                stp(lambda e, c=c: e.tensor_scalar(
                    q_xn[:, c:c + h], q_a[:, c:c + h], 0.5, 2.0,
                    ALU.mult, ALU.max))
            yield
            for i in range(2):
                c = co[i]
                stp(lambda e, c=c: e.tensor_copy(
                    q_i3[:, c:c + h], q_xn[:, c:c + h]))
            yield
            for i in range(2):
                c = co[i]
                stp(lambda e, c=c: e.scalar_tensor_tensor(
                    q_t1[:, c:c + h], q_i1[:, c:c + h], 0.5, q_i2[:, c:c + h],
                    ALU.mult, ALU.add))
            yield
            for i in range(2):
                c = co[i]
                stp(lambda e, c=c: e.scalar_tensor_tensor(
                    q_t2[:, c:c + h], q_i3[:, c:c + h], 2.0, q_t1[:, c:c + h],
                    ALU.mult, ALU.add))
            yield
            for i in range(2):
                c = co[i]
                stp(lambda e, c=c: e.tensor_scalar(
                    q_t2[:, c:c + h], q_t2[:, c:c + h], -6.0, None, ALU.add))
            yield
            for i in range(2):
                c, gg = co[i], go[i]
                sbb = q_Sb[:, gg:gg + g].unsqueeze(2).broadcast_to((128, g, 16))
                t23 = q_t2[:, c:c + h].rearrange("p (g w) -> p g w", w=16)
                o3 = q_t1[:, c:c + h].rearrange("p (g w) -> p g w", w=16)
                stp(lambda e, t23=t23, sbb=sbb, o3=o3: e.tensor_tensor(
                    o3, t23, sbb, ALU.mult))
            yield
            for i in range(2):
                c = co[i]
                stp(lambda e, c=c: e.scalar_tensor_tensor(
                    dst_bf16[:, c:c + h], q_sg[:, c:c + h], -1.0,
                    q_t1[:, c:c + h], ALU.add, ALU.mult),
                    waits=(list(dst_waits) if i == 0 else ()))
            yield

        def transpose8(src_bf16, nblk, pt_off=0, waits=()):
            for m in range(nblk):
                w = waits if m == 0 else ()
                E("tensor", lambda e, m=m: e.transpose(
                    PT[:, pt_off + 128 * m:pt_off + 128 * (m + 1)],
                    src_bf16[:, 128 * m:128 * (m + 1)], t_id[:, :]),
                    waits=w, inc=(pe if m == nblk - 1 else None))
            return pe.n

        dma("sync", dS, t_id[:, :], ident)
        dma("sync", dS, t_mask[:, :], maskt)
        dma("sync", dS, t_sel[:, :], selv)
        dma("sync", dS, t_ri[0:1, :], rolei)
        dma("sync", dS, t_ep[0:1, 0:2], epochi)
        a_small = dS.n

        def _init_regs_sync(e):
            regs["sy_role"] = e.alloc_register("sy_role")
            e.wait_ge(dS.h, a_small)
            return e.load(regs["sy_role"], t_ri[0:1, 0:1])
        E("sync", _init_regs_sync, waits=(), inc=None)

        def _init_regs_gp(e):
            e.memset(t_zz[0:1, :], 0)
            regs["gp_role"] = e.alloc_register("gp_role")
            regs["gp_f"] = e.alloc_register("gp_f")
            regs["gp_c"] = e.alloc_register("gp_c")
            regs["gp_k"] = e.alloc_register("gp_k")
            e.wait_ge(dS.h, a_small)
            e.load(regs["gp_role"], t_ri[0:1, 0:1])
            e.load(regs["gp_k"], t_ep[0:1, 0:1])
            with e.If_eq(regs["gp_role"], 0):
                e.dma_start(out=shflag[0, 0, 0:2],
                            in_=t_zz[0:1, 0:2]).then_inc(dP.h, 16)
                e.dma_start(out=shflag[0, 1, 0:2],
                            in_=t_zz[0:1, 0:2]).then_inc(dP.h, 16)
            with e.Else():
                e.dma_start(out=shflag[1, 0, 0:2],
                            in_=t_zz[0:1, 0:2]).then_inc(dP.h, 16)
                e.dma_start(out=shflag[1, 1, 0:2],
                            in_=t_zz[0:1, 0:2]).then_inc(dP.h, 16)
            return e.wait_ge(dP.h, 32)
        E("gpsimd", _init_regs_gp, waits=(), inc=None)
        dP.n += 32

        def _barrier_body(e, bar, after_dS, dp_wait):
            e.wait_ge(dS.h, after_dS)
            r = regs["gp_role"]
            f, c, k = regs["gp_f"], regs["gp_c"], regs["gp_k"]
            for me in (0, 1):
                other = 1 - me
                with (e.If_eq(r, 0) if me == 0 else e.Else()):
                    e.dma_start(out=shflag[me, bar, 0:2],
                                in_=t_ep[0:1, 0:2]).then_inc(dP.h, 16)
                    e.wait_ge(dP.h, dp_wait)
                    e.load(f, shflag[other, bar:bar + 1, 0:1])
                    e.reg_alu(c, f, k, ALU.not_equal)
                    with e.While(c):
                        e.load(f, shflag[other, bar:bar + 1, 0:1])
                        e.reg_alu(c, f, k, ALU.not_equal)
            return e.nop()

        def pair_barrier(bar, after_dS):
            dp_wait = dP.n + 16
            E("gpsimd", lambda e, bar=bar, a=after_dS, w=dp_wait:
              _barrier_body(e, bar, a, w), waits=(), inc=gp)
            dP.n += 16
            return gp.n

        for (w_in, w_off, scale, ncol) in [
                (wq_in, WQ_O, 0.125, 512), (wk_in, WK_O, 1.0, 512),
                (wv_in, WV_O, 1.0, 512), (wo_in, WO_O, 1.0, 1024)]:
            nch = w_in.shape[0] // 128
            for pc in range(nch):
                wv_ = dma("sync", dS, t_x[0][:, :ncol],
                          w_in[128 * pc:128 * (pc + 1), :],
                          waits=[(ac, ac.n)])
                E("scalar", lambda e, w_off=w_off, pc=pc, scale=scale,
                  ncol=ncol: e.activation(
                      big[:, w_off + ncol * pc:w_off + ncol * (pc + 1)],
                      t_x[0][:, :ncol], AF.Copy, scale=scale),
                  waits=[(dS, wv_)], inc=ac)
        aw_done_ac = ac.n

        st_dma_free = [0, 0]
        qout_free_pe = 0
        land_free_dv = [0, 0]
        ln_done_ac_ref = [0]

        def prep_piece_gen(src_ap, cn, dst0_ap, dst1_ap, par):
            nonlocal qout_free_pe
            land = t_x[par % 2][:, :cn]
            ldw = dma("gpsimd", dP, land, src_ap,
                      waits=[(dv, land_free_dv[par % 2]),
                             (ac, ln_done_ac_ref[0])])
            yield
            yield from quant_gen(land, cn, q_out, after=[(dP, ldw)],
                                 dst_waits=[(pe, qout_free_pe)])
            qdv = dv.n
            land_free_dv[par % 2] = qdv
            nb8 = cn // 128
            tpe = transpose8(q_out, nb8,
                             waits=[(dv, qdv), (ac, aw_done_ac)])
            qout_free_pe = tpe
            sti = st[par % 2]
            E("vector", lambda e, sti=sti, nb8=nb8: e.tensor_copy(
                sti[:, :128 * nb8], PT[:, :128 * nb8]),
              waits=[(pe, tpe), (dS, st_dma_free[par % 2])], inc=dv)
            src_st = sti[:, :128 * nb8].rearrange("p (m w) -> p m w", w=128)
            d0 = dst0_ap.rearrange("(m p) w -> p m w", p=128)
            d1 = dst1_ap.rearrange("(m p) w -> p m w", p=128)
            dvw = dv.n

            def _branched_store(e, d0=d0, d1=d1, src_st=src_st, dvw=dvw):
                e.wait_ge(dv.h, dvw)
                with e.If_eq(regs["sy_role"], 0):
                    e.dma_start(out=d0, in_=src_st).then_inc(dS.h, 16)
                with e.Else():
                    e.dma_start(out=d1, in_=src_st).then_inc(dS.h, 16)
                return None
            E("sync", _branched_store, waits=(), inc=None)
            dS.n += 16
            st_dma_free[par % 2] = dS.n
            yield

        def xnq_tile_gen(t, par):
            nonlocal qout_free_pe
            land = t_x[par % 2][:, :]
            ldw = dma("gpsimd", dP, land,
                      x_own[128 * t:128 * (t + 1), :],
                      waits=[(dv, land_free_dv[par % 2]),
                             (ac, ln_done_ac_ref[0])])
            E("vector", lambda e, t=t: e.tensor_scalar(
                t_xn[:, :], land[:, :], lnsv[:, 2 * t:2 * t + 1],
                lnsv[:, 2 * t + 1:2 * t + 2], ALU.mult, ALU.add),
              waits=[(dP, ldw), (dv, dv.n)], inc=dv)
            land_free_dv[par % 2] = dv.n
            yield
            yield from quant_gen(t_xn, D, q_out, after=[(dv, dv.n)],
                                 dst_waits=[(pe, qout_free_pe)])
            tpe = transpose8(q_out, 8, waits=[(dv, dv.n)])
            qout_free_pe = tpe
            for c in range(DCH):
                E("vector", lambda e, c=c, t=t: e.tensor_copy(
                    xnqT[:, OWN * c + 128 * t:OWN * c + 128 * (t + 1)],
                    PT[:, 128 * c:128 * (c + 1)]),
                  waits=([(pe, tpe)] if c == 0 else ()), inc=dv)
            if debug:
                dstq = xnq_dbg[:, :, 128 * t:128 * (t + 1)].rearrange(
                    "c p w -> p c w")
                dma("sync", dS, dstq,
                    q_out[:, :].rearrange("p (c w) -> p c w", w=128),
                    waits=[(dv, dv.n)])
            yield

        deferred = []

        def build_deferred():
            par = 0
            for (src, sh_dst) in [(wg_sh, shwg), (wu_sh, shwu)]:
                for lb in range(12):
                    deferred.append(prep_piece_gen(
                        src[128 * lb:128 * (lb + 1), :], D,
                        sh_dst[D * lb:D * (lb + 1), :],
                        sh_dst[D * (12 + lb):D * (13 + lb), :], par))
                    par += 1
            for t in range(NTO):
                deferred.append(xnq_tile_gen(t, par))
                par += 1
            for lb in range(4):
                for pc in range(3):
                    c0 = 1024 * pc
                    cn = min(1024, IFF - c0)
                    deferred.append(prep_piece_gen(
                        wd_sh[128 * lb:128 * (lb + 1), c0:c0 + cn], cn,
                        shwd[IFF * lb + c0:IFF * lb + c0 + cn, :],
                        shwd[IFF * (4 + lb) + c0:IFF * (4 + lb) + c0 + cn, :],
                        par))
                    par += 1

        def pull(n):
            k = 0
            while deferred and k < n:
                try:
                    next(deferred[0])
                    k += 1
                except StopIteration:
                    deferred.pop(0)

        def pull_all():
            while deferred:
                pull(1000)

        wg4 = shwg.rearrange("(b c p) i -> p c b i", c=8, p=128)
        wu4 = shwu.rearrange("(b c p) i -> p c b i", c=8, p=128)
        wd4 = shwd.rearrange("(h ic p) w -> h p ic w", h=8, p=128)

        def ln_tile(xsrc, ti, store_dst, save_ti=None):
            land = t_x[ti % 2][:, :]
            ldw = dma("sync", dS, land, xsrc[128 * ti:128 * (ti + 1), :],
                      waits=[(dv, dv.n), (ac, ac.n)])
            E("vector", lambda e: e.bn_stats(bnb[:, 0:6], land[:, 0:512]),
              waits=[(dS, ldw)], inc=dv)
            E("vector", lambda e: e.bn_stats(bnb[:, 6:12], land[:, 512:1024]),
              inc=dv)
            E("vector", lambda e: e.bn_aggr(mv[:, :], bnb[:, :]),
              waits=[(dv, dv.n)], inc=dv)
            E("vector", lambda e: e.tensor_scalar(
                ln_s[:, 0:1], mv[:, 1:2], 1e-5, None, ALU.add),
              waits=[(dv, dv.n)], inc=dv)
            E("scalar", lambda e: e.activation(
                ln_s[:, 1:2], ln_s[:, 0:1], AF.Sqrt),
              waits=[(dv, dv.n)], inc=ac)
            E("vector", lambda e: e.reciprocal(ln_s[:, 2:3], ln_s[:, 1:2]),
              waits=[(ac, ac.n)], inc=dv)
            E("vector", lambda e: e.scalar_tensor_tensor(
                ln_s[:, 3:4], mv[:, 0:1], -1.0, ln_s[:, 2:3],
                ALU.mult, ALU.mult), waits=[(dv, dv.n)], inc=dv)
            if save_ti is not None:
                E("vector", lambda e, t=save_ti: e.tensor_copy(
                    lnsv[:, 2 * t:2 * t + 2], ln_s[:, 2:4]),
                  waits=[(dv, dv.n)], inc=dv)
            E("scalar", lambda e: e.activation(
                t_xnb[:, :], land[:, :], AF.Identity,
                scale=ln_s[:, 2:3], bias=ln_s[:, 3:4]),
              waits=[(dv, dv.n)], inc=ac)
            tpe = transpose8(t_xnb, 8, waits=[(ac, ac.n), (dv, dv.n)])
            sti = st[ti % 2]
            E("vector", lambda e, sti=sti: e.tensor_copy(sti[:, :], PT[:, :]),
              waits=[(pe, tpe), (dS, st_dma_free[ti % 2])], inc=dv)
            dst = store_dst[:, :, 128 * ti:128 * (ti + 1)].rearrange(
                "c p w -> p c w")
            st_dma_free[ti % 2] = dma(
                "sync", dS, dst,
                sti[:, :].rearrange("p (c w) -> p c w", w=128),
                waits=[(dv, dv.n)])
            return st_dma_free[ti % 2]

        ln_dma_at = []
        for ti in range(NT):
            ln_dma_at.append(ln_tile(xb, ti, xnT_d))
        for ti in range(NTO):
            ln_tile(x_own, ti, xnTo_d, save_ti=ti)
        ln_done_dS = dS.n
        ln_done_dv = dv.n
        ln_done_ac_ref[0] = ac.n
        land_free_dv[0] = dv.n
        land_free_dv[1] = dv.n
        build_deferred()

        kT = big[:, KT_O:KT_O + 4096]
        qT = big[:, QT_O:QT_O + 4096]
        vh = big[:, VH_O:VH_O + 4160]
        xr = [big[:, XR_O + 4096 * i:XR_O + 4096 * (i + 1)] for i in range(2)]
        pnd = [big[:, PTND_O + 2048 * i:PTND_O + 2048 * (i + 1)]
               for i in range(2)]
        pdg = [big[:, PTDG_O + 512 * j:PTDG_O + 512 * (j + 1)]
               for j in range(4)]

        vh3 = vh.rearrange("p (c w) -> p c w", w=65)
        E("vector", lambda e: e.memset(vh3[:, :, 64:65], 1.0),
          waits=[], inc=dv)
        E("vector", lambda e: e.memset(t_ones[0:1, :], 1.0), inc=dv)
        for j in range(4):
            E("vector", lambda e, j=j: e.memset(pdg[j][:, :], 0.0), inc=dv)
        attn_init_dv = dv.n

        xr_free_pe = [0, 0]
        pnd_free_pe = [0, 0]
        pdg_free_pe = [0, 0, 0, 0]
        p1b_free_dv = [0, 0]
        p0_free_ac = 0
        p2div_free_dv = 0
        outst_free_dS = [0, 0]
        rr_free_dv = [0, 0]
        dbg_gate_dv = 0
        div_pending = None
        ogi = 0

        for hp in range(NHP):
            kq_ready_ac = {}
            v_ready_dv = {}
            for tg in range(DCH):
                xbuf = xr[tg % 2]
                lw = [(pe, xr_free_pe[tg % 2]), (dS, ln_dma_at[4 * tg + 3])]
                src = xnT_d[:, :, 512 * tg:512 * (tg + 1)].rearrange(
                    "c p w -> p c w")
                ldx = dma("sync", dS,
                          xbuf.rearrange("p (c w) -> p c w", w=512),
                          src, waits=lw)
                for c in range(DCH):
                    E("tensor", lambda e, c=c, xbuf=xbuf, hp=hp: e.matmul(
                        P2[:, :],
                        big[:, WK_O + 512 * c + 128 * hp:
                             WK_O + 512 * c + 128 * (hp + 1)],
                        xbuf[:, 512 * c:512 * (c + 1)],
                        start=(c == 0), stop=(c == DCH - 1)),
                      waits=([(dS, ldx), (ac, ac.n), (dv, dv.n)]
                             if c == 0 else ()),
                      inc=(pe if c == DCH - 1 else None))
                E("scalar", lambda e, tg=tg: e.activation(
                    kT[:, 512 * tg:512 * (tg + 1)], P2[:, :], AF.Copy),
                  waits=[(pe, pe.n)], inc=ac)
                for c in range(DCH):
                    E("tensor", lambda e, c=c, xbuf=xbuf, hp=hp: e.matmul(
                        P2[:, :],
                        big[:, WQ_O + 512 * c + 128 * hp:
                             WQ_O + 512 * c + 128 * (hp + 1)],
                        xbuf[:, 512 * c:512 * (c + 1)],
                        start=(c == 0), stop=(c == DCH - 1)),
                      waits=([(ac, ac.n)] if c == 0 else ()),
                      inc=(pe if c == DCH - 1 else None))
                E("scalar", lambda e, tg=tg: e.activation(
                    qT[:, 512 * tg:512 * (tg + 1)], P2[:, :], AF.Copy),
                  waits=[(pe, pe.n)], inc=ac)
                kq_ready_ac[tg] = ac.n
                for tt in range(4):
                    for c in range(DCH):
                        E("tensor", lambda e, c=c, tt=tt, xbuf=xbuf, hp=hp:
                          e.matmul(
                              P2[:, 128 * tt:128 * (tt + 1)],
                              xbuf[:, 512 * c + 128 * tt:
                                   512 * c + 128 * (tt + 1)],
                              big[:, WV_O + 512 * c + 128 * hp:
                                   WV_O + 512 * c + 128 * (hp + 1)],
                              start=(c == 0), stop=(c == DCH - 1)),
                          waits=([(ac, kq_ready_ac[tg]),
                                  (dv, dv.n)] if (tt == 0 and c == 0) else ()),
                          inc=(pe if c == DCH - 1 else None))
                vpe = pe.n
                for tt in range(4):
                    c32 = 4 * tg + tt
                    for h2 in range(2):
                        E("vector", lambda e, tt=tt, h2=h2, c32=c32:
                          e.tensor_copy(
                              vh[:, 2080 * h2 + 65 * c32:
                                 2080 * h2 + 65 * c32 + 64],
                              P2[:, 128 * tt + 64 * h2:
                                 128 * tt + 64 * h2 + 64]),
                          waits=([(pe, vpe),
                                  (dv, attn_init_dv)] if tt == 0 and h2 == 0
                                 else ()), inc=dv)
                v_ready_dv[tg] = dv.n
                xr_free_pe[tg % 2] = pe.n
            proj_ac = ac.n
            proj_dv = dv.n
            if debug and hp == 0:
                dma("sync", dS, kqv_dbg[0, :, 0:4096], kT,
                    waits=[(ac, proj_ac), (dv, proj_dv)])
                dma("sync", dS, kqv_dbg[1, :, 0:4096], qT, waits=())
                dma("sync", dS, kqv_dbg[2, :, :], vh, waits=())

            for wk in range(16):
                E("tensor", lambda e, wk=wk: e.matmul(
                    P0[:, 512 * (wk % 4):512 * (wk % 4) + 512],
                    kT[0:64, 0:128], qT[0:64, 0:512],
                    start=True, stop=True),
                  waits=([(ac, p0_free_ac), (ac, proj_ac), (dv, proj_dv)]
                         if wk == 0 else ()),
                  inc=(pe if wk == 15 else None))

            for h2 in range(2):
                r0 = 64 * h2
                for g in range(NQG):
                    nb2 = 2 * g
                    PB = [P1, P3][ogi % 2]
                    exp_ac = {}
                    scores_pe = {}
                    for b in range(nb2 + 1):
                        is_diag = (b == nb2)
                        if not is_diag:
                            off = 1024 * (b % 2)
                            for j2 in range(2):
                                kc = 2 * b + j2
                                w = []
                                if j2 == 0:
                                    w = [(ac, exp_ac.get(b - 2, p0_free_ac)),
                                         (ac, proj_ac), (dv, proj_dv)]
                                E("tensor", lambda e, kc=kc, j2=j2, r0=r0,
                                  g=g, off=off: e.matmul(
                                      P0[:, off + 512 * j2:
                                         off + 512 * (j2 + 1)],
                                      kT[r0:r0 + 64,
                                         128 * kc:128 * (kc + 1)],
                                      qT[r0:r0 + 64, 512 * g:512 * (g + 1)],
                                      start=True, stop=True),
                                  waits=w, inc=(pe if j2 == 1 else None))
                            scores_pe[b] = pe.n
                        else:
                            for j in range(4):
                                kc = 4 * g + j
                                nw = 512 - 128 * j
                                w = []
                                if j == 0:
                                    w = [(ac, exp_ac.get(nb2 - 1,
                                                         p0_free_ac)),
                                         (ac, proj_ac), (dv, proj_dv)]
                                E("tensor", lambda e, kc=kc, j=j, nw=nw,
                                  r0=r0, g=g: e.matmul(
                                      P0[:, 512 * j:512 * j + nw],
                                      kT[r0:r0 + 64,
                                         128 * kc:128 * (kc + 1)],
                                      qT[r0:r0 + 64,
                                         512 * g + 128 * j:512 * (g + 1)],
                                      start=True, stop=True),
                                  waits=w, inc=pe)
                                scores_pe[(b, j)] = pe.n
                        if b >= 1:
                            bb = b - 1
                            pb = pnd[bb % 2]
                            for j2 in range(2):
                                kc = 2 * bb + j2
                                w = []
                                if j2 == 0:
                                    w = [(ac, exp_ac[bb])]
                                    if b == 1:
                                        w.append((dv, p1b_free_dv[ogi % 2]))
                                E("tensor", lambda e, kc=kc, j2=j2, h2=h2,
                                  b=b, pb=pb, PB=PB: e.matmul(
                                      PB[0:65, :],
                                      vh[:, 2080 * h2 + 65 * kc:
                                         2080 * h2 + 65 * kc + 65],
                                      pb[:, 512 * j2:512 * (j2 + 1)],
                                      start=(b == 1 and j2 == 0), stop=False),
                                  waits=w, inc=(pe if j2 == 1 else None))
                            pnd_free_pe[bb % 2] = pe.n
                        if not is_diag:
                            off = 1024 * (b % 2)
                            E("scalar", lambda e, b=b, off=off: e.activation(
                                pnd[b % 2][:, 0:1024],
                                P0[:, off:off + 1024], AF.Exp),
                              waits=[(pe, scores_pe[b]),
                                     (pe, pnd_free_pe[b % 2])], inc=ac)
                            exp_ac[b] = ac.n
                            pull(1)
                        else:
                            for j in range(4):
                                E("vector", lambda e, j=j: e.tensor_tensor(
                                    P0[:, 512 * j:512 * j + 128],
                                    P0[:, 512 * j:512 * j + 128],
                                    t_mask[:, :], ALU.add),
                                  waits=[(pe, scores_pe[(b, j)])], inc=dv)
                                mdv = dv.n
                                nw = 512 - 128 * j
                                E("scalar", lambda e, j=j, nw=nw:
                                  e.activation(
                                      pdg[j][:, 128 * j:128 * j + nw],
                                      P0[:, 512 * j:512 * j + nw], AF.Exp),
                                  waits=[(dv, mdv),
                                         (pe, pdg_free_pe[j])], inc=ac)
                                exp_ac[(b, j)] = ac.n
                    p0_free_ac = ac.n
                    for j in range(4):
                        kc = 4 * g + j
                        w = [(ac, exp_ac[(nb2, j)])]
                        if nb2 == 0 and j == 0:
                            w.append((dv, p1b_free_dv[ogi % 2]))
                        E("tensor", lambda e, kc=kc, j=j, h2=h2, g=g,
                          nb2=nb2, PB=PB: e.matmul(
                              PB[0:65, :],
                              vh[:, 2080 * h2 + 65 * kc:
                                 2080 * h2 + 65 * kc + 65],
                              pdg[j][:, :],
                              start=(nb2 == 0 and j == 0), stop=(j == 3)),
                          waits=w, inc=(pe if j == 3 else None))
                        pdg_free_pe[j] = pe.n
                    av_pe = pe.n
                    oi = ogi % 2
                    E("vector", lambda e, oi=oi, PB=PB: e.reciprocal(
                        rden[oi][0:1, :], PB[64:65, :]),
                      waits=[(pe, av_pe), (pe, rr_free_dv[oi])], inc=dv)
                    rdv = dv.n

                    def _divflush(oi=oi, PB=PB, rdv=rdv, hp=hp, r0=r0, g=g,
                                  u=ogi % 2):
                        nonlocal p2div_free_dv, rr_free_dv, outst_free_dS
                        E("tensor", lambda e, oi=oi: e.matmul(
                            P2[0:64, :], t_ones[0:1, 0:64],
                            rden[oi][0:1, :], start=True, stop=True),
                          waits=[(dv, max(rdv, p2div_free_dv))], inc=pe)
                        rr_free_dv[oi] = pe.n
                        E("vector", lambda e, oi=oi: e.tensor_copy(
                            rb64[oi][0:64, :], P2[0:64, :]),
                          waits=[(pe, pe.n)], inc=dv)
                        p2div_free_dv = dv.n
                        E("vector", lambda e, oi=oi, PB=PB: e.tensor_tensor(
                            outst[oi][0:64, :], PB[0:64, :],
                            rb64[oi][0:64, :], ALU.mult),
                          waits=[(dS, outst_free_dS[oi]), (dv, dv.n)],
                          inc=dv)
                        p1b_free_dv[u] = dv.n
                        od = dma("sync", dS,
                                 attnT_d[hp, r0:r0 + 64,
                                         512 * g:512 * (g + 1)],
                                 outst[oi][0:64, :], waits=[(dv, dv.n)])
                        outst_free_dS[oi] = od
                    if div_pending is not None:
                        div_pending()
                    div_pending = _divflush
                    ogi += 1
                    pull(8)

        if div_pending is not None:
            div_pending()
            div_pending = None
        pull_all()
        wprep_dS = dS.n
        wbar_gp = pair_barrier(0, wprep_dS)
        attn_done_dS = dS.n

        ar = [big[:, AR_O + 2048 * i:AR_O + 2048 * (i + 1)] for i in range(2)]
        ar_free_pe = [pe.n, pe.n]
        for tg in range(DCH):
            abuf = ar[tg % 2]
            src = attnT_d[:, :, 512 * tg:512 * (tg + 1)].rearrange(
                "c p w -> p c w")
            lda = dma("sync", dS, abuf.rearrange("p (c w) -> p c w", w=512),
                      src, waits=[(pe, ar_free_pe[tg % 2]),
                                  (dS, attn_done_dS)])
            for hc in range(DCH):
                for a in range(4):
                    E("tensor", lambda e, a=a, hc=hc, abuf=abuf: e.matmul(
                        P2[:, :],
                        big[:, WO_O + 1024 * a + 128 * hc:
                             WO_O + 1024 * a + 128 * (hc + 1)],
                        abuf[:, 512 * a:512 * (a + 1)],
                        start=(a == 0), stop=(a == 3)),
                      waits=([(dS, lda), (dv, dv.n)] if a == 0 else ()),
                      inc=(pe if a == 3 else None))
                sti = st[hc % 2]
                E("vector", lambda e, sti=sti: e.tensor_copy(
                    sti[:, 0:512], P2[:, :]),
                  waits=[(pe, pe.n), (dS, st_dma_free[hc % 2])], inc=dv)
                d0 = shs[0, 128 * hc:128 * (hc + 1), 512 * tg:512 * (tg + 1)]
                d1 = shs[1, 128 * hc:128 * (hc + 1), 512 * tg:512 * (tg + 1)]
                dvw = dv.n

                def _st_s(e, d0=d0, d1=d1, sti=sti, dvw=dvw):
                    e.wait_ge(dv.h, dvw)
                    with e.If_eq(regs["sy_role"], 0):
                        e.dma_start(out=d0, in_=sti[:, 0:512]).then_inc(
                            dS.h, 16)
                    with e.Else():
                        e.dma_start(out=d1, in_=sti[:, 0:512]).then_inc(
                            dS.h, 16)
                    return None
                E("sync", _st_s, waits=(), inc=None)
                dS.n += 16
                st_dma_free[hc % 2] = dS.n
            ar_free_pe[tg % 2] = pe.n
        wo_dS = dS.n
        if debug:
            dma("sync", dS, sp_dbg, shs[0], waits=[(dS, wo_dS)])
        sbar_gp = pair_barrier(1, wo_dS)

        hqT = big[:, HQT_O:HQT_O + ICH * 1024]
        wgb = [big[:, WG_O + 4096 * i:WG_O + 4096 * (i + 1)] for i in range(2)]
        wub = [big[:, WU_O + 4096 * i:WU_O + 4096 * (i + 1)] for i in range(2)]
        wdb = big[:, WD_O:WD_O + 2816]
        gst = big[:, GST_O:GST_O + 512]

        wgb_free = [pe.n, pe.n]
        hq_free_pe = pe.n
        gst_free_dA = 0
        ffn_first = True
        h_prev_dv = dv.n
        hq_pending = None

        for tg in range(2):
            for ib in range(NIB):
                bi = ib % 2
                for hc in range(DCH):
                    ldg = dma("sync", dS,
                              wgb[bi][:, 512 * hc:512 * (hc + 1)].rearrange(
                                  "p (b i) -> p b i", i=128),
                              wg4[:, hc, 4 * ib:4 * (ib + 1), :],
                              waits=([(pe, wgb_free[bi]), (gp, wbar_gp)]
                                     if hc == 0 else ()))
                for hc in range(DCH):
                    ldu = dma("sync", dS,
                              wub[bi][:, 512 * hc:512 * (hc + 1)].rearrange(
                                  "p (b i) -> p b i", i=128),
                              wu4[:, hc, 4 * ib:4 * (ib + 1), :],
                              waits=())
                for tt in range(8):
                    t16 = 8 * tg + tt
                    for c in range(DCH):
                        E("tensor", lambda e, c=c, t16=t16, bi=bi: e.matmul(
                            P1[:, :],
                            xnqT[:, OWN * c + 128 * t16:
                                 OWN * c + 128 * (t16 + 1)],
                            wgb[bi][:, 512 * c:512 * (c + 1)],
                            start=(c == 0), stop=(c == DCH - 1)),
                          waits=(([(dS, ldg), (ac, ac.n)]
                                  + ([(dv, dv.n)] if ffn_first else []))
                                 if c == 0 else ()),
                          inc=(pe if c == DCH - 1 else None))
                    gate_pe = pe.n
                    for c in range(DCH):
                        E("tensor", lambda e, c=c, t16=t16, bi=bi: e.matmul(
                            P3[:, :],
                            xnqT[:, OWN * c + 128 * t16:
                                 OWN * c + 128 * (t16 + 1)],
                            wub[bi][:, 512 * c:512 * (c + 1)],
                            start=(c == 0), stop=(c == DCH - 1)),
                          waits=([(dS, ldu), (dv, h_prev_dv)]
                                 if c == 0 else ()),
                          inc=(pe if c == DCH - 1 else None))
                    up_pe = pe.n
                    ffn_first = False
                    if hq_pending is not None:
                        hq_pending()
                        hq_pending = None
                    E("scalar", lambda e: e.activation(
                        hsc[:, :], P1[:, :], AF.Silu),
                      waits=[(pe, gate_pe), (dv, dv.n)], inc=ac)
                    hb = hbuf2[(tt // 2) % 2]
                    hcol = 512 * (tt % 2)
                    E("vector", lambda e, hb=hb, hcol=hcol: e.tensor_tensor(
                        hb[:, hcol:hcol + 512], P3[:, :], hsc[:, :],
                        ALU.mult),
                      waits=[(ac, ac.n), (pe, up_pe)], inc=dv)
                    h_prev_dv = dv.n
                    if tt % 2 == 0:
                        continue
                    qdv = quant(hb, 1024, q_out, after=[(dv, dv.n)],
                                dst_waits=[(pe, qout_free_pe)])
                    nb4 = 4 if ib < NIB - 1 else 2

                    def _flush(qdv=qdv, nb4=nb4, ib=ib, tt=tt):
                        nonlocal qout_free_pe
                        mlist = list(range(nb4)) + [4 + m for m in range(nb4)]
                        first = True
                        for m in mlist:
                            w = ([(dv, qdv), (pe, hq_free_pe)]
                                 if first else ())
                            first = False
                            E("tensor", lambda e, m=m: e.transpose(
                                PT[:, 128 * m:128 * (m + 1)],
                                q_out[:, 128 * m:128 * (m + 1)], t_id[:, :]),
                              waits=w, inc=(pe if m == mlist[-1] else None))
                        tpe = pe.n
                        qout_free_pe = tpe
                        first = True
                        for m in mlist:
                            ic = 4 * ib + (m % 4)
                            tcol = tt - 1 + (m // 4)
                            E("vector", lambda e, ic=ic, m=m, tcol=tcol:
                              e.tensor_copy(
                                  hqT[:, 1024 * ic + 128 * tcol:
                                      1024 * ic + 128 * (tcol + 1)],
                                  PT[:, 128 * m:128 * (m + 1)]),
                              waits=([(pe, tpe)] if first else ()), inc=dv)
                            first = False
                    hq_pending = _flush
                wgb_free[bi] = pe.n
            if hq_pending is not None:
                hq_pending()
                hq_pending = None
            hq_dv = dv.n
            for hc in range(DCH):
                ldd = dma("sync", dS,
                          wdb.rearrange("p (c w) -> p c w", w=128),
                          wd4[hc, :, :, :],
                          waits=[(pe, pe.n)])
                for half in range(2):
                    for ic in range(ICH):
                        E("tensor", lambda e, ic=ic, half=half: e.matmul(
                            P2[:, :],
                            wdb[:, 128 * ic:128 * (ic + 1)],
                            hqT[:, 1024 * ic + 512 * half:
                                1024 * ic + 512 * (half + 1)],
                            start=(ic == 0), stop=(ic == ICH - 1)),
                          waits=([(dS, ldd), (dv, hq_dv), (ac, ac.n)]
                                 if ic == 0 else ()),
                          inc=(pe if ic == ICH - 1 else None))
                    E("scalar", lambda e: e.activation(
                        gst[:, :], P2[:, :], AF.Silu),
                      waits=[(pe, pe.n), (dA, gst_free_dA)], inc=ac)
                    gst_free_dA = dma(
                        "scalar", dA,
                        gT_d[hc, :, 1024 * tg + 512 * half:
                             1024 * tg + 512 * (half + 1)],
                        gst[:, :], waits=[(ac, ac.n)])
            hq_free_pe = pe.n
        ffn_dA = dA.n

        selA = t_sel[:, 0:1]
        selB = t_sel[:, 1:2]
        shs3 = [shs[p].rearrange("(c p2) w -> c p2 w", p2=128)
                for p in range(2)]
        fb = [[big[:, 1024 * (6 * i + j):1024 * (6 * i + j + 1)]
               for j in range(6)] for i in range(2)]
        fb_free_dv = [0, 0]
        tx_free_dv = [dv.n, dv.n]
        fo_free_dS = 0
        ld_at = {}
        for t in range(NTO + 1):
            if t < NTO:
                par = t % 2
                w0 = [(gp, sbar_gp), (dv, fb_free_dv[par]), (dA, ffn_dA)]
                if t == 0:
                    w0.append((pe, pe.n))
                l1 = dma("scalar", dA,
                         fb[par][0].rearrange("p (c w) -> p c w", w=128),
                         shs3[0][:, :, 128 * t:128 * (t + 1)].rearrange(
                             "c p w -> p c w"), waits=w0)
                l2 = dma("scalar", dA,
                         fb[par][1].rearrange("p (c w) -> p c w", w=128),
                         shs3[1][:, :, 128 * t:128 * (t + 1)].rearrange(
                             "c p w -> p c w"), waits=())
                l3 = dma("scalar", dA,
                         fb[par][2].rearrange("p (c w) -> p c w", w=128),
                         shs3[0][:, :, OWN + 128 * t:OWN + 128 * (t + 1)
                                 ].rearrange("c p w -> p c w"), waits=())
                l4 = dma("scalar", dA,
                         fb[par][3].rearrange("p (c w) -> p c w", w=128),
                         shs3[1][:, :, OWN + 128 * t:OWN + 128 * (t + 1)
                                 ].rearrange("c p w -> p c w"), waits=())
                l5 = dma("scalar", dA,
                         fb[par][4].rearrange("p (c w) -> p c w", w=128),
                         gT_d[:, :, 128 * t:128 * (t + 1)].rearrange(
                             "c p w -> p c w"), waits=())
                l6 = dma("scalar", dA,
                         fb[par][5].rearrange("p (c w) -> p c w", w=128),
                         xnTo_d[:, :, 128 * t:128 * (t + 1)].rearrange(
                             "c p w -> p c w"), waits=())
                lx = dma("sync", dS, t_x[par][:, :],
                         x_own[128 * t:128 * (t + 1), :],
                         waits=[(dv, tx_free_dv[par])])
                ld_at[t] = (l6, lx)
            if t == 0:
                continue
            tc = t - 1
            par = tc % 2
            l6, lx = ld_at[tc]
            E("vector", lambda e, par=par: e.tensor_tensor(
                f_t[:, :], fb[par][2][:, :], fb[par][3][:, :], ALU.add),
              waits=[(dA, l6), (dv, dv.n)], inc=dv)
            E("vector", lambda e: e.tensor_scalar(
                f_s[:, :], f_t[:, :], selB, None, ALU.mult),
              waits=[(dv, dv.n)], inc=dv)
            E("vector", lambda e, par=par: e.tensor_tensor(
                f_t[:, :], fb[par][0][:, :], fb[par][1][:, :], ALU.add),
              waits=[(dv, dv.n)], inc=dv)
            E("vector", lambda e: e.scalar_tensor_tensor(
                f_s2[:, :], f_t[:, :], selA, f_s[:, :], ALU.mult, ALU.add),
              waits=[(dv, dv.n)], inc=dv)
            E("vector", lambda e, par=par: e.tensor_tensor(
                f_t[:, :], f_s2[:, :], fb[par][5][:, :], ALU.add),
              waits=[(dv, dv.n)], inc=dv)
            E("vector", lambda e, par=par: e.tensor_tensor(
                f_m[:, :], f_t[:, :], fb[par][4][:, :], ALU.mult),
              waits=[(dv, dv.n)], inc=dv)
            fb_free_dv[par] = dv.n
            tpe = transpose8(f_m, 8, waits=[(dv, dv.n)])
            E("vector", lambda e, par=par: e.tensor_tensor(
                f_o[:, :], PT[:, :], t_x[par][:, :], ALU.add),
              waits=[(pe, tpe), (dS, lx), (dS, fo_free_dS)], inc=dv)
            tx_free_dv[par] = dv.n
            fo_free_dS = dma("sync", dS,
                             out_own[128 * tc:128 * (tc + 1), :],
                             f_o[:, :], waits=[(dv, dv.n)])
        E("sync", lambda e: e.nop(), waits=[(dS, dS.n)], inc=None)

        with nc.Block() as block:
            def runner(engname):
                def _(e):
                    for waits, fn, inc, k in sch.steps[engname]:
                        for (s, v) in waits:
                            e.wait_ge(s.h, v)
                        instr = fn(e)
                        if inc is not None:
                            assert instr is not None, engname
                            instr.then_inc(inc.h, k)
                return _
            block.tensor(runner("tensor"))
            block.vector(runner("vector"))
            block.scalar(runner("scalar"))
            block.gpsimd(runner("gpsimd"))
            block.sync(runner("sync"))

    return nc


_cache = {}


def _get_nc():
    if "nc" not in _cache:
        _cache["nc"] = build_program(DEBUG)
    return _cache["nc"]


def kernel(L0_attn_x, L0_attn_Wq, L0_attn_Wk, L0_attn_Wv, L0_attn_Wo,
           L0_ffn_w_gate, L0_ffn_w_up, L0_ffn_w_down):
    x = np.ascontiguousarray(np.asarray(L0_attn_x, dtype=np.float32))
    Wq = np.ascontiguousarray(np.asarray(L0_attn_Wq, dtype=np.float32))
    Wk = np.ascontiguousarray(np.asarray(L0_attn_Wk, dtype=np.float32))
    Wv = np.ascontiguousarray(np.asarray(L0_attn_Wv, dtype=np.float32))
    Wo = np.ascontiguousarray(np.asarray(L0_attn_Wo, dtype=np.float32))
    wg = np.ascontiguousarray(np.asarray(L0_ffn_w_gate, dtype=np.float32))
    wu = np.ascontiguousarray(np.asarray(L0_ffn_w_up, dtype=np.float32))
    wd = np.ascontiguousarray(np.asarray(L0_ffn_w_down, dtype=np.float32))

    wg_p = np.zeros((IPAD, D), dtype=np.float32)
    wg_p[:IFF] = wg
    wu_p = np.zeros((IPAD, D), dtype=np.float32)
    wu_p[:IFF] = wu

    ident = np.eye(128, dtype=np.float32).astype(ml_dtypes.bfloat16)
    r_idx = np.arange(128)
    maskt = np.where(r_idx[:, None] <= r_idx[None, :], 0.0, NEG).astype(
        np.float32)

    epoch = int(np.random.randint(1, 2**30))
    in_maps = []
    for core in range(8):
        b = core // 2
        r = core % 2
        selv = np.zeros((128, 2), dtype=np.float32)
        selv[:, r] = 1.0
        in_maps.append({
            "xb": x[b],
            "x_own": x[b, r * OWN:(r + 1) * OWN],
            "wq_in": Wq[:, 512 * r:512 * (r + 1)],
            "wk_in": Wk[:, 512 * r:512 * (r + 1)],
            "wv_in": Wv[:, 512 * r:512 * (r + 1)],
            "wo_in": Wo[512 * r:512 * (r + 1), :],
            "wg_sh": wg_p[1536 * r:1536 * (r + 1)],
            "wu_sh": wu_p[1536 * r:1536 * (r + 1)],
            "wd_sh": wd[512 * r:512 * (r + 1)],
            "ident": ident,
            "maskt": maskt,
            "selv": selv,
            "rolei": np.array([[r, 0]], dtype=np.int32),
            "epochi": np.array([[epoch, 0]], dtype=np.int32),
        })

    nc = _get_nc()
    res = run_bass_kernel_spmd(nc, in_maps, core_ids=list(range(8)))
    out = np.empty((4, S, D), dtype=np.float32)
    for core in range(8):
        b = core // 2
        r = core % 2
        out[b, r * OWN:(r + 1) * OWN] = res.results[core]["out_own"]
    return out
